# version 06
# baseline (speedup 1.0000x reference)
import numpy as np
import concourse.bass as bass
import concourse.mybir as mybir
from concourse.bass_utils import run_bass_kernel_spmd

F32 = mybir.dt.float32
BF16 = mybir.dt.bfloat16
AF = mybir.ActivationFunctionType
ALU = mybir.AluOpType
AX = mybir.AxisListType

D = 2048
HK, HV, DK, DV = 16, 32, 128, 128
ECH = 8192
KC = 4
C = 128
EPS_LN = 1e-5
EPS_L2 = 1e-6
EPS_RMS = 1e-6
EPS_TER = 1e-8
LOGC = float(-0.5 * np.log(DK))
NEG = -1e30

DEFSEM = {"t": "pe", "v": "ve", "s": "ac", "g": "po", "y": "so"}


class Prog:

    def __init__(self, E, eng, sems):
        self.E = E
        self.eng = eng
        self.sems = sems
        self.cnt = {k: 0 for k in sems}
        self.lastw = {}
        self.reads = {}
        self.dsems = {}
        self.dpool = iter([f"dm{i:02d}" for i in range(64)])
        self.junk = None

    def op(self, ec, fn, reads=(), writes=(), sem=None, n=1, xwait=(), dma=False, bump=False):
        if dma:
            key = writes[0] if writes else ("sp:" + reads[0])
            if key not in self.dsems:
                self.dsems[key] = next(self.dpool)
            sem = self.dsems[key]
            n = 16
        if sem is None:
            sem = DEFSEM[ec]
        waits = {}
        for b in reads:
            lw = self.lastw.get(b)
            if lw is not None:
                waits[lw[0]] = max(waits.get(lw[0], 0), lw[1])
        for b in writes:
            lw = self.lastw.get(b)
            if lw is not None:
                waits[lw[0]] = max(waits.get(lw[0], 0), lw[1])
            for rec, (s2, v2) in self.reads.get(b, {}).items():
                waits[s2] = max(waits.get(s2, 0), v2)
        for (s2, v2) in xwait:
            waits[s2] = max(waits.get(s2, 0), v2)
        if self.E == ec:
            for s2, v2 in waits.items():
                if v2 > 0:
                    self.eng.wait_ge(self.sems[s2], v2)
            ins = fn(self.eng)
            if bump:
                if ec == "v" or ec == "g":
                    ins = self.eng.tensor_copy(out=self.junk[0:1, 2:4], in_=self.junk[0:1, 0:2])
                elif ec == "s":
                    ins = self.eng.copy(out=self.junk[0:1, 2:4], in_=self.junk[0:1, 0:2])
            if ins is not None:
                ins.then_inc(self.sems[sem], n)
        self.cnt[sem] += n
        ms = (sem, self.cnt[sem])
        for b in reads:
            self.reads.setdefault(b, {})[ec] = ms
        for b in writes:
            self.lastw[b] = (sem, self.cnt[sem], ec)
            self.reads[b] = {}
        return ms

    def barrier(self):
        vals = []
        for ec in ("t", "v", "s", "g", "y"):
            s = DEFSEM[ec]
            self.cnt[s] += 1
            vals.append((ec, s, self.cnt[s]))
            if self.E == ec:
                self.eng.nop().then_inc(self.sems[s], 1)
        dma_waits = [(s, self.cnt[s]) for s in self.dsems.values()]
        for ec2 in ("t", "v", "s", "g", "y"):
            if self.E == ec2:
                for (ec, s, v) in vals:
                    if ec != ec2:
                        self.eng.wait_ge(self.sems[s], v)
                for (s, v) in dma_waits:
                    if v > 0:
                        self.eng.wait_ge(self.sems[s], v)
        self.lastw = {}
        self.reads = {}


def build(L=2048, dbg=False, maxphase=99, dlim=(99, 99, 99)):
    NT = L // 128
    NCH = L // C
    nc = bass.Bass("TRN2", target_bir_lowering=False, debug=False,
                   detect_race_conditions=False)

    dram = {}
    def dt_(name, shape, dtype, kind):
        dram[name] = nc.dram_tensor(name, shape, dtype, kind=kind).ap()
    dt_("x", [L, D], F32, "ExternalInput")
    dt_("wqkv", [D, ECH], F32, "ExternalInput")
    dt_("wz", [D, HV * DV], F32, "ExternalInput")
    dt_("wab", [D, 64], F32, "ExternalInput")
    dt_("cw", [128, 64 * KC], F32, "ExternalInput")
    dt_("alog", [32, 1], F32, "ExternalInput")
    dt_("dtb", [32, 1], F32, "ExternalInput")
    dt_("nw", [128, 1], F32, "ExternalInput")
    dt_("wout", [HV * DV, D], F32, "ExternalInput")
    dt_("wupT", [D, ECH], F32, "ExternalInput")
    dt_("wgT", [D, ECH], F32, "ExternalInput")
    dt_("wdT", [ECH, D], F32, "ExternalInput")
    dt_("out", [L, D], F32, "ExternalOutput")
    sk = "ExternalOutput" if dbg else "Internal"
    dt_("q_sp", [HK * DK, L], BF16, sk)
    dt_("k_sp", [HK * DK, L], BF16, sk)
    dt_("kt_sp", [L, HK * DK], BF16, sk)
    dt_("v_sp", [L, HV * DV], BF16, sk)
    dt_("sz_sp", [L, HV * DV], BF16, sk)
    dt_("y_sp", [HV * DV, L], BF16, sk)
    dt_("x1_sp", [L, D], F32, sk)
    dt_("h_sp", [ECH, L], BF16, sk)
    dt_("f0_sp", [L, 1536], F32, "Internal")
    dt_("m0_sp", [L, 1536], F32, "Internal")
    dt_("gg_sp", [2, 64, 128], F32, "Internal")
    dt_("gc_sp", [2, 32], F32, "Internal")
    dt_("scl_sp", [1, 8], F32, "Internal")
    if dbg:
        for i in range(10):
            dt_(f"dbg{i}", [128, 512], F32, sk)

    import contextlib
    stack = contextlib.ExitStack()

    PERMW = 10200
    TMPW = 42600
    perm = stack.enter_context(nc.sbuf_tensor("perm", [128, PERMW], F32))
    tmp = stack.enter_context(nc.sbuf_tensor("tmp", [128, TMPW], F32))
    A = {}

    def carve(arena, off, name, parts, words, dtype):
        assert off + words <= (PERMW if arena is perm else TMPW), (name, off, words)
        v = arena[0:parts, off:off + words]
        if dtype == BF16:
            v = v.bitcast(BF16)
        A[name] = v
        return off + words

    o = 0
    o = carve(perm, o, "ones1", 1, 128, F32)
    o = carve(perm, o, "onesc_bf", 128, 1, BF16)
    o = carve(perm, o, "onesc_f", 128, 1, F32)
    o = carve(perm, o, "id_bf", 128, 64, BF16)
    o = carve(perm, o, "id_f", 128, 128, F32)
    o = carve(perm, o, "triu_f", 128, 128, F32)
    o = carve(perm, o, "maskA", 128, 512, F32)
    o = carve(perm, o, "maskM", 128, 512, F32)
    o = carve(perm, o, "epsln", 128, 1, F32)
    o = carve(perm, o, "epsl2", 128, 1, F32)
    o = carve(perm, o, "epsrms", 128, 1, F32)
    o = carve(perm, o, "logcc", 128, 1, F32)
    o = carve(perm, o, "alog", 32, 1, F32)
    o = carve(perm, o, "dtb", 32, 1, F32)
    o = carve(perm, o, "negea", 32, 1, F32)
    o = carve(perm, o, "nw", 128, 1, F32)
    o = carve(perm, o, "cw", 128, 64 * KC, F32)
    o = carve(perm, o, "gT", 32, L, F32)
    o = carve(perm, o, "lbT", 32, L, F32)
    o = carve(perm, o, "beT", 32, L, F32)
    o = carve(perm, o, "ssq", 128, 32 * NT, F32)
    o = carve(perm, o, "lnss", 128, 32 * NT, F32)
    o = carve(perm, o, "scl", 1, 8, F32)
    o = carve(perm, o, "srep", 128, 8, F32)
    o = carve(perm, o, "wacc", 128, 64, F32)
    o = carve(perm, o, "wred", 128, 4, F32)
    for nm in ("sumx", "sumq", "mcol", "vcol", "rstd"):
        o = carve(perm, o, nm, 128, 1, F32)
    for nm in ("g_tm", "lb_tm", "be_tm", "Gtil", "Ghat", "gcrep", "e1", "e2",
               "cS", "egc", "csub", "msq", "rr"):
        o = carve(perm, o, nm, 128, 32, F32)
    o = carve(perm, o, "GG", 128, 64, F32)
    o = carve(perm, o, "GGT", 64, 128, F32)
    o = carve(perm, o, "junk", 1, 4, F32)
    assert o <= PERMW, o

    o = 0
    o = carve(tmp, o, "lnd", 128, 16 * L // 2, BF16)
    o = carve(tmp, o, "wz0", 128, 16 * 512 // 2, BF16)
    o = carve(tmp, o, "wz1", 128, 16 * 512 // 2, BF16)
    o = carve(tmp, o, "wq0", 128, 16 * 128 // 2, BF16)
    o = carve(tmp, o, "wq1", 128, 16 * 128 // 2, BF16)
    o = carve(tmp, o, "wabt", 128, 16 * 64 // 2, BF16)
    oz = o
    WSCW = 4096 if L >= 2048 else 512
    carve(tmp, oz, "wsc", 128, WSCW, F32)
    o = carve(tmp, o, "qkv0", 128, (L + KC - 1 + 1) // 2, BF16)
    o = carve(tmp, o, "qkv1", 128, (L + KC - 1 + 1) // 2, BF16)
    o = carve(tmp, o, "cacc", 128, L // 2, BF16)
    o = carve(tmp, o, "mix", 128, L // 2, BF16)
    o = carve(tmp, o, "sqt", 128, L // 2, BF16)
    o = carve(tmp, o, "tstage", 128, 16 * 128 // 2, BF16)
    o = carve(tmp, o, "absb", 64, L, F32)
    o = carve(tmp, o, "spsc", 32, L, F32)
    o = carve(tmp, o, "xa", 128, D, F32)
    o = carve(tmp, o, "lnst", 128, D, F32)
    o = carve(tmp, o, "lntile", 128, D // 2, BF16)
    assert o <= TMPW, o

    o = 0
    o = carve(tmp, o, "S", 128, HV * DV, F32)
    o = carve(tmp, o, "S0b", 128, HV * DV // 2, BF16)
    o = carve(tmp, o, "grw", 128, 32 * 128, F32)
    o = carve(tmp, o, "qb0", 128, HK * 128 // 2, BF16)
    o = carve(tmp, o, "qb1", 128, HK * 128 // 2, BF16)
    o = carve(tmp, o, "kb0", 128, HK * 128 // 2, BF16)
    o = carve(tmp, o, "kb1", 128, HK * 128 // 2, BF16)
    o = carve(tmp, o, "ktb0", 128, HK * 128 // 2, BF16)
    o = carve(tmp, o, "ktb1", 128, HK * 128 // 2, BF16)
    o = carve(tmp, o, "vb0", 128, HV * 128 // 2, BF16)
    o = carve(tmp, o, "vb1", 128, HV * 128 // 2, BF16)
    o = carve(tmp, o, "zb0", 128, HV * 128 // 2, BF16)
    o = carve(tmp, o, "zb1", 128, HV * 128 // 2, BF16)
    o = carve(tmp, o, "flat", 1, 8192, F32)
    o = carve(tmp, o, "Din", 128, 512, F32)
    o = carve(tmp, o, "Din2", 128, 512, F32)
    o = carve(tmp, o, "t2f", 128, 512, F32)
    o = carve(tmp, o, "t3f", 128, 512, F32)
    o = carve(tmp, o, "ot4", 128, 512, F32)
    o = carve(tmp, o, "o4", 128, 512, F32)
    o = carve(tmp, o, "yg", 128, 512, F32)
    o = carve(tmp, o, "ygs", 128, 512, F32)
    for nm in ("EA", "EM", "B4", "M4", "BT4", "MT4", "Zb", "Pb", "Yb", "Ub", "Ut", "y4"):
        o = carve(tmp, o, nm, 128, 256, BF16)
    o = carve(tmp, o, "ystg", 128, HV * 128 // 2, BF16)
    assert o <= TMPW, o

    o = 16 * L // 2
    o = carve(tmp, o, "wo_sb", 128, 32 * 512 // 2, BF16)
    o = carve(tmp, o, "yt0", 128, 32 * 128 // 2, BF16)
    o = carve(tmp, o, "yt1", 128, 32 * 128 // 2, BF16)
    o = carve(tmp, o, "mstage", 128, D, F32)
    o = carve(tmp, o, "x1t", 128, D, F32)
    o = carve(tmp, o, "xa4", 128, D, F32)
    o = carve(tmp, o, "lnst4", 128, D, F32)
    o = carve(tmp, o, "lnt4", 128, D // 2, BF16)
    assert o <= TMPW, o
    o = 16 * L // 2
    o = carve(tmp, o, "wu0", 128, 16 * 128, F32)
    o = carve(tmp, o, "wu1", 128, 16 * 128, F32)
    o = carve(tmp, o, "wg0", 128, 16 * 128, F32)
    o = carve(tmp, o, "wg1", 128, 16 * 128, F32)
    o = carve(tmp, o, "tqu0", 128, 16 * 128 // 2, BF16)
    o = carve(tmp, o, "tqg0", 128, 16 * 128 // 2, BF16)
    o = carve(tmp, o, "tqu1", 128, 16 * 128 // 2, BF16)
    o = carve(tmp, o, "tqg1", 128, 16 * 128 // 2, BF16)
    o = carve(tmp, o, "sgn10", 128, 16 * 128 // 2, BF16)
    o = carve(tmp, o, "sgn20", 128, 16 * 128 // 2, BF16)
    o = carve(tmp, o, "sgn11", 128, 16 * 128 // 2, BF16)
    o = carve(tmp, o, "sgn21", 128, 16 * 128 // 2, BF16)
    o = carve(tmp, o, "hst", 128, L // 2, BF16)
    o = carve(tmp, o, "gsc", 128, 512 // 2, BF16)
    assert o <= TMPW, o

    o = 0
    o = carve(tmp, o, "td", 128, 64 * 512 // 2, BF16)
    o = carve(tmp, o, "hb0", 128, 64 * 128 // 2, BF16)
    o = carve(tmp, o, "hb1", 128, 64 * 128 // 2, BF16)
    o = carve(tmp, o, "wd0", 128, 512, F32)
    o = carve(tmp, o, "wd1", 128, 512, F32)
    o = carve(tmp, o, "fst", 128, D, F32)
    o = carve(tmp, o, "x6t", 128, D, F32)
    o = carve(tmp, o, "xa6", 128, D, F32)
    o = carve(tmp, o, "lnst6", 128, D, F32)
    o = carve(tmp, o, "sg6a", 128, 512 // 2, BF16)
    o = carve(tmp, o, "sg6b", 128, 512 // 2, BF16)
    assert o <= TMPW, o

    P_ = {}
    for i in range(8):
        P_[i] = stack.enter_context(nc.psum_tensor(f"pb{i}", [128, 512], F32))
        A[f"P{i}"] = P_[i]

    sems = {}
    for s in ("pe", "ve", "ac", "po", "so"):
        sems[s] = stack.enter_context(nc.semaphore(s))
    for i in range(64):
        sems[f"dm{i:02d}"] = stack.enter_context(nc.semaphore(f"dm{i:02d}"))

    lnd3 = A["lnd"][:, :].rearrange("p (dt t) -> p dt t", dt=16)

    def program(E, eng):
        p = Prog(E, eng, sems)
        p.junk = A["junk"]
        op = p.op

        def ln_full(xbuf, lnstbuf, outbf, rkey):
            op("v", lambda e: e.tensor_reduce(out=A["sumx"][:], in_=A[xbuf][:], axis=AX.X, op=ALU.add),
               reads=(xbuf,), writes=("sumx",))
            op("v", lambda e: e.scalar_tensor_tensor(out=A[lnstbuf][:], in0=A[xbuf][:], scalar=1.0,
                                                     in1=A[xbuf][:], op0=ALU.mult, op1=ALU.mult,
                                                     accum_out=A["sumq"][:]),
               reads=(xbuf,), writes=(lnstbuf, "sumq"))
            op("v", lambda e: e.tensor_scalar(out=A["mcol"][:], in0=A["sumx"][:], scalar1=1.0 / D,
                                              scalar2=None, op0=ALU.mult),
               reads=("sumx",), writes=("mcol",))
            op("v", lambda e: e.tensor_tensor(out=A["vcol"][:], in0=A["mcol"][:], in1=A["mcol"][:],
                                              op=ALU.mult), reads=("mcol",), writes=("vcol",))
            op("v", lambda e: e.scalar_tensor_tensor(out=A["vcol"][:], in0=A["sumq"][:], scalar=1.0 / D,
                                                     in1=A["vcol"][:], op0=ALU.mult, op1=ALU.subtract),
               reads=("sumq", "vcol"), writes=("vcol",), bump=True)
            op("s", lambda e: e.activation(out=A["rstd"][:], in_=A["vcol"][:], func=AF.Ln,
                                           bias=A["epsln"][:]), reads=("vcol", "epsln"), writes=("rstd",))
            op("s", lambda e: e.activation(out=A["rstd"][:], in_=A["rstd"][:], func=AF.Exp, scale=-0.5),
               reads=("rstd",), writes=("rstd",), bump=True)
            op("v", lambda e: e.scalar_tensor_tensor(
                out=A[outbf][:], in0=A[xbuf][:], scalar=A["mcol"][:],
                in1=A["rstd"][:].broadcast_to([128, D]), op0=ALU.subtract, op1=ALU.mult),
               reads=(xbuf, "mcol", "rstd"), writes=(outbf,))

        def transpose_to_lnd(srcbf, tt, bank0):
            for g4 in range(4):
                bk = f"P{bank0 + g4 % 2}"
                for j in range(4):
                    dtile = g4 * 4 + j
                    op("t", lambda e, bk=bk, j=j, dtile=dtile: e.matmul(
                        A[bk][:, j * 128:(j + 1) * 128], A[srcbf][:, dtile * 128:(dtile + 1) * 128],
                        A["id_bf"][:], start=True, stop=True),
                       reads=(srcbf, "id_bf"), writes=(bk,))
                op("v", lambda e, bk=bk, g4=g4, tt=tt: e.tensor_copy(
                    out=lnd3[:, g4 * 4:(g4 + 1) * 4, tt * 128:(tt + 1) * 128], in_=A[bk][:]),
                   reads=(bk,), writes=("lnd",))

        def finwait():
            if E == "y":
                for s in list(p.dsems.values()):
                    if p.cnt[s] > 0:
                        eng.wait_ge(sems[s], p.cnt[s])

        op("v", lambda e: e.memset(A["junk"][:], 0.0), writes=("junkb",))
        op("v", lambda e: e.memset(A["ones1"][:], 1.0), writes=("ones1",))
        op("v", lambda e: e.memset(A["onesc_bf"][:], 1.0), writes=("onesc_bf",))
        op("v", lambda e: e.memset(A["onesc_f"][:], 1.0), writes=("onesc_f",))
        op("v", lambda e: e.memset(A["id_bf"][:], 1.0), writes=("id_bf",))
        op("v", lambda e: e.memset(A["id_f"][:], 1.0), writes=("id_f",))
        op("v", lambda e: e.memset(A["triu_f"][:], 1.0), writes=("triu_f",))
        op("v", lambda e: e.memset(A["maskA"][:], 0.0), writes=("maskA",))
        op("v", lambda e: e.memset(A["maskM"][:], 0.0), writes=("maskM",))
        op("v", lambda e: e.memset(A["epsln"][:], EPS_LN), writes=("epsln",))
        op("v", lambda e: e.memset(A["epsl2"][:], EPS_L2), writes=("epsl2",))
        op("v", lambda e: e.memset(A["epsrms"][:], EPS_RMS), writes=("epsrms",))
        op("v", lambda e: e.memset(A["logcc"][:], LOGC), writes=("logcc",))
        op("y", lambda e: e.dma_start(out=A["cw"][:], in_=dram["cw"]), dma=True, writes=("cw",))
        op("y", lambda e: e.dma_start(out=A["alog"][:], in_=dram["alog"]), dma=True, writes=("alog",))
        op("y", lambda e: e.dma_start(out=A["dtb"][:], in_=dram["dtb"]), dma=True, writes=("dtb",))
        op("y", lambda e: e.dma_start(out=A["nw"][:], in_=dram["nw"]), dma=True, writes=("nw",))
        op("g", lambda e: e.affine_select(out=A["id_bf"][:], in_=A["id_bf"][:], pattern=[[1, 128]],
                                          compare_op=ALU.is_equal, fill=0.0, channel_multiplier=-1),
           reads=("id_bf",), writes=("id_bf",))
        op("g", lambda e: e.affine_select(out=A["id_f"][:], in_=A["id_f"][:], pattern=[[1, 128]],
                                          compare_op=ALU.is_equal, fill=0.0, channel_multiplier=-1),
           reads=("id_f",), writes=("id_f",))
        op("g", lambda e: e.affine_select(out=A["triu_f"][:], in_=A["triu_f"][:], pattern=[[1, 128]],
                                          compare_op=ALU.is_ge, fill=0.0, channel_multiplier=-1),
           reads=("triu_f",), writes=("triu_f",))
        op("g", lambda e: e.affine_select(out=A["maskA"][:].rearrange("p (h s) -> p h s", h=4),
                                          in_=A["maskA"][:].rearrange("p (h s) -> p h s", h=4),
                                          pattern=[[0, 4], [-1, 128]], compare_op=ALU.is_ge,
                                          fill=NEG, channel_multiplier=1, base=-1),
           reads=("maskA",), writes=("maskA",))
        op("g", lambda e: e.affine_select(out=A["maskM"][:].rearrange("p (h s) -> p h s", h=4),
                                          in_=A["maskM"][:].rearrange("p (h s) -> p h s", h=4),
                                          pattern=[[0, 4], [-1, 128]], compare_op=ALU.is_ge,
                                          fill=NEG, channel_multiplier=1, base=0),
           reads=("maskM",), writes=("maskM",))
        op("s", lambda e: e.activation(out=A["negea"][:], in_=A["alog"][:], func=AF.Exp),
           reads=("alog",), writes=("negea",), bump=True)

        p.barrier()

        if maxphase < 1:
            finwait()
            return
        for tt in range(NT):
            op("y", lambda e, tt=tt: e.dma_start(out=A["xa"][:], in_=dram["x"][tt * 128:(tt + 1) * 128, :]),
               dma=True, writes=("xa",))
            ln_full("xa", "lnst", "lntile", None)
            if dbg and tt == 0:
                op("y", lambda e: e.dma_start(out=dram["dbg8"][:], in_=A["xa"][:, 0:512]),
                   dma=True, reads=("xa",), writes=("ddbg8",))
                op("v", lambda e: e.tensor_copy(out=A["lnst"][:, 0:512],
                                                in_=A["lntile"][:, 0:512]),
                   reads=("lntile",), writes=("lnst",))
                op("v", lambda e: e.tensor_copy(out=A["lnst"][:, 512:513], in_=A["mcol"][:]),
                   reads=("mcol",), writes=("lnst",))
                op("v", lambda e: e.tensor_copy(out=A["lnst"][:, 513:514], in_=A["rstd"][:]),
                   reads=("rstd",), writes=("lnst",))
                op("v", lambda e: e.tensor_copy(out=A["lnst"][:, 514:515], in_=A["sumq"][:]),
                   reads=("sumq",), writes=("lnst",))
                op("v", lambda e: e.tensor_copy(out=A["lnst"][:, 515:516], in_=A["vcol"][:]),
                   reads=("vcol",), writes=("lnst",))
                op("y", lambda e: e.dma_start(out=dram["dbg9"][:], in_=A["lnst"][:, 0:512]),
                   dma=True, reads=("lnst",), writes=("ddbg9",))
                op("y", lambda e: e.dma_start(out=dram["dbg9"][:, 0:4].rearrange("p f -> p f"),
                                              in_=A["lnst"][:, 512:516]),
                   dma=True, reads=("lnst",), writes=("ddbg9",))
            transpose_to_lnd("lntile", tt, 0)

        if maxphase < 2:
            finwait()
            return
        op("g", lambda e: e.dma_start(out=A["wabt"][:].rearrange("p (t c) -> p t c", t=16),
                                      in_=dram["wab"].rearrange("(t p) c -> p t c", p=128)),
           dma=True, writes=("wabt",))
        for tp in range(L // 512):
            bk = f"P{2 + tp % 2}"
            for dtile in range(16):
                op("t", lambda e, bk=bk, tp=tp, dtile=dtile: e.matmul(
                    A[bk][0:64, :], A["wabt"][:, dtile * 64:(dtile + 1) * 64],
                    lnd3[:, dtile, tp * 512:(tp + 1) * 512],
                    start=(dtile == 0), stop=(dtile == 15)),
                   reads=("wabt", "lnd"), writes=(bk,))
            op("v", lambda e, bk=bk, tp=tp: e.tensor_copy(out=A["absb"][:, tp * 512:(tp + 1) * 512],
                                                          in_=A[bk][0:64, :]),
               reads=(bk,), writes=("absb",))
        op("v", lambda e: e.tensor_scalar(out=A["spsc"][:], in0=A["absb"][0:32, :], scalar1=A["dtb"][:],
                                          scalar2=None, op0=ALU.add), reads=("absb", "dtb"), writes=("spsc",))
        for (dst, src_is_neg) in (("gT", False), ("lbT", True)):
            if src_is_neg:
                op("v", lambda e: e.tensor_scalar(out=A["spsc"][:], in0=A["absb"][32:64, :], scalar1=-1.0,
                                                  scalar2=None, op0=ALU.mult), reads=("absb",), writes=("spsc",))
            op("s", lambda e, dst=dst: e.activation(out=A[dst][:], in_=A["spsc"][:], func=AF.Abs),
               reads=("spsc",), writes=(dst,))
            op("s", lambda e, dst=dst: e.activation(out=A[dst][:], in_=A[dst][:], func=AF.Exp, scale=-1.0),
               reads=(dst,), writes=(dst,))
            op("s", lambda e, dst=dst: e.activation(out=A[dst][:], in_=A[dst][:], func=AF.Ln, bias=1.0),
               reads=(dst,), writes=(dst,))
            op("v", lambda e: e.tensor_scalar(out=A["spsc"][:], in0=A["spsc"][:], scalar1=0.0, scalar2=None,
                                              op0=ALU.max), reads=("spsc",), writes=("spsc",))
            op("v", lambda e, dst=dst: e.tensor_tensor(out=A[dst][:], in0=A["spsc"][:], in1=A[dst][:],
                                                       op=ALU.add), reads=("spsc", dst), writes=(dst,))
        op("v", lambda e: e.tensor_scalar(out=A["gT"][:], in0=A["gT"][:], scalar1=A["negea"][:],
                                          scalar2=-1.0, op0=ALU.mult, op1=ALU.mult),
           reads=("gT", "negea"), writes=("gT",))
        op("v", lambda e: e.tensor_scalar(out=A["lbT"][:], in0=A["lbT"][:], scalar1=-1.0, scalar2=None,
                                          op0=ALU.mult), reads=("lbT",), writes=("lbT",))
        op("s", lambda e: e.activation(out=A["beT"][:], in_=A["lbT"][:], func=AF.Exp),
           reads=("lbT",), writes=("beT",))

        if maxphase < 3:
            finwait()
            return
        op("v", lambda e: e.memset(A["wacc"][:, 0:4], 0.0), writes=("wacc",))

        WCOL = WSCW // 16
        WC2 = WSCW // 64
        NPU = ECH // WCOL
        NPD = D // WC2
        NBLK = 2 * NPU + NPD
        NAB = max(1, WSCW // 2048)

        def scale_block(bi):
            if bi < 2 * NPU:
                wi, cb2 = bi // NPU, bi % NPU
                wn = ("wupT", "wgT")[wi]
                s_ = dram[wn][:, cb2 * WCOL:(cb2 + 1) * WCOL].rearrange("(t p) c -> p t c", p=128)
                tdim = 16
            else:
                wi, cb2 = 2, bi - 2 * NPU
                s_ = dram["wdT"][:, cb2 * WC2:(cb2 + 1) * WC2].rearrange("(t p) c -> p t c", p=128)
                tdim = 64
            op("y", lambda e, s_=s_, tdim=tdim: e.dma_start(
                out=A["wsc"][:].rearrange("p (t c) -> p t c", t=tdim),
                in_=s_), dma=True, writes=("wsc",))
            for kq_ in range(NAB):
                w_ = WSCW // NAB
                op("s", lambda e, kq_=kq_, w_=w_: e.activation(
                    out=A["lnst"][:, 0:w_], in_=A["wsc"][:, kq_ * w_:(kq_ + 1) * w_], func=AF.Abs,
                    accum_out=A["wred"][:, kq_:kq_ + 1]),
                   reads=("wsc",), writes=("lnst", "wred"))
            op("v", lambda e: e.tensor_reduce(
                out=A["wred"][:, 3:4], in_=A["wred"][:, 0:NAB], axis=AX.X, op=ALU.add),
               reads=("wred",), writes=("wred",))
            op("v", lambda e, wi=wi: e.tensor_tensor(
                out=A["wacc"][:, wi:wi + 1], in0=A["wacc"][:, wi:wi + 1], in1=A["wred"][:, 3:4],
                op=ALU.add), reads=("wred", "wacc"), writes=("wacc",), bump=True)

        for cb in range(HV * DV // 512):
            nper_cb = (NBLK + 7) // 8
            for sbi in range(cb * nper_cb, min(NBLK, (cb + 1) * nper_cb)):
                scale_block(sbi)
            wzb = f"wz{cb % 2}"
            op("g", lambda e, wzb=wzb, cb=cb: e.dma_start(
                out=A[wzb][:].rearrange("p (t c) -> p t c", t=16),
                in_=dram["wz"][:, cb * 512:(cb + 1) * 512].rearrange("(t p) c -> p t c", p=128)),
               dma=True, writes=(wzb,))
            for tt in range(NT):
                bk = f"P{4 + tt % 2}"
                for dtile in range(16):
                    op("t", lambda e, bk=bk, wzb=wzb, dtile=dtile, tt=tt: e.matmul(
                        A[bk][:], lnd3[:, dtile, tt * 128:(tt + 1) * 128],
                        A[wzb][:, dtile * 512:(dtile + 1) * 512],
                        start=(dtile == 0), stop=(dtile == 15)),
                       reads=(wzb, "lnd"), writes=(bk,))
                slot = f"lt{tt % 2}"
                op("s", lambda e, bk=bk, tt=tt: e.activation(
                    out=A["lntile"][:, (tt % 2) * 512:(tt % 2 + 1) * 512], in_=A[bk][:], func=AF.Silu),
                   reads=(bk,), writes=(slot,))
                op("y", lambda e, tt=tt, cb=cb: e.dma_start(
                    out=dram["sz_sp"][tt * 128:(tt + 1) * 128, cb * 512:(cb + 1) * 512],
                    in_=A["lntile"][:, (tt % 2) * 512:(tt % 2 + 1) * 512]),
                   dma=True, reads=(slot,))

        if maxphase < 4:
            finwait()
            return
        op("v", lambda e: e.tensor_copy(out=A["junk"][0:1, 2:4], in_=A["junk"][0:1, 0:2]),
           reads=("wsc",), writes=("wsc",))
        op("s", lambda e: e.copy(out=A["junk"][0:1, 2:4], in_=A["junk"][0:1, 0:2]),
           reads=("wsc",), writes=("wsc",))
        op("v", lambda e: e.memset(A["qkv0"][:, 0:KC - 1], 0.0), writes=())
        op("v", lambda e: e.memset(A["qkv1"][:, 0:KC - 1], 0.0), writes=())
        for ct in range(64):
            wqb = f"wq{ct % 2}"
            qkb = f"qkv{ct % 2}"
            op("g", lambda e, wqb=wqb, ct=ct: e.dma_start(
                out=A[wqb][:].rearrange("p (t c) -> p t c", t=16),
                in_=dram["wqkv"][:, ct * 128:(ct + 1) * 128].rearrange("(t p) c -> p t c", p=128)),
               dma=True, writes=(wqb,))
            for tp in range(L // 512):
                bk = f"P{tp % 2}"
                for dtile in range(16):
                    op("t", lambda e, bk=bk, wqb=wqb, dtile=dtile, tp=tp: e.matmul(
                        A[bk][:], A[wqb][:, dtile * 128:(dtile + 1) * 128],
                        lnd3[:, dtile, tp * 512:(tp + 1) * 512],
                        start=(dtile == 0), stop=(dtile == 15)),
                       reads=(wqb, "lnd"), writes=(bk,))
                op("v", lambda e, bk=bk, qkb=qkb, tp=tp: e.tensor_copy(
                    out=A[qkb][:, KC - 1 + tp * 512: KC - 1 + (tp + 1) * 512], in_=A[bk][:]),
                   reads=(bk,), writes=(qkb,))
            op("v", lambda e, qkb=qkb, ct=ct: e.tensor_scalar(
                out=A["cacc"][:], in0=A[qkb][:, 0:L], scalar1=A["cw"][:, ct * KC:ct * KC + 1],
                scalar2=None, op0=ALU.mult), reads=(qkb, "cw"), writes=("cacc",))
            for i in range(1, KC):
                op("v", lambda e, qkb=qkb, ct=ct, i=i: e.scalar_tensor_tensor(
                    out=A["cacc"][:], in0=A[qkb][:, i:i + L], scalar=A["cw"][:, ct * KC + i:ct * KC + i + 1],
                    in1=A["cacc"][:], op0=ALU.mult, op1=ALU.add),
                   reads=(qkb, "cw", "cacc"), writes=("cacc",))
            op("s", lambda e: e.activation(out=A["mix"][:], in_=A["cacc"][:], func=AF.Silu),
               reads=("cacc",), writes=("mix",))
            if ct < 32:
                dst = dram["q_sp"] if ct < 16 else dram["k_sp"]
                row = (ct % 16) * 128
                op("y", lambda e, dst=dst, row=row: e.dma_start(out=dst[row:row + 128, :], in_=A["mix"][:]),
                   dma=True, reads=("mix",))
                op("s", lambda e: e.activation(out=A["sqt"][:], in_=A["mix"][:], func=AF.Square),
                   reads=("mix",), writes=("sqt",))
                for j in range(NT):
                    op("t", lambda e, j=j: e.matmul(
                        A["P6"][:, j:j + 1], A["sqt"][:, j * 128:(j + 1) * 128], A["onesc_bf"][:, 0:1],
                        start=True, stop=True), reads=("sqt", "onesc_bf"), writes=("P6",) if j == 0 else ())
                op("t", lambda e: e.nop(), writes=("P6",))
                op("v", lambda e, ct=ct: e.tensor_copy(out=A["ssq"][:, ct * NT:(ct + 1) * NT],
                                                       in_=A["P6"][:, 0:NT]),
                   reads=("P6",), writes=("ssq",), bump=True)
            if ct >= 16:
                for g4 in range(NT // 4):
                    for j in range(4):
                        ts_ = g4 * 4 + j
                        op("t", lambda e, j=j, ts_=ts_: e.matmul(
                            A["P7"][:, j * 128:(j + 1) * 128], A["mix"][:, ts_ * 128:(ts_ + 1) * 128],
                            A["id_bf"][:], start=True, stop=True),
                           reads=("mix", "id_bf"), writes=("P7",))
                    op("v", lambda e, g4=g4: e.tensor_copy(
                        out=A["tstage"][:, g4 * 512:(g4 + 1) * 512], in_=A["P7"][:]),
                       reads=("P7",), writes=("tstage",))
                if ct < 32:
                    op("y", lambda e, ct=ct: e.dma_start(
                        out=dram["kt_sp"][:, (ct - 16) * 128:(ct - 15) * 128].rearrange(
                            "(j p) w -> p j w", p=128),
                        in_=A["tstage"][:, 0:NT * 128].rearrange("p (j w) -> p j w", w=128)),
                       dma=True, reads=("tstage",))
                else:
                    op("y", lambda e, ct=ct: e.dma_start(
                        out=dram["v_sp"][:, (ct - 32) * 128:(ct - 31) * 128].rearrange(
                            "(j p) w -> p j w", p=128),
                        in_=A["tstage"][:, 0:NT * 128].rearrange("p (j w) -> p j w", w=128)),
                       dma=True, reads=("tstage",))

        if maxphase < 5:
            finwait()
            return
        p.barrier()
        nel = float(D * ECH)
        op("t", lambda e: e.matmul(A["P0"][0:1, 0:4], A["onesc_f"][:], A["wacc"][:, 0:4],
                                   start=True, stop=True), reads=("wacc", "onesc_f"), writes=("P0",))
        op("t", lambda e: e.matmul(A["P0"][0:1, 508:509], A["onesc_bf"][0:1, 0:1],
                           A["onesc_bf"][0:1, 0:1], start=True, stop=True),
           reads=("onesc_bf",), writes=("P0",))
        op("v", lambda e: e.tensor_scalar(out=A["scl"][0:1, 0:3], in0=A["P0"][0:1, 0:3],
                                          scalar1=1.0 / nel, scalar2=EPS_TER, op0=ALU.mult, op1=ALU.add),
           reads=("P0",), writes=("scl",), bump=True)
        op("v", lambda e: e.tensor_scalar(out=A["scl"][0:1, 3:6], in0=A["scl"][0:1, 0:3],
                                          scalar1=0.5, scalar2=None, op0=ALU.mult),
           reads=("scl",), writes=("scl",), bump=True)
        op("y", lambda e: e.dma_start(out=dram["scl_sp"][0:1, 0:6], in_=A["scl"][0:1, 0:6]),
           dma=True, reads=("scl",), writes=("dscl",))
        op("y", lambda e: e.dma_start(
            out=A["srep"][:, 0:3],
            in_=dram["scl_sp"][0:1, 3:6].partition_broadcast(128).rearrange("p o f -> p (o f)")),
           dma=True, reads=("dscl",), writes=("srep",))
        op("y", lambda e: e.dma_start(
            out=A["srep"][:, 3:6],
            in_=dram["scl_sp"][0:1, 3:6].partition_broadcast(128).rearrange("p o f -> p (o f)")),
           dma=True, reads=("dscl",), writes=("srep",))
        p.barrier()

        if maxphase < 6:
            finwait()
            return
        op("s", lambda e: e.activation(out=A["lnss"][:], in_=A["ssq"][:], func=AF.Ln, bias=A["epsl2"][:]),
           reads=("ssq", "epsl2"), writes=("lnss",), bump=True)
        op("v", lambda e: e.memset(A["S"][:], 0.0), writes=("S",))
        lnss3 = A["lnss"][:, :].rearrange("p (h t) -> p h t", h=32)
        GGe = A["GG"][:, 0:32].rearrange("p (h d) -> p h d", d=2)
        Gve = A["GG"][:, 32:64].rearrange("p (h d) -> p h d", d=2)

        def chunk_loads(c):
            b = c % 2
            for (nm, src, hh) in ((f"qb{b}", dram["q_sp"][:, c * 128:(c + 1) * 128].rearrange("(h p) t -> p h t", p=128), 16),
                              (f"kb{b}", dram["k_sp"][:, c * 128:(c + 1) * 128].rearrange("(h p) t -> p h t", p=128), 16),
                              (f"ktb{b}", dram["kt_sp"][c * 128:(c + 1) * 128, :], 0),
                              (f"vb{b}", dram["v_sp"][c * 128:(c + 1) * 128, :], 0),
                              (f"zb{b}", dram["sz_sp"][c * 128:(c + 1) * 128, :], 0)):
                o_ = A[nm][:] if hh == 0 else A[nm][:].rearrange("p (h t) -> p h t", h=hh)
                op("y", lambda e, o_=o_, src=src: e.dma_start(out=o_, in_=src),
                   dma=True, writes=(nm,))

        chunk_loads(0)
        for c in range(min(NCH, dlim[0])):
            b = c % 2
            qb, kb, ktb, vb, zb = (f"qb{b}", f"kb{b}", f"ktb{b}", f"vb{b}", f"zb{b}")
            if dlim[1] == 0 and dlim[2] < 1:
                continue
            for (i, srcn, dstn) in ((0, "gT", "g_tm"), (1, "lbT", "lb_tm"), (2, "beT", "be_tm")):
                op("t", lambda e, i=i, srcn=srcn, c=c: e.matmul(
                    A["P0"][:, i * 32:(i + 1) * 32], A[srcn][:, c * 128:(c + 1) * 128],
                    A["id_f"][0:32, 0:32], start=True, stop=True),
                   reads=(srcn, "id_f"), writes=("P0",) if i == 0 else ())
            op("t", lambda e: e.matmul(A["P0"][0:1, 508:509], A["onesc_bf"][0:1, 0:1],
                                       A["onesc_bf"][0:1, 0:1], start=True, stop=True),
               reads=("onesc_bf",), writes=("P0",))
            for (i, dstn) in ((0, "g_tm"), (1, "lb_tm"), (2, "be_tm")):
                op("v", lambda e, i=i, dstn=dstn: e.tensor_copy(out=A[dstn][:], in_=A["P0"][:, i * 32:(i + 1) * 32]),
                   reads=("P0",), writes=(dstn,), bump=True)
            if dlim[1] == 0 and dlim[2] < 2:
                continue
            op("t", lambda e: e.matmul(A["P1"][:, 0:32], A["triu_f"][:], A["g_tm"][:], start=True, stop=True),
               reads=("triu_f", "g_tm"), writes=("P1",))
            op("t", lambda e: e.matmul(A["P1"][0:1, 508:509], A["onesc_bf"][0:1, 0:1],
                                       A["onesc_bf"][0:1, 0:1], start=True, stop=True),
               reads=("onesc_bf",), writes=("P1",))
            op("v", lambda e: e.tensor_copy(out=A["GG"][:, 32:64], in_=A["P1"][:, 0:32]),
               reads=("P1",), writes=("GG",), bump=True)
            op("v", lambda e, c=c: e.scalar_tensor_tensor(
                out=GGe, in0=lnss3[:, 16:32, c:c + 1].broadcast_to([128, 16, 2]), scalar=0.5,
                in1=Gve, op0=ALU.mult, op1=ALU.add), reads=("lnss", "GG"), writes=("GG",), bump=True)
            op("v", lambda e, c=c: e.scalar_tensor_tensor(
                out=A["Gtil"][:].rearrange("p (h d) -> p h d", d=2),
                in0=lnss3[:, 16:32, c:c + 1].broadcast_to([128, 16, 2]), scalar=-0.5,
                in1=Gve, op0=ALU.mult, op1=ALU.add), reads=("lnss", "GG"), writes=("Gtil",))
            op("v", lambda e: e.tensor_tensor(out=A["Gtil"][:], in0=A["Gtil"][:], in1=A["lb_tm"][:],
                                              op=ALU.add), reads=("Gtil", "lb_tm"), writes=("Gtil",),
               bump=True)
            op("v", lambda e, c=c: e.scalar_tensor_tensor(
                out=A["Ghat"][:].rearrange("p (h d) -> p h d", d=2),
                in0=lnss3[:, 0:16, c:c + 1].broadcast_to([128, 16, 2]), scalar=-0.5,
                in1=Gve, op0=ALU.mult, op1=ALU.add), reads=("lnss", "GG"), writes=("Ghat",), bump=True)
            if dlim[1] == 0 and dlim[2] < 3:
                continue
            op("t", lambda e: e.transpose(A["P2"][0:64, 0:128], A["GG"][:], A["id_f"][:]),
               reads=("GG", "id_f"), writes=("P2",))
            op("t", lambda e: e.matmul(A["P2"][0:1, 508:509], A["onesc_bf"][0:1, 0:1],
                                       A["onesc_bf"][0:1, 0:1], start=True, stop=True),
               reads=("onesc_bf",), writes=("P2",))
            if dlim[1] == 0 and dlim[2] < 30:
                continue
            op("v", lambda e: e.tensor_copy(out=A["GGT"][:], in_=A["P2"][0:64, 0:128]),
               reads=("P2",), writes=("GGT",), bump=True)
            if dlim[1] == 0 and dlim[2] < 31:
                continue
            op("y", lambda e, c=c: e.dma_start(out=dram["gg_sp"][c % 2:c % 2 + 1].rearrange("b p f -> (b p) f"),
                                                  in_=A["GGT"][:]),
               dma=True, reads=("GGT",), writes=(f"gg{c % 2}",))
            if dlim[1] == 0 and dlim[2] < 32:
                continue
            op("y", lambda e, c=c: e.dma_start(
                out=dram["gc_sp"][c % 2:c % 2 + 1].rearrange("b (f o) -> (b f) o", o=1),
                in_=A["GGT"][32:64, 127:128]), dma=True, reads=("GGT",), writes=(f"ggc{c % 2}",))
            if dlim[1] == 0 and dlim[2] < 4:
                continue
            op("y", lambda e, c=c: e.dma_start(
                out=A["grw"][:],
                in_=dram["gg_sp"][c % 2:c % 2 + 1].rearrange("b p f -> b (p f)")[0:1, 0:4096]
                    .partition_broadcast(128).rearrange("p o f -> p (o f)")),
               dma=True, reads=(f"gg{c % 2}",), writes=("grw",))
            op("y", lambda e, c=c: e.dma_start(
                out=A["gcrep"][:],
                in_=dram["gc_sp"][c % 2:c % 2 + 1].partition_broadcast(128).rearrange("p o f -> p (o f)")),
               dma=True, reads=(f"ggc{c % 2}",), writes=("gcrep",))
            if dlim[1] == 0 and dlim[2] < 5:
                continue
            op("v", lambda e: e.tensor_sub(out=A["csub"][:], in0=A["gcrep"][:], in1=A["GG"][:, 0:32]),
               reads=("gcrep", "GG"), writes=("csub",), bump=True)
            op("s", lambda e: e.activation(out=A["e1"][:], in_=A["Gtil"][:], func=AF.Exp),
               reads=("Gtil",), writes=("e1",), bump=True)
            op("s", lambda e: e.activation(out=A["e2"][:], in_=A["Ghat"][:], func=AF.Exp,
                                           bias=A["logcc"][:]),
               reads=("Ghat", "logcc"), writes=("e2",), bump=True)
            op("s", lambda e: e.activation(out=A["cS"][:], in_=A["csub"][:], func=AF.Exp),
               reads=("csub",), writes=("cS",), bump=True)
            op("s", lambda e: e.activation(out=A["egc"][:], in_=A["gcrep"][:], func=AF.Exp),
               reads=("gcrep",), writes=("egc",), bump=True)
            op("v", lambda e: e.tensor_copy(out=A["S0b"][:], in_=A["S"][:]), reads=("S",), writes=("S0b",))
            if dbg and c == 0:
                op("y", lambda e: e.dma_start(out=dram["dbg0"][:, 0:64], in_=A["GG"][:]),
                   dma=True, reads=("GG",), writes=("ddbg0",))
                op("y", lambda e: e.dma_start(out=dram["dbg1"][:], in_=A["grw"][:, 0:512]),
                   dma=True, reads=("grw",), writes=("ddbg1",))
                op("v", lambda e: e.tensor_copy(out=A["GG"][0:32, 0:1], in_=A["negea"][:]),
                   reads=("negea",), writes=("GGdbg",), bump=True)
                op("y", lambda e: e.dma_start(out=dram["dbg0"][0:32, 384:416],
                                              in_=A["gT"][:, 0:32]),
                   dma=True, reads=("gT",), writes=("ddbg0",))
                op("y", lambda e: e.dma_start(out=dram["dbg0"][0:32, 416:417],
                                              in_=A["GG"][0:32, 0:1],
                                              allow_slow_non_contiguous=True),
                   dma=True, reads=("GGdbg",), writes=("ddbg0",))
                for nm_, off_ in (("Gtil", 64), ("Ghat", 96), ("e1", 128), ("e2", 160),
                                  ("cS", 192), ("egc", 224), ("gcrep", 256), ("g_tm", 288),
                                  ("lb_tm", 320), ("be_tm", 352)):
                    op("y", lambda e, nm_=nm_, off_=off_: e.dma_start(
                        out=dram["dbg0"][:, off_:off_ + 32], in_=A[nm_][:]),
                       dma=True, reads=(nm_,), writes=("ddbg0",))
            if c + 1 < NCH:
                chunk_loads(c + 1)
            for gi in range(min(HV // 4, dlim[1])):
                v0 = gi * 4
                kq0 = v0 // 2
                for kk in range(2):
                    op("t", lambda e, kk=kk, kq0=kq0, kb=kb: e.matmul(
                        A["P0"][:, kk * 128:(kk + 1) * 128], A[kb][:, (kq0 + kk) * 128:(kq0 + kk + 1) * 128],
                        A[kb][:, (kq0 + kk) * 128:(kq0 + kk + 1) * 128], start=True, stop=True),
                       reads=(kb,), writes=("P0",) if kk == 0 else ())
                    op("t", lambda e, kk=kk, kq0=kq0, qb=qb, kb=kb: e.matmul(
                        A["P0"][:, 256 + kk * 128:256 + (kk + 1) * 128],
                        A[qb][:, (kq0 + kk) * 128:(kq0 + kk + 1) * 128],
                        A[kb][:, (kq0 + kk) * 128:(kq0 + kk + 1) * 128], start=True, stop=True),
                       reads=(qb, kb))
                op("t", lambda e: e.nop(), writes=("P0",))
                if dlim[2] < 1:
                    continue
                bb4 = lambda t_: t_.broadcast_to([128, 4, 128])
                h3 = lambda name: A[name][:].rearrange("p (h s) -> p h s", h=4)
                op("v", lambda e, v0=v0: e.scalar_tensor_tensor(
                    out=h3("Din"), in0=A["grw"][:, v0 * 128:(v0 + 4) * 128].rearrange("p (h s) -> p h s", h=4),
                    scalar=-1.0, in1=bb4(A["Gtil"][:, v0:v0 + 4]), op0=ALU.mult, op1=ALU.add),
                   reads=("grw", "Gtil"), writes=("Din",))
                op("v", lambda e: e.tensor_tensor(out=A["Din"][:], in0=A["Din"][:], in1=A["maskA"][:],
                                                  op=ALU.add), reads=("Din", "maskA"), writes=("Din",))
                op("v", lambda e, v0=v0: e.scalar_tensor_tensor(
                    out=h3("Din2"), in0=A["grw"][:, v0 * 128:(v0 + 4) * 128].rearrange("p (h s) -> p h s", h=4),
                    scalar=-1.0, in1=bb4(A["Ghat"][:, v0:v0 + 4]), op0=ALU.mult, op1=ALU.add),
                   reads=("grw", "Ghat"), writes=("Din2",))
                op("v", lambda e: e.tensor_tensor(out=A["Din2"][:], in0=A["Din2"][:], in1=A["maskM"][:],
                                                  op=ALU.add), reads=("Din2", "maskM"), writes=("Din2",))
                op("s", lambda e: e.activation(out=A["EA"][:], in_=A["Din"][:], func=AF.Exp),
                   reads=("Din",), writes=("EA",))
                op("s", lambda e: e.activation(out=A["EM"][:], in_=A["Din2"][:], func=AF.Exp,
                                               bias=A["logcc"][:]),
                   reads=("Din2", "logcc"), writes=("EM",))
                for kk in range(2):
                    kkx = A["P0"][:, kk * 128:(kk + 1) * 128].rearrange(
                        "p (o s) -> p o s", o=1).broadcast_to([128, 2, 128])
                    qkx = A["P0"][:, 256 + kk * 128:256 + (kk + 1) * 128].rearrange(
                        "p (o s) -> p o s", o=1).broadcast_to([128, 2, 128])
                    op("v", lambda e, kkx=kkx, kk=kk: e.scalar_tensor_tensor(
                        out=A["B4"][:, kk * 256:(kk + 1) * 256].rearrange("p (d s) -> p d s", d=2),
                        in0=kkx, scalar=-1.0,
                        in1=A["EA"][:, kk * 256:(kk + 1) * 256].rearrange("p (d s) -> p d s", d=2),
                        op0=ALU.mult, op1=ALU.mult),
                       reads=("P0", "EA"), writes=("B4",))
                    op("v", lambda e, qkx=qkx, kk=kk: e.scalar_tensor_tensor(
                        out=A["M4"][:, kk * 256:(kk + 1) * 256].rearrange("p (d s) -> p d s", d=2),
                        in0=qkx, scalar=1.0,
                        in1=A["EM"][:, kk * 256:(kk + 1) * 256].rearrange("p (d s) -> p d s", d=2),
                        op0=ALU.mult, op1=ALU.mult),
                       reads=("P0", "EM"), writes=("M4",))
                if dlim[2] < 2:
                    continue
                for j in range(4):
                    op("t", lambda e, j=j: e.matmul(A["P1"][:, j * 128:(j + 1) * 128],
                                                    A["B4"][:, j * 128:(j + 1) * 128], A["id_bf"][:],
                                                    start=True, stop=True),
                       reads=("B4", "id_bf"), writes=("P1",) if j == 0 else ())
                for j in range(4):
                    op("t", lambda e, j=j: e.matmul(A["P2"][:, j * 128:(j + 1) * 128],
                                                    A["M4"][:, j * 128:(j + 1) * 128], A["id_bf"][:],
                                                    start=True, stop=True),
                       reads=("M4", "id_bf"), writes=("P2",) if j == 0 else ())
                for j in range(4):
                    op("t", lambda e, j=j, v0=v0, kq0=kq0, kb=kb: e.matmul(
                        A["P3"][:, j * 128:(j + 1) * 128],
                        A[kb][:, (kq0 + j // 2) * 128:(kq0 + j // 2 + 1) * 128],
                        A["S0b"][:, (v0 + j) * 128:(v0 + j + 1) * 128], start=True, stop=True),
                       reads=(kb, "S0b"), writes=("P3",) if j == 0 else ())
                for j in range(4):
                    op("t", lambda e, j=j, v0=v0, kq0=kq0, qb=qb: e.matmul(
                        A["P4"][:, j * 128:(j + 1) * 128],
                        A[qb][:, (kq0 + j // 2) * 128:(kq0 + j // 2 + 1) * 128],
                        A["S0b"][:, (v0 + j) * 128:(v0 + j + 1) * 128], start=True, stop=True),
                       reads=(qb, "S0b"), writes=("P4",) if j == 0 else ())
                op("t", lambda e: e.nop(), writes=("P1", "P2", "P3", "P4"))
                op("v", lambda e: e.tensor_copy(out=A["BT4"][:], in_=A["P1"][:]), reads=("P1",), writes=("BT4",))
                op("v", lambda e: e.tensor_copy(out=A["MT4"][:], in_=A["P2"][:]), reads=("P2",), writes=("MT4",))
                op("v", lambda e, v0=v0: e.scalar_tensor_tensor(
                    out=h3("ot4"), in0=A["P4"][:].rearrange("p (h s) -> p h s", h=4), scalar=1.0,
                    in1=bb4(A["e2"][:, v0:v0 + 4]), op0=ALU.mult, op1=ALU.mult),
                   reads=("P4", "e2"), writes=("ot4",))
                op("v", lambda e, v0=v0: e.scalar_tensor_tensor(
                    out=h3("t2f"), in0=A["P3"][:].rearrange("p (h s) -> p h s", h=4), scalar=1.0,
                    in1=bb4(A["e1"][:, v0:v0 + 4]), op0=ALU.mult, op1=ALU.mult),
                   reads=("P3", "e1"), writes=("t2f",))
                op("v", lambda e, v0=v0, vb=vb: e.scalar_tensor_tensor(
                    out=h3("t3f"), in0=A[vb][:, v0 * 128:(v0 + 4) * 128].rearrange("p (h s) -> p h s", h=4),
                    scalar=1.0, in1=bb4(A["be_tm"][:, v0:v0 + 4]), op0=ALU.mult, op1=ALU.mult),
                   reads=(vb, "be_tm"), writes=("t3f",))
                op("v", lambda e: e.tensor_sub(out=A["Yb"][:],
                                               in0=A["t3f"][:], in1=A["t2f"][:]),
                   reads=("t3f", "t2f"), writes=("Yb",))
                if dbg and c == 0 and gi == 0:
                    op("v", lambda e: e.tensor_copy(out=A["Din"][:], in_=A["Yb"][:]),
                       reads=("Yb",), writes=("Din",))
                    op("y", lambda e: e.dma_start(out=dram["dbg2"][:], in_=A["Din"][:]),
                       dma=True, reads=("Din",), writes=("ddbg2",))
                    op("v", lambda e: e.tensor_copy(out=A["Din"][:], in_=A["B4"][:]),
                       reads=("B4",), writes=("Din",))
                    op("y", lambda e: e.dma_start(out=dram["dbg3"][:], in_=A["Din"][:]),
                       dma=True, reads=("Din",), writes=("ddbg3",))
                    op("v", lambda e: e.tensor_copy(out=A["Din"][:], in_=A["M4"][:]),
                       reads=("M4",), writes=("Din",))
                    op("y", lambda e: e.dma_start(out=dram["dbg4"][:], in_=A["Din"][:]),
                       dma=True, reads=("Din",), writes=("ddbg4",))
                if dlim[2] < 3:
                    continue
                zcur, pcur = "BT4", "B4"
                for j in range(7):
                    for jj in range(4):
                        op("t", lambda e, jj=jj, zcur=zcur: e.matmul(
                            A["P5"][:, jj * 128:(jj + 1) * 128], A[zcur][:, jj * 128:(jj + 1) * 128],
                            A["Yb"][:, jj * 128:(jj + 1) * 128], start=True, stop=True),
                           reads=(zcur, "Yb"), writes=("P5",) if jj == 0 else ())
                    if j < 6:
                        for jj in range(4):
                            op("t", lambda e, jj=jj, zcur=zcur, pcur=pcur: e.matmul(
                                A["P6"][:, jj * 128:(jj + 1) * 128], A[pcur][:, jj * 128:(jj + 1) * 128],
                                A[zcur][:, jj * 128:(jj + 1) * 128], start=True, stop=True),
                               reads=(pcur, zcur), writes=("P6",) if jj == 0 else ())
                        for jj in range(4):
                            op("t", lambda e, jj=jj, zcur=zcur, pcur=pcur: e.matmul(
                                A["P7"][:, jj * 128:(jj + 1) * 128], A[zcur][:, jj * 128:(jj + 1) * 128],
                                A[pcur][:, jj * 128:(jj + 1) * 128], start=True, stop=True),
                               reads=(zcur, pcur), writes=("P7",) if jj == 0 else ())
                    op("t", lambda e: e.nop(), writes=("P5", "P6", "P7") if j < 6 else ("P5",))
                    op("v", lambda e: e.tensor_add(out=A["Yb"][:], in0=A["P5"][:], in1=A["Yb"][:]),
                       reads=("P5", "Yb"), writes=("Yb",))
                    if j < 6:
                        op("v", lambda e: e.tensor_copy(out=A["Zb"][:], in_=A["P6"][:]),
                           reads=("P6",), writes=("Zb",))
                        op("v", lambda e: e.tensor_copy(out=A["Pb"][:], in_=A["P7"][:]),
                           reads=("P7",), writes=("Pb",))
                        zcur, pcur = "Zb", "Pb"
                op("v", lambda e: e.tensor_copy(out=A["Ub"][:], in_=A["Yb"][:]), reads=("Yb",), writes=("Ub",))
                if dlim[2] < 4:
                    continue
                if dbg and c == 0 and gi == 0:
                    op("v", lambda e: e.tensor_copy(out=A["Din"][:], in_=A["Ub"][:]),
                       reads=("Ub",), writes=("Din",))
                    op("y", lambda e: e.dma_start(out=dram["dbg5"][:], in_=A["Din"][:]),
                       dma=True, reads=("Din",), writes=("ddbg5",))
                for jj in range(4):
                    op("t", lambda e, jj=jj: e.matmul(
                        A["P5"][:, jj * 128:(jj + 1) * 128], A["MT4"][:, jj * 128:(jj + 1) * 128],
                        A["Ub"][:, jj * 128:(jj + 1) * 128], start=True, stop=True),
                       reads=("MT4", "Ub"), writes=("P5",) if jj == 0 else ())
                op("t", lambda e: e.nop(), writes=("P5",))
                op("v", lambda e: e.tensor_add(out=A["o4"][:], in0=A["P5"][:], in1=A["ot4"][:]),
                   reads=("P5", "ot4"), writes=("o4",))
                if dbg and c == 0 and gi == 0:
                    op("y", lambda e: e.dma_start(out=dram["dbg6"][:], in_=A["o4"][:]),
                       dma=True, reads=("o4",), writes=("ddbg6",))
                    op("y", lambda e: e.dma_start(out=dram["dbg7"][:], in_=A["ot4"][:]),
                       dma=True, reads=("ot4",), writes=("ddbg7",))
                op("v", lambda e, v0=v0, zb=zb: e.tensor_mul(out=A["yg"][:], in0=A["o4"][:],
                                                             in1=A[zb][:, v0 * 128:(v0 + 4) * 128]),
                   reads=("o4", zb), writes=("yg",))
                op("v", lambda e: e.tensor_mul(out=A["ygs"][:], in0=A["yg"][:], in1=A["yg"][:]),
                   reads=("yg",), writes=("ygs",))
                op("v", lambda e, v0=v0: e.tensor_reduce(
                    out=A["msq"][:, v0:v0 + 4], in_=A["ygs"][:].rearrange("p (h s) -> p h s", h=4),
                    axis=AX.X, op=ALU.add), reads=("ygs",), writes=("msq",), bump=True)
                op("s", lambda e, v0=v0: e.activation(out=A["rr"][:, v0:v0 + 4], in_=A["msq"][:, v0:v0 + 4],
                                                      func=AF.Ln, bias=A["epsrms"][:], scale=1.0 / DV),
                   reads=("msq", "epsrms"), writes=("rr",))
                op("s", lambda e, v0=v0: e.activation(out=A["rr"][:, v0:v0 + 4], in_=A["rr"][:, v0:v0 + 4],
                                                      func=AF.Exp, scale=-0.5), reads=("rr",), writes=("rr",),
                   bump=True)
                op("v", lambda e, v0=v0: e.scalar_tensor_tensor(
                    out=A["y4"][:].rearrange("p (h s) -> p h s", h=4),
                    in0=A["yg"][:].rearrange("p (h s) -> p h s", h=4), scalar=1.0,
                    in1=bb4(A["rr"][:, v0:v0 + 4]), op0=ALU.mult, op1=ALU.mult),
                   reads=("yg", "rr"), writes=("y4",))
                for jj in range(4):
                    op("t", lambda e, jj=jj: e.matmul(
                        A["P6"][:, jj * 128:(jj + 1) * 128], A["y4"][:, jj * 128:(jj + 1) * 128],
                        A["id_bf"][:], start=True, stop=True),
                       reads=("y4", "id_bf"), writes=("P6",) if jj == 0 else ())
                op("t", lambda e: e.nop(), writes=("P6",))
                op("v", lambda e, v0=v0: e.tensor_copy(out=A["ystg"][:, v0 * 128:(v0 + 4) * 128],
                                                       in_=A["P6"][:]), reads=("P6",), writes=("ystg",))
                op("v", lambda e, v0=v0: e.scalar_tensor_tensor(
                    out=A["Ut"][:].rearrange("p (h s) -> p h s", h=4),
                    in0=A["Ub"][:].rearrange("p (h s) -> p h s", h=4), scalar=1.0,
                    in1=bb4(A["cS"][:, v0:v0 + 4]), op0=ALU.mult, op1=ALU.mult),
                   reads=("Ub", "cS"), writes=("Ut",))
                for jj in range(4):
                    op("t", lambda e, jj=jj, v0=v0, kq0=kq0, ktb=ktb: e.matmul(
                        A["P7"][:, jj * 128:(jj + 1) * 128],
                        A[ktb][:, (kq0 + jj // 2) * 128:(kq0 + jj // 2 + 1) * 128],
                        A["Ut"][:, jj * 128:(jj + 1) * 128], start=True, stop=True),
                       reads=(ktb, "Ut"), writes=("P7",) if jj == 0 else ())
                op("t", lambda e: e.nop(), writes=("P7",))
                op("v", lambda e, v0=v0: e.scalar_tensor_tensor(
                    out=A["S"][:, v0 * 128:(v0 + 4) * 128].rearrange("p (h s) -> p h s", h=4),
                    in0=A["S"][:, v0 * 128:(v0 + 4) * 128].rearrange("p (h s) -> p h s", h=4),
                    scalar=1.0, in1=bb4(A["egc"][:, v0:v0 + 4]), op0=ALU.mult, op1=ALU.mult),
                   reads=("S", "egc"), writes=())
                op("v", lambda e, v0=v0: e.tensor_add(out=A["S"][:, v0 * 128:(v0 + 4) * 128],
                                                      in0=A["S"][:, v0 * 128:(v0 + 4) * 128],
                                                      in1=A["P7"][:]),
                   reads=("P7", "S"), writes=("S",))
            op("y", lambda e, c=c: e.dma_start(
                out=dram["y_sp"][:, c * 128:(c + 1) * 128].rearrange("(h p) t -> p h t", p=128),
                in_=A["ystg"][:].rearrange("p (h t) -> p h t", t=128)), dma=True, reads=("ystg",))
        p.barrier()

        if maxphase < 7:
            finwait()
            return
        for dq in range(4):
            op("g", lambda e, dq=dq: e.dma_start(
                out=A["wo_sb"][:].rearrange("p (t c) -> p t c", t=32),
                in_=dram["wout"][:, dq * 512:(dq + 1) * 512].rearrange(
                    "(t p) c -> p t c", p=128)), dma=True, writes=("wo_sb",))
            op("v", lambda e: e.tensor_scalar(out=A["wo_sb"][:], in0=A["wo_sb"][:], scalar1=A["nw"][:],
                                              scalar2=None, op0=ALU.mult),
               reads=("wo_sb", "nw"), writes=("wo_sb",))
            for tt in range(NT):
                yb_ = f"yt{tt % 2}"
                op("y", lambda e, yb_=yb_, tt=tt: e.dma_start(
                    out=A[yb_][:].rearrange("p (h t) -> p h t", t=128),
                    in_=dram["y_sp"][:, tt * 128:(tt + 1) * 128].rearrange(
                        "(h p) t -> p h t", p=128)), dma=True, writes=(yb_,))
                bk = f"P{tt % 2}"
                for hv in range(HV):
                    op("t", lambda e, bk=bk, hv=hv, yb_=yb_: e.matmul(
                        A[bk][:], A[yb_][:, hv * 128:(hv + 1) * 128],
                        A["wo_sb"][:, hv * 512:(hv + 1) * 512], start=(hv == 0), stop=(hv == HV - 1)),
                       reads=(yb_, "wo_sb"), writes=(bk,) if hv == 0 else ())
                op("t", lambda e: e.nop(), writes=(bk,))
                if dq < 3:
                    op("v", lambda e, bk=bk, dq=dq: e.tensor_copy(out=A["mstage"][:, 0:512], in_=A[bk][:]),
                       reads=(bk,), writes=("mslot",))
                    op("y", lambda e, tt=tt, dq=dq: e.dma_start(
                        out=dram["m0_sp"][tt * 128:(tt + 1) * 128, dq * 512:(dq + 1) * 512],
                        in_=A["mstage"][:, 0:512]), dma=True, reads=("mslot",), writes=("dm0sp",))
                else:
                    op("y", lambda e, tt=tt: e.dma_start(
                        out=A["mstage"][:, 0:1536], in_=dram["m0_sp"][tt * 128:(tt + 1) * 128, :]),
                       dma=True, reads=("dm0sp",), writes=("mstage",))
                    op("v", lambda e, bk=bk: e.tensor_copy(out=A["mstage"][:, 1536:2048], in_=A[bk][:]),
                       reads=(bk,), writes=("mstage",))
                    ln_full("mstage", "lnst4", "lnt4", None)
                    op("y", lambda e, tt=tt: e.dma_start(out=A["xa4"][:],
                                                         in_=dram["x"][tt * 128:(tt + 1) * 128, :]),
                       dma=True, writes=("xa4",))
                    op("v", lambda e: e.scalar_tensor_tensor(
                        out=A["x1t"][:], in0=A["mstage"][:], scalar=A["mcol"][:],
                        in1=A["rstd"][:].broadcast_to([128, D]), op0=ALU.subtract, op1=ALU.mult),
                       reads=("mstage", "mcol", "rstd"), writes=("x1t",))
                    op("v", lambda e: e.tensor_add(out=A["x1t"][:], in0=A["x1t"][:], in1=A["xa4"][:]),
                       reads=("x1t", "xa4"), writes=("x1t",))
                    op("y", lambda e, tt=tt: e.dma_start(out=dram["x1_sp"][tt * 128:(tt + 1) * 128, :],
                                                         in_=A["x1t"][:]), dma=True, reads=("x1t",))
                    ln_full("x1t", "lnst4", "lnt4", None)
                    transpose_to_lnd("lnt4", tt, 2)
        p.barrier()

        if maxphase < 8:
            finwait()
            return
        for et in range(64):
            bi = et % 2
            wub, wgb = f"wu{bi}", f"wg{bi}"
            op("y", lambda e, wub=wub, et=et: e.dma_start(
                out=A[wub][:].rearrange("p (t c) -> p t c", t=16),
                in_=dram["wupT"][:, et * 128:(et + 1) * 128].rearrange(
                    "(t p) c -> p t c", p=128)), dma=True, writes=(wub,))
            op("y", lambda e, wgb=wgb, et=et: e.dma_start(
                out=A[wgb][:].rearrange("p (t c) -> p t c", t=16),
                in_=dram["wgT"][:, et * 128:(et + 1) * 128].rearrange(
                    "(t p) c -> p t c", p=128)), dma=True, writes=(wgb,))
            sg1, sg2 = f"sgn1{bi}", f"sgn2{bi}"
            for (wb_, tq, sc) in ((wub, f"tqu{bi}", 3), (wgb, f"tqg{bi}", 4)):
                op("s", lambda e, wb_=wb_, sc=sc, sg1=sg1: e.activation(out=A[sg1][:], in_=A[wb_][:], func=AF.Sign,
                                                               bias=A["srep"][:, sc:sc + 1], scale=-1.0),
                   reads=(wb_, "srep"), writes=(sg1,))
                op("s", lambda e, wb_=wb_, sc=sc, sg2=sg2: e.activation(out=A[sg2][:], in_=A[wb_][:], func=AF.Sign,
                                                               bias=A["srep"][:, sc:sc + 1], scale=1.0),
                   reads=(wb_, "srep"), writes=(sg2,))
                op("v", lambda e, tq=tq, sg1=sg1, sg2=sg2: e.tensor_sub(out=A[tq][:], in0=A[sg2][:], in1=A[sg1][:]),
                   reads=(sg1, sg2), writes=(tq,))
            for tp in range(L // 512):
                bk = f"P{tp % 2}"
                for dtile in range(16):
                    op("t", lambda e, bk=bk, dtile=dtile, tp=tp: e.matmul(
                        A[bk][:], A[f"tqu{bi}"][:, dtile * 128:(dtile + 1) * 128],
                        lnd3[:, dtile, tp * 512:(tp + 1) * 512], start=(dtile == 0), stop=(dtile == 15)),
                       reads=(f"tqu{bi}", "lnd"), writes=(bk,) if dtile == 0 else ())
                op("t", lambda e: e.nop(), writes=(bk,))
                op("s", lambda e, bk=bk, tp=tp: e.activation(
                    out=A["hst"][:, tp * 512:(tp + 1) * 512], in_=A[bk][:], func=AF.Silu,
                    scale=A["srep"][:, 0:1]), reads=(bk, "srep"), writes=("hst",))
            for tp in range(L // 512):
                bk = f"P{2 + tp % 2}"
                for dtile in range(16):
                    op("t", lambda e, bk=bk, dtile=dtile, tp=tp: e.matmul(
                        A[bk][:], A[f"tqg{bi}"][:, dtile * 128:(dtile + 1) * 128],
                        lnd3[:, dtile, tp * 512:(tp + 1) * 512], start=(dtile == 0), stop=(dtile == 15)),
                       reads=(f"tqg{bi}", "lnd"), writes=(bk,) if dtile == 0 else ())
                op("t", lambda e: e.nop(), writes=(bk,))
                op("v", lambda e, bk=bk: e.tensor_scalar(out=A["gsc"][:], in0=A[bk][:],
                                                         scalar1=A["srep"][:, 1:2], scalar2=None,
                                                         op0=ALU.mult), reads=(bk, "srep"), writes=("gsc",))
                op("v", lambda e, tp=tp: e.tensor_mul(out=A["hst"][:, tp * 512:(tp + 1) * 512],
                                                      in0=A["hst"][:, tp * 512:(tp + 1) * 512],
                                                      in1=A["gsc"][:]),
                   reads=("hst", "gsc"), writes=("hst",))
            op("y", lambda e, et=et: e.dma_start(out=dram["h_sp"][et * 128:(et + 1) * 128, :],
                                                 in_=A["hst"][:]), dma=True, reads=("hst",))
        p.barrier()

        if maxphase < 9:
            finwait()
            return
        for dq in range(4):
            for et in range(64):
                bi = et % 2
                wdb = f"wd{bi}"
                op("y", lambda e, wdb=wdb, et=et, dq=dq: e.dma_start(
                    out=A[wdb][:], in_=dram["wdT"][et * 128:(et + 1) * 128, dq * 512:(dq + 1) * 512]),
                   dma=True, writes=(wdb,))
                op("s", lambda e, wdb=wdb: e.activation(out=A["sg6a"][:], in_=A[wdb][:], func=AF.Sign,
                                                        bias=A["srep"][:, 5:6], scale=-1.0),
                   reads=(wdb, "srep"), writes=("sg6a",))
                op("s", lambda e, wdb=wdb: e.activation(out=A["sg6b"][:], in_=A[wdb][:], func=AF.Sign,
                                                        bias=A["srep"][:, 5:6], scale=1.0),
                   reads=(wdb, "srep"), writes=("sg6b",))
                op("v", lambda e, et=et: e.tensor_sub(out=A["td"][:, et * 512:(et + 1) * 512],
                                                      in0=A["sg6b"][:], in1=A["sg6a"][:]),
                   reads=("sg6a", "sg6b"), writes=("td",))
            for tt in range(NT):
                bi = tt % 2
                hb_ = f"hb{bi}"
                op("y", lambda e, hb_=hb_, tt=tt: e.dma_start(
                    out=A[hb_][:].rearrange("p (h t) -> p h t", t=128),
                    in_=dram["h_sp"][:, tt * 128:(tt + 1) * 128].rearrange(
                        "(h p) t -> p h t", p=128)), dma=True, writes=(hb_,))
                bk = f"P{tt % 2}"
                for et in range(64):
                    op("t", lambda e, bk=bk, et=et, hb_=hb_: e.matmul(
                        A[bk][:], A[hb_][:, et * 128:(et + 1) * 128],
                        A["td"][:, et * 512:(et + 1) * 512], start=(et == 0), stop=(et == 63)),
                       reads=(hb_, "td"), writes=(bk,) if et == 0 else ())
                op("t", lambda e: e.nop(), writes=(bk,))
                if dq < 3:
                    op("v", lambda e, bk=bk: e.tensor_scalar(out=A["fst"][:, 0:512], in0=A[bk][:],
                                                             scalar1=A["srep"][:, 2:3], scalar2=None,
                                                             op0=ALU.mult), reads=(bk, "srep"),
                       writes=("fslot",))
                    op("y", lambda e, tt=tt, dq=dq: e.dma_start(
                        out=dram["f0_sp"][tt * 128:(tt + 1) * 128, dq * 512:(dq + 1) * 512],
                        in_=A["fst"][:, 0:512]), dma=True, reads=("fslot",), writes=("df0sp",))
                else:
                    op("y", lambda e, tt=tt: e.dma_start(
                        out=A["fst"][:, 0:1536], in_=dram["f0_sp"][tt * 128:(tt + 1) * 128, :]),
                       dma=True, reads=("df0sp",), writes=("fst",))
                    op("v", lambda e, bk=bk: e.tensor_scalar(out=A["fst"][:, 1536:2048], in0=A[bk][:],
                                                             scalar1=A["srep"][:, 2:3], scalar2=None,
                                                             op0=ALU.mult), reads=(bk, "srep"),
                       writes=("fst",))
                    op("v", lambda e: e.tensor_reduce(out=A["sumx"][:], in_=A["fst"][:], axis=AX.X,
                                                      op=ALU.add), reads=("fst",), writes=("sumx",))
                    op("v", lambda e: e.scalar_tensor_tensor(out=A["lnst6"][:], in0=A["fst"][:], scalar=1.0,
                                                             in1=A["fst"][:], op0=ALU.mult, op1=ALU.mult,
                                                             accum_out=A["sumq"][:]),
                       reads=("fst",), writes=("lnst6", "sumq"))
                    op("v", lambda e: e.tensor_scalar(out=A["mcol"][:], in0=A["sumx"][:], scalar1=1.0 / D,
                                                      scalar2=None, op0=ALU.mult),
                       reads=("sumx",), writes=("mcol",))
                    op("v", lambda e: e.tensor_tensor(out=A["vcol"][:], in0=A["mcol"][:], in1=A["mcol"][:],
                                                      op=ALU.mult), reads=("mcol",), writes=("vcol",))
                    op("v", lambda e: e.scalar_tensor_tensor(out=A["vcol"][:], in0=A["sumq"][:],
                                                             scalar=1.0 / D, in1=A["vcol"][:],
                                                             op0=ALU.mult, op1=ALU.subtract),
                       reads=("sumq", "vcol"), writes=("vcol",), bump=True)
                    op("s", lambda e: e.activation(out=A["rstd"][:], in_=A["vcol"][:], func=AF.Ln,
                                                   bias=A["epsln"][:]), reads=("vcol", "epsln"),
                       writes=("rstd",))
                    op("s", lambda e: e.activation(out=A["rstd"][:], in_=A["rstd"][:], func=AF.Exp,
                                                   scale=-0.5), reads=("rstd",), writes=("rstd",),
                       bump=True)
                    op("y", lambda e, tt=tt: e.dma_start(out=A["xa6"][:],
                                                         in_=dram["x1_sp"][tt * 128:(tt + 1) * 128, :]),
                       dma=True, writes=("xa6",))
                    op("v", lambda e: e.scalar_tensor_tensor(
                        out=A["x6t"][:], in0=A["fst"][:], scalar=A["mcol"][:],
                        in1=A["rstd"][:].broadcast_to([128, D]), op0=ALU.subtract, op1=ALU.mult),
                       reads=("fst", "mcol", "rstd"), writes=("x6t",))
                    op("v", lambda e: e.tensor_add(out=A["x6t"][:], in0=A["x6t"][:], in1=A["xa6"][:]),
                       reads=("x6t", "xa6"), writes=("x6t",))
                    op("y", lambda e, tt=tt: e.dma_start(out=dram["out"][tt * 128:(tt + 1) * 128, :],
                                                         in_=A["x6t"][:]), dma=True, reads=("x6t",))
        finwait()

    with nc.Block() as blk:
        @blk.tensor
        def _(e):
            program("t", e)

        @blk.vector
        def _(e):
            program("v", e)

        @blk.scalar
        def _(e):
            program("s", e)

        @blk.gpsimd
        def _(e):
            program("g", e)

        @blk.sync
        def _(e):
            program("y", e)

    stack.close()
    return nc


def host_inputs(core_x, W):
    c = np.ascontiguousarray
    cw = W["conv_w"].reshape(64, 128, KC).transpose(1, 0, 2).reshape(128, 64 * KC)
    return {
        "x": c(core_x.astype(np.float32, copy=False)),
        "wqkv": c(W["Wqkv"]), "wz": c(W["Wz"]),
        "wab": c(np.concatenate([W["Wa"], W["Wb"]], axis=1)),
        "cw": c(cw),
        "alog": c(W["A_log"].reshape(32, 1)), "dtb": c(W["dt_bias"].reshape(32, 1)),
        "nw": c(W["norm_w"].reshape(128, 1)),
        "wout": c(W["Wout"]),
        "wupT": c(W["W_up"].T), "wgT": c(W["W_gate"].T), "wdT": c(W["W_down"].T),
    }


def kernel(L0_attn_x, L0_attn_cu_seqlens, L0_attn_Wqkv, L0_attn_Wz, L0_attn_Wa, L0_attn_Wb,
           L0_attn_conv_w, L0_attn_A_log, L0_attn_dt_bias, L0_attn_norm_w, L0_attn_Wout,
           L0_ffn_W_up, L0_ffn_W_gate, L0_ffn_W_down):
    W = {
        "Wqkv": np.asarray(L0_attn_Wqkv, np.float32), "Wz": np.asarray(L0_attn_Wz, np.float32),
        "Wa": np.asarray(L0_attn_Wa, np.float32), "Wb": np.asarray(L0_attn_Wb, np.float32),
        "conv_w": np.asarray(L0_attn_conv_w, np.float32), "A_log": np.asarray(L0_attn_A_log, np.float32),
        "dt_bias": np.asarray(L0_attn_dt_bias, np.float32), "norm_w": np.asarray(L0_attn_norm_w, np.float32),
        "Wout": np.asarray(L0_attn_Wout, np.float32), "W_up": np.asarray(L0_ffn_W_up, np.float32),
        "W_gate": np.asarray(L0_ffn_W_gate, np.float32), "W_down": np.asarray(L0_ffn_W_down, np.float32),
    }
    x = np.asarray(L0_attn_x, np.float32)
    nc = build(L=x.shape[1])
    in_maps = [host_inputs(x[b], W) for b in range(x.shape[0])]
    res = run_bass_kernel_spmd(nc, in_maps, core_ids=list(range(x.shape[0])))
    return np.stack([res.results[b]["out"] for b in range(x.shape[0])], axis=0)


# version 07
# speedup vs baseline: 1.0491x; 1.0491x over previous
import numpy as np
import concourse.bass as bass
import concourse.mybir as mybir
from concourse.bass_utils import run_bass_kernel_spmd

F32 = mybir.dt.float32
BF16 = mybir.dt.bfloat16
AF = mybir.ActivationFunctionType
ALU = mybir.AluOpType
AX = mybir.AxisListType

D = 2048
HK, HV, DK, DV = 16, 32, 128, 128
ECH = 8192
KC = 4
C = 128
EPS_LN = 1e-5
EPS_L2 = 1e-6
EPS_RMS = 1e-6
EPS_TER = 1e-8
LOGC = float(-0.5 * np.log(DK))
NEG = -1e30

DEFSEM = {"t": "pe", "v": "ve", "s": "ac", "g": "po", "y": "so"}


class Prog:

    def __init__(self, E, eng, sems):
        self.E = E
        self.eng = eng
        self.sems = sems
        self.cnt = {k: 0 for k in sems}
        self.lastw = {}
        self.reads = {}
        self.dsems = {}
        self.dpool = iter([f"dm{i:02d}" for i in range(64)])
        self.junk = None

    def op(self, ec, fn, reads=(), writes=(), sem=None, n=1, xwait=(), dma=False, bump=False):
        if dma:
            key = writes[0] if writes else ("sp:" + reads[0])
            if key not in self.dsems:
                self.dsems[key] = next(self.dpool)
            sem = self.dsems[key]
            n = 16
        if sem is None:
            sem = DEFSEM[ec]
        waits = {}
        for b in reads:
            lw = self.lastw.get(b)
            if lw is not None:
                waits[lw[0]] = max(waits.get(lw[0], 0), lw[1])
        for b in writes:
            lw = self.lastw.get(b)
            if lw is not None:
                waits[lw[0]] = max(waits.get(lw[0], 0), lw[1])
            for rec, (s2, v2) in self.reads.get(b, {}).items():
                waits[s2] = max(waits.get(s2, 0), v2)
        for (s2, v2) in xwait:
            waits[s2] = max(waits.get(s2, 0), v2)
        if self.E == ec:
            for s2, v2 in waits.items():
                if v2 > 0:
                    self.eng.wait_ge(self.sems[s2], v2)
            ins = fn(self.eng)
            if bump:
                if ec == "v" or ec == "g":
                    ins = self.eng.tensor_copy(out=self.junk[0:1, 2:4], in_=self.junk[0:1, 0:2])
                elif ec == "s":
                    ins = self.eng.copy(out=self.junk[0:1, 2:4], in_=self.junk[0:1, 0:2])
            if ins is not None:
                ins.then_inc(self.sems[sem], n)
        self.cnt[sem] += n
        ms = (sem, self.cnt[sem])
        for b in reads:
            self.reads.setdefault(b, {})[ec] = ms
        for b in writes:
            self.lastw[b] = (sem, self.cnt[sem], ec)
            self.reads[b] = {}
        return ms

    def barrier(self):
        vals = []
        for ec in ("t", "v", "s", "g", "y"):
            s = DEFSEM[ec]
            self.cnt[s] += 1
            vals.append((ec, s, self.cnt[s]))
            if self.E == ec:
                self.eng.nop().then_inc(self.sems[s], 1)
        dma_waits = [(s, self.cnt[s]) for s in self.dsems.values()]
        for ec2 in ("t", "v", "s", "g", "y"):
            if self.E == ec2:
                for (ec, s, v) in vals:
                    if ec != ec2:
                        self.eng.wait_ge(self.sems[s], v)
                for (s, v) in dma_waits:
                    if v > 0:
                        self.eng.wait_ge(self.sems[s], v)
        self.lastw = {}
        self.reads = {}


def build(L=2048, dbg=False, maxphase=99, dlim=(99, 99, 99)):
    NT = L // 128
    NCH = L // C
    nc = bass.Bass("TRN2", target_bir_lowering=False, debug=False,
                   detect_race_conditions=False)

    dram = {}
    def dt_(name, shape, dtype, kind):
        dram[name] = nc.dram_tensor(name, shape, dtype, kind=kind).ap()
    dt_("x", [L, D], F32, "ExternalInput")
    dt_("wqkv", [D, ECH], F32, "ExternalInput")
    dt_("wz", [D, HV * DV], F32, "ExternalInput")
    dt_("wab", [D, 64], F32, "ExternalInput")
    dt_("cw", [128, 64 * KC], F32, "ExternalInput")
    dt_("alog", [32, 1], F32, "ExternalInput")
    dt_("dtb", [32, 1], F32, "ExternalInput")
    dt_("nw", [128, 1], F32, "ExternalInput")
    dt_("wout", [HV * DV, D], F32, "ExternalInput")
    dt_("wupT", [D, ECH], F32, "ExternalInput")
    dt_("wgT", [D, ECH], F32, "ExternalInput")
    dt_("wdT", [ECH, D], F32, "ExternalInput")
    dt_("out", [L, D], F32, "ExternalOutput")
    sk = "ExternalOutput" if dbg else "Internal"
    dt_("q_sp", [HK * DK, L], BF16, sk)
    dt_("k_sp", [HK * DK, L], BF16, sk)
    dt_("kt_sp", [L, HK * DK], BF16, sk)
    dt_("v_sp", [L, HV * DV], BF16, sk)
    dt_("sz_sp", [L, HV * DV], BF16, sk)
    dt_("y_sp", [HV * DV, L], BF16, sk)
    dt_("x1_sp", [L, D], F32, sk)
    dt_("h_sp", [ECH, L], BF16, sk)
    dt_("f0_sp", [L, 1536], F32, "Internal")
    dt_("m0_sp", [L, 1536], F32, "Internal")
    dt_("gg_sp", [2, 64, 128], F32, "Internal")
    dt_("gc_sp", [2, 32], F32, "Internal")
    dt_("scl_sp", [1, 8], F32, "Internal")
    if dbg:
        for i in range(10):
            dt_(f"dbg{i}", [128, 512], F32, sk)

    import contextlib
    stack = contextlib.ExitStack()

    PERMW = 10200
    TMPW = 42600
    perm = stack.enter_context(nc.sbuf_tensor("perm", [128, PERMW], F32))
    tmp = stack.enter_context(nc.sbuf_tensor("tmp", [128, TMPW], F32))
    A = {}

    def carve(arena, off, name, parts, words, dtype):
        assert off + words <= (PERMW if arena is perm else TMPW), (name, off, words)
        v = arena[0:parts, off:off + words]
        if dtype == BF16:
            v = v.bitcast(BF16)
        A[name] = v
        return off + words

    o = 0
    o = carve(perm, o, "ones1", 1, 128, F32)
    o = carve(perm, o, "onesc_bf", 128, 1, BF16)
    o = carve(perm, o, "onesc_f", 128, 1, F32)
    o = carve(perm, o, "id_bf", 128, 64, BF16)
    o = carve(perm, o, "id_f", 128, 128, F32)
    o = carve(perm, o, "triu_f", 128, 128, F32)
    o = carve(perm, o, "maskA", 128, 512, F32)
    o = carve(perm, o, "maskM", 128, 512, F32)
    o = carve(perm, o, "epsln", 128, 1, F32)
    o = carve(perm, o, "epsl2", 128, 1, F32)
    o = carve(perm, o, "epsrms", 128, 1, F32)
    o = carve(perm, o, "logcc", 128, 1, F32)
    o = carve(perm, o, "alog", 32, 1, F32)
    o = carve(perm, o, "dtb", 32, 1, F32)
    o = carve(perm, o, "negea", 32, 1, F32)
    o = carve(perm, o, "nw", 128, 1, F32)
    o = carve(perm, o, "cw", 128, 64 * KC, F32)
    o = carve(perm, o, "gT", 32, L, F32)
    o = carve(perm, o, "lbT", 32, L, F32)
    o = carve(perm, o, "beT", 32, L, F32)
    o = carve(perm, o, "ssq", 128, 32 * NT, F32)
    o = carve(perm, o, "lnss", 128, 32 * NT, F32)
    o = carve(perm, o, "scl", 1, 8, F32)
    o = carve(perm, o, "srep", 128, 8, F32)
    o = carve(perm, o, "wacc", 128, 64, F32)
    o = carve(perm, o, "wred", 128, 4, F32)
    for nm in ("sumx", "sumq", "mcol", "vcol", "rstd"):
        o = carve(perm, o, nm, 128, 1, F32)
    for nm in ("g_tm", "lb_tm", "be_tm", "Gtil", "Ghat", "gcrep", "e1", "e2",
               "cS", "egc", "csub", "msq", "rr"):
        o = carve(perm, o, nm, 128, 32, F32)
    o = carve(perm, o, "GG", 128, 64, F32)
    o = carve(perm, o, "GGT", 64, 128, F32)
    o = carve(perm, o, "junk", 1, 4, F32)
    assert o <= PERMW, o

    o = 0
    o = carve(tmp, o, "lnd", 128, 16 * L // 2, BF16)
    o = carve(tmp, o, "wz0", 128, 16 * 512 // 2, BF16)
    o = carve(tmp, o, "wz1", 128, 16 * 512 // 2, BF16)
    o = carve(tmp, o, "wq0", 128, 16 * 128 // 2, BF16)
    o = carve(tmp, o, "wq1", 128, 16 * 128 // 2, BF16)
    o = carve(tmp, o, "wabt", 128, 16 * 64 // 2, BF16)
    oz = o
    WSCW = 4096 if L >= 2048 else 512
    carve(tmp, oz, "wsc", 128, WSCW, F32)
    o = carve(tmp, o, "qkv0", 128, (L + KC - 1 + 1) // 2, BF16)
    o = carve(tmp, o, "qkv1", 128, (L + KC - 1 + 1) // 2, BF16)
    o = carve(tmp, o, "cacc", 128, L // 2, BF16)
    o = carve(tmp, o, "mix", 128, L // 2, BF16)
    o = carve(tmp, o, "sqt", 128, L // 2, BF16)
    o = carve(tmp, o, "tstage", 128, 16 * 128 // 2, BF16)
    o = carve(tmp, o, "absb", 64, L, F32)
    o = carve(tmp, o, "spsc", 32, L, F32)
    o = carve(tmp, o, "xa", 128, D, F32)
    o = carve(tmp, o, "lnst", 128, D, F32)
    o = carve(tmp, o, "lntile", 128, D // 2, BF16)
    assert o <= TMPW, o

    o = 0
    o = carve(tmp, o, "S", 128, HV * DV, F32)
    o = carve(tmp, o, "S0b", 128, HV * DV // 2, BF16)
    o = carve(tmp, o, "grw", 128, 32 * 128, F32)
    o = carve(tmp, o, "qb0", 128, HK * 128 // 2, BF16)
    o = carve(tmp, o, "qb1", 128, HK * 128 // 2, BF16)
    o = carve(tmp, o, "kb0", 128, HK * 128 // 2, BF16)
    o = carve(tmp, o, "kb1", 128, HK * 128 // 2, BF16)
    o = carve(tmp, o, "ktb0", 128, HK * 128 // 2, BF16)
    o = carve(tmp, o, "ktb1", 128, HK * 128 // 2, BF16)
    o = carve(tmp, o, "vb0", 128, HV * 128 // 2, BF16)
    o = carve(tmp, o, "vb1", 128, HV * 128 // 2, BF16)
    o = carve(tmp, o, "zb0", 128, HV * 128 // 2, BF16)
    o = carve(tmp, o, "zb1", 128, HV * 128 // 2, BF16)
    for sfx in ("0", "1"):
        for nm in ("Din", "Din2", "t2f", "t3f", "ot4", "o4", "yg", "ygs"):
            o = carve(tmp, o, nm + sfx, 128, 512, F32)
        for nm in ("EA", "EM", "B4", "M4", "BT4", "MT4", "Zb", "Pb", "Yb", "Ub", "Ut", "y4"):
            o = carve(tmp, o, nm + sfx, 128, 256, BF16)
    o = carve(tmp, o, "ystg", 128, HV * 128 // 2, BF16)
    assert o <= TMPW, o

    o = 16 * L // 2
    o = carve(tmp, o, "wo_sb", 128, 32 * 512 // 2, BF16)
    o = carve(tmp, o, "yt0", 128, 32 * 128 // 2, BF16)
    o = carve(tmp, o, "yt1", 128, 32 * 128 // 2, BF16)
    o = carve(tmp, o, "mstage", 128, D, F32)
    o = carve(tmp, o, "x1t", 128, D, F32)
    o = carve(tmp, o, "xa4", 128, D, F32)
    o = carve(tmp, o, "lnst4", 128, D, F32)
    o = carve(tmp, o, "lnt4", 128, D // 2, BF16)
    assert o <= TMPW, o
    o = 16 * L // 2
    o = carve(tmp, o, "wu0", 128, 16 * 128, F32)
    o = carve(tmp, o, "wu1", 128, 16 * 128, F32)
    o = carve(tmp, o, "wg0", 128, 16 * 128, F32)
    o = carve(tmp, o, "wg1", 128, 16 * 128, F32)
    o = carve(tmp, o, "tqu0", 128, 16 * 128 // 2, BF16)
    o = carve(tmp, o, "tqg0", 128, 16 * 128 // 2, BF16)
    o = carve(tmp, o, "tqu1", 128, 16 * 128 // 2, BF16)
    o = carve(tmp, o, "tqg1", 128, 16 * 128 // 2, BF16)
    o = carve(tmp, o, "sgn10", 128, 16 * 128 // 2, BF16)
    o = carve(tmp, o, "sgn20", 128, 16 * 128 // 2, BF16)
    o = carve(tmp, o, "sgn11", 128, 16 * 128 // 2, BF16)
    o = carve(tmp, o, "sgn21", 128, 16 * 128 // 2, BF16)
    o = carve(tmp, o, "hst", 128, L // 2, BF16)
    o = carve(tmp, o, "gsc", 128, 512 // 2, BF16)
    assert o <= TMPW, o

    o = 0
    o = carve(tmp, o, "td", 128, 64 * 512 // 2, BF16)
    o = carve(tmp, o, "hb0", 128, 64 * 128 // 2, BF16)
    o = carve(tmp, o, "hb1", 128, 64 * 128 // 2, BF16)
    o = carve(tmp, o, "wd0", 128, 512, F32)
    o = carve(tmp, o, "wd1", 128, 512, F32)
    o = carve(tmp, o, "fst", 128, D, F32)
    o = carve(tmp, o, "x6t", 128, D, F32)
    o = carve(tmp, o, "xa6", 128, D, F32)
    o = carve(tmp, o, "lnst6", 128, D, F32)
    o = carve(tmp, o, "sg6a", 128, 512 // 2, BF16)
    o = carve(tmp, o, "sg6b", 128, 512 // 2, BF16)
    assert o <= TMPW, o

    P_ = {}
    for i in range(8):
        P_[i] = stack.enter_context(nc.psum_tensor(f"pb{i}", [128, 512], F32))
        A[f"P{i}"] = P_[i]

    sems = {}
    for s in ("pe", "ve", "ac", "po", "so"):
        sems[s] = stack.enter_context(nc.semaphore(s))
    for i in range(64):
        sems[f"dm{i:02d}"] = stack.enter_context(nc.semaphore(f"dm{i:02d}"))

    lnd3 = A["lnd"][:, :].rearrange("p (dt t) -> p dt t", dt=16)

    def program(E, eng):
        p = Prog(E, eng, sems)
        p.junk = A["junk"]
        op = p.op

        def ln_full(xbuf, lnstbuf, outbf, rkey):
            op("v", lambda e: e.tensor_reduce(out=A["sumx"][:], in_=A[xbuf][:], axis=AX.X, op=ALU.add),
               reads=(xbuf,), writes=("sumx",))
            op("v", lambda e: e.scalar_tensor_tensor(out=A[lnstbuf][:], in0=A[xbuf][:], scalar=1.0,
                                                     in1=A[xbuf][:], op0=ALU.mult, op1=ALU.mult,
                                                     accum_out=A["sumq"][:]),
               reads=(xbuf,), writes=(lnstbuf, "sumq"))
            op("v", lambda e: e.tensor_scalar(out=A["mcol"][:], in0=A["sumx"][:], scalar1=1.0 / D,
                                              scalar2=None, op0=ALU.mult),
               reads=("sumx",), writes=("mcol",))
            op("v", lambda e: e.tensor_tensor(out=A["vcol"][:], in0=A["mcol"][:], in1=A["mcol"][:],
                                              op=ALU.mult), reads=("mcol",), writes=("vcol",))
            op("v", lambda e: e.scalar_tensor_tensor(out=A["vcol"][:], in0=A["sumq"][:], scalar=1.0 / D,
                                                     in1=A["vcol"][:], op0=ALU.mult, op1=ALU.subtract),
               reads=("sumq", "vcol"), writes=("vcol",), bump=True)
            op("s", lambda e: e.activation(out=A["rstd"][:], in_=A["vcol"][:], func=AF.Ln,
                                           bias=A["epsln"][:]), reads=("vcol", "epsln"), writes=("rstd",))
            op("s", lambda e: e.activation(out=A["rstd"][:], in_=A["rstd"][:], func=AF.Exp, scale=-0.5),
               reads=("rstd",), writes=("rstd",), bump=True)
            op("v", lambda e: e.scalar_tensor_tensor(
                out=A[outbf][:], in0=A[xbuf][:], scalar=A["mcol"][:],
                in1=A["rstd"][:].broadcast_to([128, D]), op0=ALU.subtract, op1=ALU.mult),
               reads=(xbuf, "mcol", "rstd"), writes=(outbf,))

        def transpose_to_lnd(srcbf, tt, bank0):
            for g4 in range(4):
                bk = f"P{bank0 + g4 % 2}"
                for j in range(4):
                    dtile = g4 * 4 + j
                    op("t", lambda e, bk=bk, j=j, dtile=dtile: e.matmul(
                        A[bk][:, j * 128:(j + 1) * 128], A[srcbf][:, dtile * 128:(dtile + 1) * 128],
                        A["id_bf"][:], start=True, stop=True),
                       reads=(srcbf, "id_bf"), writes=(bk,))
                op("v", lambda e, bk=bk, g4=g4, tt=tt: e.tensor_copy(
                    out=lnd3[:, g4 * 4:(g4 + 1) * 4, tt * 128:(tt + 1) * 128], in_=A[bk][:]),
                   reads=(bk,), writes=("lnd",))

        def finwait():
            if E == "y":
                for s in list(p.dsems.values()):
                    if p.cnt[s] > 0:
                        eng.wait_ge(sems[s], p.cnt[s])

        op("v", lambda e: e.memset(A["junk"][:], 0.0), writes=("junkb",))
        op("v", lambda e: e.memset(A["ones1"][:], 1.0), writes=("ones1",))
        op("v", lambda e: e.memset(A["onesc_bf"][:], 1.0), writes=("onesc_bf",))
        op("v", lambda e: e.memset(A["onesc_f"][:], 1.0), writes=("onesc_f",))
        op("v", lambda e: e.memset(A["id_bf"][:], 1.0), writes=("id_bf",))
        op("v", lambda e: e.memset(A["id_f"][:], 1.0), writes=("id_f",))
        op("v", lambda e: e.memset(A["triu_f"][:], 1.0), writes=("triu_f",))
        op("v", lambda e: e.memset(A["maskA"][:], 0.0), writes=("maskA",))
        op("v", lambda e: e.memset(A["maskM"][:], 0.0), writes=("maskM",))
        op("v", lambda e: e.memset(A["epsln"][:], EPS_LN), writes=("epsln",))
        op("v", lambda e: e.memset(A["epsl2"][:], EPS_L2), writes=("epsl2",))
        op("v", lambda e: e.memset(A["epsrms"][:], EPS_RMS), writes=("epsrms",))
        op("v", lambda e: e.memset(A["logcc"][:], LOGC), writes=("logcc",))
        op("y", lambda e: e.dma_start(out=A["cw"][:], in_=dram["cw"]), dma=True, writes=("cw",))
        op("y", lambda e: e.dma_start(out=A["alog"][:], in_=dram["alog"]), dma=True, writes=("alog",))
        op("y", lambda e: e.dma_start(out=A["dtb"][:], in_=dram["dtb"]), dma=True, writes=("dtb",))
        op("y", lambda e: e.dma_start(out=A["nw"][:], in_=dram["nw"]), dma=True, writes=("nw",))
        op("g", lambda e: e.affine_select(out=A["id_bf"][:], in_=A["id_bf"][:], pattern=[[1, 128]],
                                          compare_op=ALU.is_equal, fill=0.0, channel_multiplier=-1),
           reads=("id_bf",), writes=("id_bf",))
        op("g", lambda e: e.affine_select(out=A["id_f"][:], in_=A["id_f"][:], pattern=[[1, 128]],
                                          compare_op=ALU.is_equal, fill=0.0, channel_multiplier=-1),
           reads=("id_f",), writes=("id_f",))
        op("g", lambda e: e.affine_select(out=A["triu_f"][:], in_=A["triu_f"][:], pattern=[[1, 128]],
                                          compare_op=ALU.is_ge, fill=0.0, channel_multiplier=-1),
           reads=("triu_f",), writes=("triu_f",))
        op("g", lambda e: e.affine_select(out=A["maskA"][:].rearrange("p (h s) -> p h s", h=4),
                                          in_=A["maskA"][:].rearrange("p (h s) -> p h s", h=4),
                                          pattern=[[0, 4], [-1, 128]], compare_op=ALU.is_ge,
                                          fill=NEG, channel_multiplier=1, base=-1),
           reads=("maskA",), writes=("maskA",))
        op("g", lambda e: e.affine_select(out=A["maskM"][:].rearrange("p (h s) -> p h s", h=4),
                                          in_=A["maskM"][:].rearrange("p (h s) -> p h s", h=4),
                                          pattern=[[0, 4], [-1, 128]], compare_op=ALU.is_ge,
                                          fill=NEG, channel_multiplier=1, base=0),
           reads=("maskM",), writes=("maskM",))
        op("s", lambda e: e.activation(out=A["negea"][:], in_=A["alog"][:], func=AF.Exp),
           reads=("alog",), writes=("negea",), bump=True)

        p.barrier()

        if maxphase < 1:
            finwait()
            return
        for tt in range(NT):
            op("y", lambda e, tt=tt: e.dma_start(out=A["xa"][:], in_=dram["x"][tt * 128:(tt + 1) * 128, :]),
               dma=True, writes=("xa",))
            ln_full("xa", "lnst", "lntile", None)
            if dbg and tt == 0:
                op("y", lambda e: e.dma_start(out=dram["dbg8"][:], in_=A["xa"][:, 0:512]),
                   dma=True, reads=("xa",), writes=("ddbg8",))
                op("v", lambda e: e.tensor_copy(out=A["lnst"][:, 0:512],
                                                in_=A["lntile"][:, 0:512]),
                   reads=("lntile",), writes=("lnst",))
                op("v", lambda e: e.tensor_copy(out=A["lnst"][:, 512:513], in_=A["mcol"][:]),
                   reads=("mcol",), writes=("lnst",))
                op("v", lambda e: e.tensor_copy(out=A["lnst"][:, 513:514], in_=A["rstd"][:]),
                   reads=("rstd",), writes=("lnst",))
                op("v", lambda e: e.tensor_copy(out=A["lnst"][:, 514:515], in_=A["sumq"][:]),
                   reads=("sumq",), writes=("lnst",))
                op("v", lambda e: e.tensor_copy(out=A["lnst"][:, 515:516], in_=A["vcol"][:]),
                   reads=("vcol",), writes=("lnst",))
                op("y", lambda e: e.dma_start(out=dram["dbg9"][:], in_=A["lnst"][:, 0:512]),
                   dma=True, reads=("lnst",), writes=("ddbg9",))
                op("y", lambda e: e.dma_start(out=dram["dbg9"][:, 0:4].rearrange("p f -> p f"),
                                              in_=A["lnst"][:, 512:516]),
                   dma=True, reads=("lnst",), writes=("ddbg9",))
            transpose_to_lnd("lntile", tt, 0)

        if maxphase < 2:
            finwait()
            return
        op("g", lambda e: e.dma_start(out=A["wabt"][:].rearrange("p (t c) -> p t c", t=16),
                                      in_=dram["wab"].rearrange("(t p) c -> p t c", p=128)),
           dma=True, writes=("wabt",))
        for tp in range(L // 512):
            bk = f"P{2 + tp % 2}"
            for dtile in range(16):
                op("t", lambda e, bk=bk, tp=tp, dtile=dtile: e.matmul(
                    A[bk][0:64, :], A["wabt"][:, dtile * 64:(dtile + 1) * 64],
                    lnd3[:, dtile, tp * 512:(tp + 1) * 512],
                    start=(dtile == 0), stop=(dtile == 15)),
                   reads=("wabt", "lnd"), writes=(bk,))
            op("v", lambda e, bk=bk, tp=tp: e.tensor_copy(out=A["absb"][:, tp * 512:(tp + 1) * 512],
                                                          in_=A[bk][0:64, :]),
               reads=(bk,), writes=("absb",))
        op("v", lambda e: e.tensor_scalar(out=A["spsc"][:], in0=A["absb"][0:32, :], scalar1=A["dtb"][:],
                                          scalar2=None, op0=ALU.add), reads=("absb", "dtb"), writes=("spsc",))
        for (dst, src_is_neg) in (("gT", False), ("lbT", True)):
            if src_is_neg:
                op("v", lambda e: e.tensor_scalar(out=A["spsc"][:], in0=A["absb"][32:64, :], scalar1=-1.0,
                                                  scalar2=None, op0=ALU.mult), reads=("absb",), writes=("spsc",))
            op("s", lambda e, dst=dst: e.activation(out=A[dst][:], in_=A["spsc"][:], func=AF.Abs),
               reads=("spsc",), writes=(dst,))
            op("s", lambda e, dst=dst: e.activation(out=A[dst][:], in_=A[dst][:], func=AF.Exp, scale=-1.0),
               reads=(dst,), writes=(dst,))
            op("s", lambda e, dst=dst: e.activation(out=A[dst][:], in_=A[dst][:], func=AF.Ln, bias=1.0),
               reads=(dst,), writes=(dst,))
            op("v", lambda e: e.tensor_scalar(out=A["spsc"][:], in0=A["spsc"][:], scalar1=0.0, scalar2=None,
                                              op0=ALU.max), reads=("spsc",), writes=("spsc",))
            op("v", lambda e, dst=dst: e.tensor_tensor(out=A[dst][:], in0=A["spsc"][:], in1=A[dst][:],
                                                       op=ALU.add), reads=("spsc", dst), writes=(dst,))
        op("v", lambda e: e.tensor_scalar(out=A["gT"][:], in0=A["gT"][:], scalar1=A["negea"][:],
                                          scalar2=-1.0, op0=ALU.mult, op1=ALU.mult),
           reads=("gT", "negea"), writes=("gT",))
        op("v", lambda e: e.tensor_scalar(out=A["lbT"][:], in0=A["lbT"][:], scalar1=-1.0, scalar2=None,
                                          op0=ALU.mult), reads=("lbT",), writes=("lbT",))
        op("s", lambda e: e.activation(out=A["beT"][:], in_=A["lbT"][:], func=AF.Exp),
           reads=("lbT",), writes=("beT",))

        if maxphase < 3:
            finwait()
            return
        op("v", lambda e: e.memset(A["wacc"][:, 0:4], 0.0), writes=("wacc",))

        WCOL = WSCW // 16
        WC2 = WSCW // 64
        NPU = ECH // WCOL
        NPD = D // WC2
        NBLK = 2 * NPU + NPD
        NAB = max(1, WSCW // 2048)

        def scale_block(bi):
            if bi < 2 * NPU:
                wi, cb2 = bi // NPU, bi % NPU
                wn = ("wupT", "wgT")[wi]
                s_ = dram[wn][:, cb2 * WCOL:(cb2 + 1) * WCOL].rearrange("(t p) c -> p t c", p=128)
                tdim = 16
            else:
                wi, cb2 = 2, bi - 2 * NPU
                s_ = dram["wdT"][:, cb2 * WC2:(cb2 + 1) * WC2].rearrange("(t p) c -> p t c", p=128)
                tdim = 64
            op("y", lambda e, s_=s_, tdim=tdim: e.dma_start(
                out=A["wsc"][:].rearrange("p (t c) -> p t c", t=tdim),
                in_=s_), dma=True, writes=("wsc",))
            for kq_ in range(NAB):
                w_ = WSCW // NAB
                op("s", lambda e, kq_=kq_, w_=w_: e.activation(
                    out=A["lnst"][:, 0:w_], in_=A["wsc"][:, kq_ * w_:(kq_ + 1) * w_], func=AF.Abs,
                    accum_out=A["wred"][:, kq_:kq_ + 1]),
                   reads=("wsc",), writes=("lnst", "wred"))
            op("v", lambda e: e.tensor_reduce(
                out=A["wred"][:, 3:4], in_=A["wred"][:, 0:NAB], axis=AX.X, op=ALU.add),
               reads=("wred",), writes=("wred",))
            op("v", lambda e, wi=wi: e.tensor_tensor(
                out=A["wacc"][:, wi:wi + 1], in0=A["wacc"][:, wi:wi + 1], in1=A["wred"][:, 3:4],
                op=ALU.add), reads=("wred", "wacc"), writes=("wacc",), bump=True)

        for cb in range(HV * DV // 512):
            nper_cb = (NBLK + 7) // 8
            for sbi in range(cb * nper_cb, min(NBLK, (cb + 1) * nper_cb)):
                scale_block(sbi)
            wzb = f"wz{cb % 2}"
            op("g", lambda e, wzb=wzb, cb=cb: e.dma_start(
                out=A[wzb][:].rearrange("p (t c) -> p t c", t=16),
                in_=dram["wz"][:, cb * 512:(cb + 1) * 512].rearrange("(t p) c -> p t c", p=128)),
               dma=True, writes=(wzb,))
            for tt in range(NT):
                bk = f"P{4 + tt % 2}"
                for dtile in range(16):
                    op("t", lambda e, bk=bk, wzb=wzb, dtile=dtile, tt=tt: e.matmul(
                        A[bk][:], lnd3[:, dtile, tt * 128:(tt + 1) * 128],
                        A[wzb][:, dtile * 512:(dtile + 1) * 512],
                        start=(dtile == 0), stop=(dtile == 15)),
                       reads=(wzb, "lnd"), writes=(bk,))
                slot = f"lt{tt % 2}"
                op("s", lambda e, bk=bk, tt=tt: e.activation(
                    out=A["lntile"][:, (tt % 2) * 512:(tt % 2 + 1) * 512], in_=A[bk][:], func=AF.Silu),
                   reads=(bk,), writes=(slot,))
                op("y", lambda e, tt=tt, cb=cb: e.dma_start(
                    out=dram["sz_sp"][tt * 128:(tt + 1) * 128, cb * 512:(cb + 1) * 512],
                    in_=A["lntile"][:, (tt % 2) * 512:(tt % 2 + 1) * 512]),
                   dma=True, reads=(slot,))

        if maxphase < 4:
            finwait()
            return
        op("v", lambda e: e.tensor_copy(out=A["junk"][0:1, 2:4], in_=A["junk"][0:1, 0:2]),
           reads=("wsc",), writes=("wsc",))
        op("s", lambda e: e.copy(out=A["junk"][0:1, 2:4], in_=A["junk"][0:1, 0:2]),
           reads=("wsc",), writes=("wsc",))
        op("v", lambda e: e.memset(A["qkv0"][:, 0:KC - 1], 0.0), writes=())
        op("v", lambda e: e.memset(A["qkv1"][:, 0:KC - 1], 0.0), writes=())
        for ct in range(64):
            wqb = f"wq{ct % 2}"
            qkb = f"qkv{ct % 2}"
            op("g", lambda e, wqb=wqb, ct=ct: e.dma_start(
                out=A[wqb][:].rearrange("p (t c) -> p t c", t=16),
                in_=dram["wqkv"][:, ct * 128:(ct + 1) * 128].rearrange("(t p) c -> p t c", p=128)),
               dma=True, writes=(wqb,))
            for tp in range(L // 512):
                bk = f"P{tp % 2}"
                for dtile in range(16):
                    op("t", lambda e, bk=bk, wqb=wqb, dtile=dtile, tp=tp: e.matmul(
                        A[bk][:], A[wqb][:, dtile * 128:(dtile + 1) * 128],
                        lnd3[:, dtile, tp * 512:(tp + 1) * 512],
                        start=(dtile == 0), stop=(dtile == 15)),
                       reads=(wqb, "lnd"), writes=(bk,))
                op("v", lambda e, bk=bk, qkb=qkb, tp=tp: e.tensor_copy(
                    out=A[qkb][:, KC - 1 + tp * 512: KC - 1 + (tp + 1) * 512], in_=A[bk][:]),
                   reads=(bk,), writes=(qkb,))
            op("v", lambda e, qkb=qkb, ct=ct: e.tensor_scalar(
                out=A["cacc"][:], in0=A[qkb][:, 0:L], scalar1=A["cw"][:, ct * KC:ct * KC + 1],
                scalar2=None, op0=ALU.mult), reads=(qkb, "cw"), writes=("cacc",))
            for i in range(1, KC):
                op("v", lambda e, qkb=qkb, ct=ct, i=i: e.scalar_tensor_tensor(
                    out=A["cacc"][:], in0=A[qkb][:, i:i + L], scalar=A["cw"][:, ct * KC + i:ct * KC + i + 1],
                    in1=A["cacc"][:], op0=ALU.mult, op1=ALU.add),
                   reads=(qkb, "cw", "cacc"), writes=("cacc",))
            op("s", lambda e: e.activation(out=A["mix"][:], in_=A["cacc"][:], func=AF.Silu),
               reads=("cacc",), writes=("mix",))
            if ct < 32:
                dst = dram["q_sp"] if ct < 16 else dram["k_sp"]
                row = (ct % 16) * 128
                op("y", lambda e, dst=dst, row=row: e.dma_start(out=dst[row:row + 128, :], in_=A["mix"][:]),
                   dma=True, reads=("mix",))
                op("s", lambda e: e.activation(out=A["sqt"][:], in_=A["mix"][:], func=AF.Square),
                   reads=("mix",), writes=("sqt",))
                for j in range(NT):
                    op("t", lambda e, j=j: e.matmul(
                        A["P6"][:, j:j + 1], A["sqt"][:, j * 128:(j + 1) * 128], A["onesc_bf"][:, 0:1],
                        start=True, stop=True), reads=("sqt", "onesc_bf"), writes=("P6",) if j == 0 else ())
                op("t", lambda e: e.nop(), writes=("P6",))
                op("v", lambda e, ct=ct: e.tensor_copy(out=A["ssq"][:, ct * NT:(ct + 1) * NT],
                                                       in_=A["P6"][:, 0:NT]),
                   reads=("P6",), writes=("ssq",), bump=True)
            if ct >= 16:
                for g4 in range(NT // 4):
                    for j in range(4):
                        ts_ = g4 * 4 + j
                        op("t", lambda e, j=j, ts_=ts_: e.matmul(
                            A["P7"][:, j * 128:(j + 1) * 128], A["mix"][:, ts_ * 128:(ts_ + 1) * 128],
                            A["id_bf"][:], start=True, stop=True),
                           reads=("mix", "id_bf"), writes=("P7",))
                    op("v", lambda e, g4=g4: e.tensor_copy(
                        out=A["tstage"][:, g4 * 512:(g4 + 1) * 512], in_=A["P7"][:]),
                       reads=("P7",), writes=("tstage",))
                if ct < 32:
                    op("y", lambda e, ct=ct: e.dma_start(
                        out=dram["kt_sp"][:, (ct - 16) * 128:(ct - 15) * 128].rearrange(
                            "(j p) w -> p j w", p=128),
                        in_=A["tstage"][:, 0:NT * 128].rearrange("p (j w) -> p j w", w=128)),
                       dma=True, reads=("tstage",))
                else:
                    op("y", lambda e, ct=ct: e.dma_start(
                        out=dram["v_sp"][:, (ct - 32) * 128:(ct - 31) * 128].rearrange(
                            "(j p) w -> p j w", p=128),
                        in_=A["tstage"][:, 0:NT * 128].rearrange("p (j w) -> p j w", w=128)),
                       dma=True, reads=("tstage",))

        if maxphase < 5:
            finwait()
            return
        p.barrier()
        nel = float(D * ECH)
        op("t", lambda e: e.matmul(A["P0"][0:1, 0:4], A["onesc_f"][:], A["wacc"][:, 0:4],
                                   start=True, stop=True), reads=("wacc", "onesc_f"), writes=("P0",))
        op("t", lambda e: e.matmul(A["P0"][0:1, 508:509], A["onesc_bf"][0:1, 0:1],
                           A["onesc_bf"][0:1, 0:1], start=True, stop=True),
           reads=("onesc_bf",), writes=("P0",))
        op("v", lambda e: e.tensor_scalar(out=A["scl"][0:1, 0:3], in0=A["P0"][0:1, 0:3],
                                          scalar1=1.0 / nel, scalar2=EPS_TER, op0=ALU.mult, op1=ALU.add),
           reads=("P0",), writes=("scl",), bump=True)
        op("v", lambda e: e.tensor_scalar(out=A["scl"][0:1, 3:6], in0=A["scl"][0:1, 0:3],
                                          scalar1=0.5, scalar2=None, op0=ALU.mult),
           reads=("scl",), writes=("scl",), bump=True)
        op("y", lambda e: e.dma_start(out=dram["scl_sp"][0:1, 0:6], in_=A["scl"][0:1, 0:6]),
           dma=True, reads=("scl",), writes=("dscl",))
        op("y", lambda e: e.dma_start(
            out=A["srep"][:, 0:3],
            in_=dram["scl_sp"][0:1, 3:6].partition_broadcast(128).rearrange("p o f -> p (o f)")),
           dma=True, reads=("dscl",), writes=("srep",))
        op("y", lambda e: e.dma_start(
            out=A["srep"][:, 3:6],
            in_=dram["scl_sp"][0:1, 3:6].partition_broadcast(128).rearrange("p o f -> p (o f)")),
           dma=True, reads=("dscl",), writes=("srep",))
        p.barrier()

        if maxphase < 6:
            finwait()
            return
        op("s", lambda e: e.activation(out=A["lnss"][:], in_=A["ssq"][:], func=AF.Ln, bias=A["epsl2"][:]),
           reads=("ssq", "epsl2"), writes=("lnss",), bump=True)
        op("v", lambda e: e.memset(A["S"][:], 0.0), writes=("S",))
        lnss3 = A["lnss"][:, :].rearrange("p (h t) -> p h t", h=32)
        GGe = A["GG"][:, 0:32].rearrange("p (h d) -> p h d", d=2)
        Gve = A["GG"][:, 32:64].rearrange("p (h d) -> p h d", d=2)

        def chunk_loads(c):
            b = c % 2
            for (nm, src, hh) in ((f"qb{b}", dram["q_sp"][:, c * 128:(c + 1) * 128].rearrange("(h p) t -> p h t", p=128), 16),
                              (f"kb{b}", dram["k_sp"][:, c * 128:(c + 1) * 128].rearrange("(h p) t -> p h t", p=128), 16),
                              (f"ktb{b}", dram["kt_sp"][c * 128:(c + 1) * 128, :], 0),
                              (f"vb{b}", dram["v_sp"][c * 128:(c + 1) * 128, :], 0),
                              (f"zb{b}", dram["sz_sp"][c * 128:(c + 1) * 128, :], 0)):
                o_ = A[nm][:] if hh == 0 else A[nm][:].rearrange("p (h t) -> p h t", h=hh)
                op("y", lambda e, o_=o_, src=src: e.dma_start(out=o_, in_=src),
                   dma=True, writes=(nm,))

        chunk_loads(0)
        for c in range(min(NCH, dlim[0])):
            b = c % 2
            qb, kb, ktb, vb, zb = (f"qb{b}", f"kb{b}", f"ktb{b}", f"vb{b}", f"zb{b}")
            if dlim[1] == 0 and dlim[2] < 1:
                continue
            for (i, srcn, dstn) in ((0, "gT", "g_tm"), (1, "lbT", "lb_tm"), (2, "beT", "be_tm")):
                op("t", lambda e, i=i, srcn=srcn, c=c: e.matmul(
                    A["P0"][:, i * 32:(i + 1) * 32], A[srcn][:, c * 128:(c + 1) * 128],
                    A["id_f"][0:32, 0:32], start=True, stop=True),
                   reads=(srcn, "id_f"), writes=("P0",) if i == 0 else ())
            op("t", lambda e: e.matmul(A["P0"][0:1, 508:509], A["onesc_bf"][0:1, 0:1],
                                       A["onesc_bf"][0:1, 0:1], start=True, stop=True),
               reads=("onesc_bf",), writes=("P0",))
            for (i, dstn) in ((0, "g_tm"), (1, "lb_tm"), (2, "be_tm")):
                op("v", lambda e, i=i, dstn=dstn: e.tensor_copy(out=A[dstn][:], in_=A["P0"][:, i * 32:(i + 1) * 32]),
                   reads=("P0",), writes=(dstn,), bump=True)
            if dlim[1] == 0 and dlim[2] < 2:
                continue
            op("t", lambda e: e.matmul(A["P1"][:, 0:32], A["triu_f"][:], A["g_tm"][:], start=True, stop=True),
               reads=("triu_f", "g_tm"), writes=("P1",))
            op("t", lambda e: e.matmul(A["P1"][0:1, 508:509], A["onesc_bf"][0:1, 0:1],
                                       A["onesc_bf"][0:1, 0:1], start=True, stop=True),
               reads=("onesc_bf",), writes=("P1",))
            op("v", lambda e: e.tensor_copy(out=A["GG"][:, 32:64], in_=A["P1"][:, 0:32]),
               reads=("P1",), writes=("GG",), bump=True)
            op("v", lambda e, c=c: e.scalar_tensor_tensor(
                out=GGe, in0=lnss3[:, 16:32, c:c + 1].broadcast_to([128, 16, 2]), scalar=0.5,
                in1=Gve, op0=ALU.mult, op1=ALU.add), reads=("lnss", "GG"), writes=("GG",), bump=True)
            op("v", lambda e, c=c: e.scalar_tensor_tensor(
                out=A["Gtil"][:].rearrange("p (h d) -> p h d", d=2),
                in0=lnss3[:, 16:32, c:c + 1].broadcast_to([128, 16, 2]), scalar=-0.5,
                in1=Gve, op0=ALU.mult, op1=ALU.add), reads=("lnss", "GG"), writes=("Gtil",))
            op("v", lambda e: e.tensor_tensor(out=A["Gtil"][:], in0=A["Gtil"][:], in1=A["lb_tm"][:],
                                              op=ALU.add), reads=("Gtil", "lb_tm"), writes=("Gtil",),
               bump=True)
            op("v", lambda e, c=c: e.scalar_tensor_tensor(
                out=A["Ghat"][:].rearrange("p (h d) -> p h d", d=2),
                in0=lnss3[:, 0:16, c:c + 1].broadcast_to([128, 16, 2]), scalar=-0.5,
                in1=Gve, op0=ALU.mult, op1=ALU.add), reads=("lnss", "GG"), writes=("Ghat",), bump=True)
            if dlim[1] == 0 and dlim[2] < 3:
                continue
            op("t", lambda e: e.transpose(A["P2"][0:64, 0:128], A["GG"][:], A["id_f"][:]),
               reads=("GG", "id_f"), writes=("P2",))
            op("t", lambda e: e.matmul(A["P2"][0:1, 508:509], A["onesc_bf"][0:1, 0:1],
                                       A["onesc_bf"][0:1, 0:1], start=True, stop=True),
               reads=("onesc_bf",), writes=("P2",))
            if dlim[1] == 0 and dlim[2] < 30:
                continue
            op("v", lambda e: e.tensor_copy(out=A["GGT"][:], in_=A["P2"][0:64, 0:128]),
               reads=("P2",), writes=("GGT",), bump=True)
            if dlim[1] == 0 and dlim[2] < 31:
                continue
            op("y", lambda e, c=c: e.dma_start(out=dram["gg_sp"][c % 2:c % 2 + 1].rearrange("b p f -> (b p) f"),
                                                  in_=A["GGT"][:]),
               dma=True, reads=("GGT",), writes=(f"gg{c % 2}",))
            if dlim[1] == 0 and dlim[2] < 32:
                continue
            op("y", lambda e, c=c: e.dma_start(
                out=dram["gc_sp"][c % 2:c % 2 + 1].rearrange("b (f o) -> (b f) o", o=1),
                in_=A["GGT"][32:64, 127:128]), dma=True, reads=("GGT",), writes=(f"ggc{c % 2}",))
            if dlim[1] == 0 and dlim[2] < 4:
                continue
            op("y", lambda e, c=c: e.dma_start(
                out=A["grw"][:],
                in_=dram["gg_sp"][c % 2:c % 2 + 1].rearrange("b p f -> b (p f)")[0:1, 0:4096]
                    .partition_broadcast(128).rearrange("p o f -> p (o f)")),
               dma=True, reads=(f"gg{c % 2}",), writes=("grw",))
            op("y", lambda e, c=c: e.dma_start(
                out=A["gcrep"][:],
                in_=dram["gc_sp"][c % 2:c % 2 + 1].partition_broadcast(128).rearrange("p o f -> p (o f)")),
               dma=True, reads=(f"ggc{c % 2}",), writes=("gcrep",))
            if dlim[1] == 0 and dlim[2] < 5:
                continue
            op("v", lambda e: e.tensor_sub(out=A["csub"][:], in0=A["gcrep"][:], in1=A["GG"][:, 0:32]),
               reads=("gcrep", "GG"), writes=("csub",), bump=True)
            op("s", lambda e: e.activation(out=A["e1"][:], in_=A["Gtil"][:], func=AF.Exp),
               reads=("Gtil",), writes=("e1",), bump=True)
            op("s", lambda e: e.activation(out=A["e2"][:], in_=A["Ghat"][:], func=AF.Exp,
                                           bias=A["logcc"][:]),
               reads=("Ghat", "logcc"), writes=("e2",), bump=True)
            op("s", lambda e: e.activation(out=A["cS"][:], in_=A["csub"][:], func=AF.Exp),
               reads=("csub",), writes=("cS",), bump=True)
            op("s", lambda e: e.activation(out=A["egc"][:], in_=A["gcrep"][:], func=AF.Exp),
               reads=("gcrep",), writes=("egc",), bump=True)
            op("v", lambda e: e.tensor_copy(out=A["S0b"][:], in_=A["S"][:]), reads=("S",), writes=("S0b",))
            if dbg and c == 0:
                op("y", lambda e: e.dma_start(out=dram["dbg0"][:, 0:64], in_=A["GG"][:]),
                   dma=True, reads=("GG",), writes=("ddbg0",))
                op("y", lambda e: e.dma_start(out=dram["dbg1"][:], in_=A["grw"][:, 0:512]),
                   dma=True, reads=("grw",), writes=("ddbg1",))
                op("v", lambda e: e.tensor_copy(out=A["GG"][0:32, 0:1], in_=A["negea"][:]),
                   reads=("negea",), writes=("GGdbg",), bump=True)
                op("y", lambda e: e.dma_start(out=dram["dbg0"][0:32, 384:416],
                                              in_=A["gT"][:, 0:32]),
                   dma=True, reads=("gT",), writes=("ddbg0",))
                op("y", lambda e: e.dma_start(out=dram["dbg0"][0:32, 416:417],
                                              in_=A["GG"][0:32, 0:1],
                                              allow_slow_non_contiguous=True),
                   dma=True, reads=("GGdbg",), writes=("ddbg0",))
                for nm_, off_ in (("Gtil", 64), ("Ghat", 96), ("e1", 128), ("e2", 160),
                                  ("cS", 192), ("egc", 224), ("gcrep", 256), ("g_tm", 288),
                                  ("lb_tm", 320), ("be_tm", 352)):
                    op("y", lambda e, nm_=nm_, off_=off_: e.dma_start(
                        out=dram["dbg0"][:, off_:off_ + 32], in_=A[nm_][:]),
                       dma=True, reads=(nm_,), writes=("ddbg0",))
            if c + 1 < NCH:
                chunk_loads(c + 1)
            NGRP = min(HV // 4, dlim[1])
            bb4 = lambda t_: t_.broadcast_to([128, 4, 128])

            def gctx(gi):
                pi = gi % 2
                return (str(pi), gi * 4, (gi * 4) // 2,
                        f"P{4 * pi + 0}", f"P{4 * pi + 1}", f"P{4 * pi + 2}", f"P{4 * pi + 3}")

            def F_stage(gi, part):
                sfx, v0, kq0, b0, b1, b2, b3 = gctx(gi)
                h3 = lambda nm: A[nm + sfx][:].rearrange("p (h s) -> p h s", h=4)
                if part == 0:
                    for kk in range(2):
                        op("t", lambda e, kk=kk: e.matmul(
                            A[b0][:, kk * 128:(kk + 1) * 128],
                            A[kb][:, (kq0 + kk) * 128:(kq0 + kk + 1) * 128],
                            A[kb][:, (kq0 + kk) * 128:(kq0 + kk + 1) * 128], start=True, stop=True),
                           reads=(kb,), writes=(b0,) if kk == 0 else ())
                        op("t", lambda e, kk=kk: e.matmul(
                            A[b0][:, 256 + kk * 128:256 + (kk + 1) * 128],
                            A[qb][:, (kq0 + kk) * 128:(kq0 + kk + 1) * 128],
                            A[kb][:, (kq0 + kk) * 128:(kq0 + kk + 1) * 128], start=True, stop=True),
                           reads=(qb, kb))
                    op("t", lambda e: e.nop(), writes=(b0,))
                elif part == 1:
                    op("v", lambda e: e.scalar_tensor_tensor(
                        out=h3("Din"), in0=A["grw"][:, v0 * 128:(v0 + 4) * 128].rearrange(
                            "p (h s) -> p h s", h=4),
                        scalar=-1.0, in1=bb4(A["Gtil"][:, v0:v0 + 4]), op0=ALU.mult, op1=ALU.add),
                       reads=("grw", "Gtil"), writes=("Din" + sfx,))
                    op("v", lambda e: e.tensor_tensor(out=A["Din" + sfx][:], in0=A["Din" + sfx][:],
                                                      in1=A["maskA"][:], op=ALU.add),
                       reads=("Din" + sfx, "maskA"), writes=("Din" + sfx,))
                    op("v", lambda e: e.scalar_tensor_tensor(
                        out=h3("Din2"), in0=A["grw"][:, v0 * 128:(v0 + 4) * 128].rearrange(
                            "p (h s) -> p h s", h=4),
                        scalar=-1.0, in1=bb4(A["Ghat"][:, v0:v0 + 4]), op0=ALU.mult, op1=ALU.add),
                       reads=("grw", "Ghat"), writes=("Din2" + sfx,))
                    op("v", lambda e: e.tensor_tensor(out=A["Din2" + sfx][:], in0=A["Din2" + sfx][:],
                                                      in1=A["maskM"][:], op=ALU.add),
                       reads=("Din2" + sfx, "maskM"), writes=("Din2" + sfx,))
                    op("s", lambda e: e.activation(out=A["EA" + sfx][:], in_=A["Din" + sfx][:], func=AF.Exp),
                       reads=("Din" + sfx,), writes=("EA" + sfx,))
                    op("s", lambda e: e.activation(out=A["EM" + sfx][:], in_=A["Din2" + sfx][:], func=AF.Exp,
                                                   bias=A["logcc"][:]),
                       reads=("Din2" + sfx, "logcc"), writes=("EM" + sfx,))
                    for kk in range(2):
                        kkx = A[b0][:, kk * 128:(kk + 1) * 128].rearrange(
                            "p (o s) -> p o s", o=1).broadcast_to([128, 2, 128])
                        qkx = A[b0][:, 256 + kk * 128:256 + (kk + 1) * 128].rearrange(
                            "p (o s) -> p o s", o=1).broadcast_to([128, 2, 128])
                        op("v", lambda e, kkx=kkx, kk=kk: e.scalar_tensor_tensor(
                            out=A["B4" + sfx][:, kk * 256:(kk + 1) * 256].rearrange("p (d s) -> p d s", d=2),
                            in0=kkx, scalar=-1.0,
                            in1=A["EA" + sfx][:, kk * 256:(kk + 1) * 256].rearrange("p (d s) -> p d s", d=2),
                            op0=ALU.mult, op1=ALU.mult),
                           reads=(b0, "EA" + sfx), writes=("B4" + sfx,))
                        op("v", lambda e, qkx=qkx, kk=kk: e.scalar_tensor_tensor(
                            out=A["M4" + sfx][:, kk * 256:(kk + 1) * 256].rearrange("p (d s) -> p d s", d=2),
                            in0=qkx, scalar=1.0,
                            in1=A["EM" + sfx][:, kk * 256:(kk + 1) * 256].rearrange("p (d s) -> p d s", d=2),
                            op0=ALU.mult, op1=ALU.mult),
                           reads=(b0, "EM" + sfx), writes=("M4" + sfx,))
                elif part == 2:
                    for j in range(4):
                        op("t", lambda e, j=j: e.matmul(A[b1][:, j * 128:(j + 1) * 128],
                                                        A["B4" + sfx][:, j * 128:(j + 1) * 128],
                                                        A["id_bf"][:], start=True, stop=True),
                           reads=("B4" + sfx, "id_bf"), writes=(b1,) if j == 0 else ())
                    for j in range(4):
                        op("t", lambda e, j=j: e.matmul(A[b2][:, j * 128:(j + 1) * 128],
                                                        A["M4" + sfx][:, j * 128:(j + 1) * 128],
                                                        A["id_bf"][:], start=True, stop=True),
                           reads=("M4" + sfx, "id_bf"), writes=(b2,) if j == 0 else ())
                    for j in range(4):
                        op("t", lambda e, j=j: e.matmul(
                            A[b3][:, j * 128:(j + 1) * 128],
                            A[kb][:, (kq0 + j // 2) * 128:(kq0 + j // 2 + 1) * 128],
                            A["S0b"][:, (v0 + j) * 128:(v0 + j + 1) * 128], start=True, stop=True),
                           reads=(kb, "S0b"), writes=(b3,) if j == 0 else ())
                    for j in range(4):
                        op("t", lambda e, j=j: e.matmul(
                            A[b0][:, j * 128:(j + 1) * 128],
                            A[qb][:, (kq0 + j // 2) * 128:(kq0 + j // 2 + 1) * 128],
                            A["S0b"][:, (v0 + j) * 128:(v0 + j + 1) * 128], start=True, stop=True),
                           reads=(qb, "S0b"), writes=(b0,) if j == 0 else ())
                    op("t", lambda e: e.nop(), writes=(b0, b1, b2, b3))
                elif part == 3:
                    op("v", lambda e: e.tensor_copy(out=A["BT4" + sfx][:], in_=A[b1][:]),
                       reads=(b1,), writes=("BT4" + sfx,))
                    op("v", lambda e: e.tensor_copy(out=A["MT4" + sfx][:], in_=A[b2][:]),
                       reads=(b2,), writes=("MT4" + sfx,))
                    op("v", lambda e: e.scalar_tensor_tensor(
                        out=h3("ot4"), in0=A[b0][:].rearrange("p (h s) -> p h s", h=4), scalar=1.0,
                        in1=bb4(A["e2"][:, v0:v0 + 4]), op0=ALU.mult, op1=ALU.mult),
                       reads=(b0, "e2"), writes=("ot4" + sfx,))
                    op("v", lambda e: e.scalar_tensor_tensor(
                        out=h3("t2f"), in0=A[b3][:].rearrange("p (h s) -> p h s", h=4), scalar=1.0,
                        in1=bb4(A["e1"][:, v0:v0 + 4]), op0=ALU.mult, op1=ALU.mult),
                       reads=(b3, "e1"), writes=("t2f" + sfx,))
                    op("v", lambda e: e.scalar_tensor_tensor(
                        out=h3("t3f"), in0=A[vb][:, v0 * 128:(v0 + 4) * 128].rearrange(
                            "p (h s) -> p h s", h=4),
                        scalar=1.0, in1=bb4(A["be_tm"][:, v0:v0 + 4]), op0=ALU.mult, op1=ALU.mult),
                       reads=(vb, "be_tm"), writes=("t3f" + sfx,))
                    op("v", lambda e: e.tensor_sub(out=A["Yb" + sfx][:], in0=A["t3f" + sfx][:],
                                                   in1=A["t2f" + sfx][:]),
                       reads=("t3f" + sfx, "t2f" + sfx), writes=("Yb" + sfx,))

            def S_level(gi, j):
                sfx, v0, kq0, b0, b1, b2, b3 = gctx(gi)
                zcur = ("BT4" + sfx) if j == 0 else ("Zb" + sfx)
                pcur = ("B4" + sfx) if j == 0 else ("Pb" + sfx)
                for jj in range(4):
                    op("t", lambda e, jj=jj, zcur=zcur: e.matmul(
                        A[b3][:, jj * 128:(jj + 1) * 128], A[zcur][:, jj * 128:(jj + 1) * 128],
                        A["Yb" + sfx][:, jj * 128:(jj + 1) * 128], start=True, stop=True),
                       reads=(zcur, "Yb" + sfx), writes=(b3,) if jj == 0 else ())
                if j < 6:
                    for jj in range(4):
                        op("t", lambda e, jj=jj, zcur=zcur, pcur=pcur: e.matmul(
                            A[b0][:, jj * 128:(jj + 1) * 128], A[pcur][:, jj * 128:(jj + 1) * 128],
                            A[zcur][:, jj * 128:(jj + 1) * 128], start=True, stop=True),
                           reads=(pcur, zcur), writes=(b0,) if jj == 0 else ())
                    for jj in range(4):
                        op("t", lambda e, jj=jj, zcur=zcur, pcur=pcur: e.matmul(
                            A[b1][:, jj * 128:(jj + 1) * 128], A[zcur][:, jj * 128:(jj + 1) * 128],
                            A[pcur][:, jj * 128:(jj + 1) * 128], start=True, stop=True),
                           reads=(zcur, pcur), writes=(b1,) if jj == 0 else ())
                op("t", lambda e: e.nop(), writes=(b3, b0, b1) if j < 6 else (b3,))
                op("v", lambda e: e.tensor_add(out=A["Yb" + sfx][:], in0=A[b3][:], in1=A["Yb" + sfx][:]),
                   reads=(b3, "Yb" + sfx), writes=("Yb" + sfx,))
                if j < 6:
                    op("s", lambda e: e.activation(out=A["Zb" + sfx][:], in_=A[b0][:], func=AF.Copy),
                       reads=(b0,), writes=("Zb" + sfx,))
                    op("s", lambda e: e.activation(out=A["Pb" + sfx][:], in_=A[b1][:], func=AF.Copy),
                       reads=(b1,), writes=("Pb" + sfx,))

            def B_stage(gi, part):
                sfx, v0, kq0, b0, b1, b2, b3 = gctx(gi)
                h3 = lambda nm: A[nm + sfx][:].rearrange("p (h s) -> p h s", h=4)
                if part == 0:
                    op("v", lambda e: e.tensor_copy(out=A["Ub" + sfx][:], in_=A["Yb" + sfx][:]),
                       reads=("Yb" + sfx,), writes=("Ub" + sfx,))
                    for jj in range(4):
                        op("t", lambda e, jj=jj: e.matmul(
                            A[b2][:, jj * 128:(jj + 1) * 128], A["MT4" + sfx][:, jj * 128:(jj + 1) * 128],
                            A["Ub" + sfx][:, jj * 128:(jj + 1) * 128], start=True, stop=True),
                           reads=("MT4" + sfx, "Ub" + sfx), writes=(b2,) if jj == 0 else ())
                    op("t", lambda e: e.nop(), writes=(b2,))
                    op("v", lambda e: e.tensor_add(out=A["o4" + sfx][:], in0=A[b2][:], in1=A["ot4" + sfx][:]),
                       reads=(b2, "ot4" + sfx), writes=("o4" + sfx,))
                    op("v", lambda e: e.tensor_mul(out=A["yg" + sfx][:], in0=A["o4" + sfx][:],
                                                   in1=A[zb][:, v0 * 128:(v0 + 4) * 128]),
                       reads=("o4" + sfx, zb), writes=("yg" + sfx,))
                    op("v", lambda e: e.tensor_mul(out=A["ygs" + sfx][:], in0=A["yg" + sfx][:],
                                                   in1=A["yg" + sfx][:]),
                       reads=("yg" + sfx,), writes=("ygs" + sfx,))
                    op("v", lambda e: e.tensor_reduce(
                        out=A["msq"][:, v0:v0 + 4], in_=A["ygs" + sfx][:].rearrange("p (h s) -> p h s", h=4),
                        axis=AX.X, op=ALU.add), reads=("ygs" + sfx,), writes=("msq",), bump=True)
                    op("s", lambda e: e.activation(out=A["rr"][:, v0:v0 + 4], in_=A["msq"][:, v0:v0 + 4],
                                                   func=AF.Ln, bias=A["epsrms"][:], scale=1.0 / DV),
                       reads=("msq", "epsrms"), writes=("rr",))
                    op("s", lambda e: e.activation(out=A["rr"][:, v0:v0 + 4], in_=A["rr"][:, v0:v0 + 4],
                                                   func=AF.Exp, scale=-0.5), reads=("rr",), writes=("rr",),
                       bump=True)
                elif part == 1:
                    op("v", lambda e: e.scalar_tensor_tensor(
                        out=h3("y4"), in0=A["yg" + sfx][:].rearrange("p (h s) -> p h s", h=4), scalar=1.0,
                        in1=bb4(A["rr"][:, v0:v0 + 4]), op0=ALU.mult, op1=ALU.mult),
                       reads=("yg" + sfx, "rr"), writes=("y4" + sfx,))
                    for jj in range(4):
                        op("t", lambda e, jj=jj: e.matmul(
                            A[b1][:, jj * 128:(jj + 1) * 128], A["y4" + sfx][:, jj * 128:(jj + 1) * 128],
                            A["id_bf"][:], start=True, stop=True),
                           reads=("y4" + sfx, "id_bf"), writes=(b1,) if jj == 0 else ())
                    op("t", lambda e: e.nop(), writes=(b1,))
                    op("v", lambda e: e.tensor_copy(out=A["ystg"][:, v0 * 128:(v0 + 4) * 128], in_=A[b1][:]),
                       reads=(b1,), writes=("ystg",))
                    op("v", lambda e: e.scalar_tensor_tensor(
                        out=A["Ut" + sfx][:].rearrange("p (h s) -> p h s", h=4),
                        in0=A["Ub" + sfx][:].rearrange("p (h s) -> p h s", h=4), scalar=1.0,
                        in1=bb4(A["cS"][:, v0:v0 + 4]), op0=ALU.mult, op1=ALU.mult),
                       reads=("Ub" + sfx, "cS"), writes=("Ut" + sfx,))
                elif part == 2:
                    for jj in range(4):
                        op("t", lambda e, jj=jj: e.matmul(
                            A[b0][:, jj * 128:(jj + 1) * 128],
                            A[ktb][:, (kq0 + jj // 2) * 128:(kq0 + jj // 2 + 1) * 128],
                            A["Ut" + sfx][:, jj * 128:(jj + 1) * 128], start=True, stop=True),
                           reads=(ktb, "Ut" + sfx), writes=(b0,) if jj == 0 else ())
                    op("t", lambda e: e.nop(), writes=(b0,))
                    op("v", lambda e: e.scalar_tensor_tensor(
                        out=A["S"][:, v0 * 128:(v0 + 4) * 128].rearrange("p (h s) -> p h s", h=4),
                        in0=A["S"][:, v0 * 128:(v0 + 4) * 128].rearrange("p (h s) -> p h s", h=4),
                        scalar=1.0, in1=bb4(A["egc"][:, v0:v0 + 4]), op0=ALU.mult, op1=ALU.mult),
                       reads=("S", "egc"), writes=())
                    op("v", lambda e: e.tensor_add(out=A["S"][:, v0 * 128:(v0 + 4) * 128],
                                                   in0=A["S"][:, v0 * 128:(v0 + 4) * 128], in1=A[b0][:]),
                       reads=(b0, "S"), writes=("S",))

            for st in range(NGRP + 1):
                if st < NGRP:
                    F_stage(st, 0)
                if st >= 1:
                    S_level(st - 1, 0)
                if st < NGRP:
                    F_stage(st, 1)
                if st >= 1:
                    S_level(st - 1, 1)
                if st < NGRP:
                    F_stage(st, 2)
                if st >= 1:
                    S_level(st - 1, 2)
                if st < NGRP:
                    F_stage(st, 3)
                if st >= 1:
                    for j_ in range(3, 7):
                        S_level(st - 1, j_)
                    B_stage(st - 1, 0)
                    B_stage(st - 1, 1)
                    B_stage(st - 1, 2)
            op("y", lambda e, c=c: e.dma_start(
                out=dram["y_sp"][:, c * 128:(c + 1) * 128].rearrange("(h p) t -> p h t", p=128),
                in_=A["ystg"][:].rearrange("p (h t) -> p h t", t=128)), dma=True, reads=("ystg",))
        p.barrier()

        if maxphase < 7:
            finwait()
            return
        for dq in range(4):
            op("g", lambda e, dq=dq: e.dma_start(
                out=A["wo_sb"][:].rearrange("p (t c) -> p t c", t=32),
                in_=dram["wout"][:, dq * 512:(dq + 1) * 512].rearrange(
                    "(t p) c -> p t c", p=128)), dma=True, writes=("wo_sb",))
            op("v", lambda e: e.tensor_scalar(out=A["wo_sb"][:], in0=A["wo_sb"][:], scalar1=A["nw"][:],
                                              scalar2=None, op0=ALU.mult),
               reads=("wo_sb", "nw"), writes=("wo_sb",))
            for tt in range(NT):
                yb_ = f"yt{tt % 2}"
                op("y", lambda e, yb_=yb_, tt=tt: e.dma_start(
                    out=A[yb_][:].rearrange("p (h t) -> p h t", t=128),
                    in_=dram["y_sp"][:, tt * 128:(tt + 1) * 128].rearrange(
                        "(h p) t -> p h t", p=128)), dma=True, writes=(yb_,))
                bk = f"P{tt % 2}"
                for hv in range(HV):
                    op("t", lambda e, bk=bk, hv=hv, yb_=yb_: e.matmul(
                        A[bk][:], A[yb_][:, hv * 128:(hv + 1) * 128],
                        A["wo_sb"][:, hv * 512:(hv + 1) * 512], start=(hv == 0), stop=(hv == HV - 1)),
                       reads=(yb_, "wo_sb"), writes=(bk,) if hv == 0 else ())
                op("t", lambda e: e.nop(), writes=(bk,))
                if dq < 3:
                    op("v", lambda e, bk=bk, dq=dq: e.tensor_copy(out=A["mstage"][:, 0:512], in_=A[bk][:]),
                       reads=(bk,), writes=("mslot",))
                    op("y", lambda e, tt=tt, dq=dq: e.dma_start(
                        out=dram["m0_sp"][tt * 128:(tt + 1) * 128, dq * 512:(dq + 1) * 512],
                        in_=A["mstage"][:, 0:512]), dma=True, reads=("mslot",), writes=("dm0sp",))
                else:
                    op("y", lambda e, tt=tt: e.dma_start(
                        out=A["mstage"][:, 0:1536], in_=dram["m0_sp"][tt * 128:(tt + 1) * 128, :]),
                       dma=True, reads=("dm0sp",), writes=("mstage",))
                    op("v", lambda e, bk=bk: e.tensor_copy(out=A["mstage"][:, 1536:2048], in_=A[bk][:]),
                       reads=(bk,), writes=("mstage",))
                    ln_full("mstage", "lnst4", "lnt4", None)
                    op("y", lambda e, tt=tt: e.dma_start(out=A["xa4"][:],
                                                         in_=dram["x"][tt * 128:(tt + 1) * 128, :]),
                       dma=True, writes=("xa4",))
                    op("v", lambda e: e.scalar_tensor_tensor(
                        out=A["x1t"][:], in0=A["mstage"][:], scalar=A["mcol"][:],
                        in1=A["rstd"][:].broadcast_to([128, D]), op0=ALU.subtract, op1=ALU.mult),
                       reads=("mstage", "mcol", "rstd"), writes=("x1t",))
                    op("v", lambda e: e.tensor_add(out=A["x1t"][:], in0=A["x1t"][:], in1=A["xa4"][:]),
                       reads=("x1t", "xa4"), writes=("x1t",))
                    op("y", lambda e, tt=tt: e.dma_start(out=dram["x1_sp"][tt * 128:(tt + 1) * 128, :],
                                                         in_=A["x1t"][:]), dma=True, reads=("x1t",))
                    ln_full("x1t", "lnst4", "lnt4", None)
                    transpose_to_lnd("lnt4", tt, 2)
        p.barrier()

        if maxphase < 8:
            finwait()
            return
        for et in range(64):
            bi = et % 2
            wub, wgb = f"wu{bi}", f"wg{bi}"
            op("y", lambda e, wub=wub, et=et: e.dma_start(
                out=A[wub][:].rearrange("p (t c) -> p t c", t=16),
                in_=dram["wupT"][:, et * 128:(et + 1) * 128].rearrange(
                    "(t p) c -> p t c", p=128)), dma=True, writes=(wub,))
            op("y", lambda e, wgb=wgb, et=et: e.dma_start(
                out=A[wgb][:].rearrange("p (t c) -> p t c", t=16),
                in_=dram["wgT"][:, et * 128:(et + 1) * 128].rearrange(
                    "(t p) c -> p t c", p=128)), dma=True, writes=(wgb,))
            sg1, sg2 = f"sgn1{bi}", f"sgn2{bi}"
            for (wb_, tq, sc) in ((wub, f"tqu{bi}", 3), (wgb, f"tqg{bi}", 4)):
                op("s", lambda e, wb_=wb_, sc=sc, sg1=sg1: e.activation(out=A[sg1][:], in_=A[wb_][:], func=AF.Sign,
                                                               bias=A["srep"][:, sc:sc + 1], scale=-1.0),
                   reads=(wb_, "srep"), writes=(sg1,))
                op("s", lambda e, wb_=wb_, sc=sc, sg2=sg2: e.activation(out=A[sg2][:], in_=A[wb_][:], func=AF.Sign,
                                                               bias=A["srep"][:, sc:sc + 1], scale=1.0),
                   reads=(wb_, "srep"), writes=(sg2,))
                op("v", lambda e, tq=tq, sg1=sg1, sg2=sg2: e.tensor_sub(out=A[tq][:], in0=A[sg2][:], in1=A[sg1][:]),
                   reads=(sg1, sg2), writes=(tq,))
            for tp in range(L // 512):
                bk = f"P{tp % 2}"
                for dtile in range(16):
                    op("t", lambda e, bk=bk, dtile=dtile, tp=tp: e.matmul(
                        A[bk][:], A[f"tqu{bi}"][:, dtile * 128:(dtile + 1) * 128],
                        lnd3[:, dtile, tp * 512:(tp + 1) * 512], start=(dtile == 0), stop=(dtile == 15)),
                       reads=(f"tqu{bi}", "lnd"), writes=(bk,) if dtile == 0 else ())
                op("t", lambda e: e.nop(), writes=(bk,))
                op("s", lambda e, bk=bk, tp=tp: e.activation(
                    out=A["hst"][:, tp * 512:(tp + 1) * 512], in_=A[bk][:], func=AF.Silu,
                    scale=A["srep"][:, 0:1]), reads=(bk, "srep"), writes=("hst",))
            for tp in range(L // 512):
                bk = f"P{2 + tp % 2}"
                for dtile in range(16):
                    op("t", lambda e, bk=bk, dtile=dtile, tp=tp: e.matmul(
                        A[bk][:], A[f"tqg{bi}"][:, dtile * 128:(dtile + 1) * 128],
                        lnd3[:, dtile, tp * 512:(tp + 1) * 512], start=(dtile == 0), stop=(dtile == 15)),
                       reads=(f"tqg{bi}", "lnd"), writes=(bk,) if dtile == 0 else ())
                op("t", lambda e: e.nop(), writes=(bk,))
                op("v", lambda e, bk=bk: e.tensor_scalar(out=A["gsc"][:], in0=A[bk][:],
                                                         scalar1=A["srep"][:, 1:2], scalar2=None,
                                                         op0=ALU.mult), reads=(bk, "srep"), writes=("gsc",))
                op("v", lambda e, tp=tp: e.tensor_mul(out=A["hst"][:, tp * 512:(tp + 1) * 512],
                                                      in0=A["hst"][:, tp * 512:(tp + 1) * 512],
                                                      in1=A["gsc"][:]),
                   reads=("hst", "gsc"), writes=("hst",))
            op("y", lambda e, et=et: e.dma_start(out=dram["h_sp"][et * 128:(et + 1) * 128, :],
                                                 in_=A["hst"][:]), dma=True, reads=("hst",))
        p.barrier()

        if maxphase < 9:
            finwait()
            return
        for dq in range(4):
            for et in range(64):
                bi = et % 2
                wdb = f"wd{bi}"
                op("y", lambda e, wdb=wdb, et=et, dq=dq: e.dma_start(
                    out=A[wdb][:], in_=dram["wdT"][et * 128:(et + 1) * 128, dq * 512:(dq + 1) * 512]),
                   dma=True, writes=(wdb,))
                op("s", lambda e, wdb=wdb: e.activation(out=A["sg6a"][:], in_=A[wdb][:], func=AF.Sign,
                                                        bias=A["srep"][:, 5:6], scale=-1.0),
                   reads=(wdb, "srep"), writes=("sg6a",))
                op("s", lambda e, wdb=wdb: e.activation(out=A["sg6b"][:], in_=A[wdb][:], func=AF.Sign,
                                                        bias=A["srep"][:, 5:6], scale=1.0),
                   reads=(wdb, "srep"), writes=("sg6b",))
                op("v", lambda e, et=et: e.tensor_sub(out=A["td"][:, et * 512:(et + 1) * 512],
                                                      in0=A["sg6b"][:], in1=A["sg6a"][:]),
                   reads=("sg6a", "sg6b"), writes=("td",))
            for tt in range(NT):
                bi = tt % 2
                hb_ = f"hb{bi}"
                op("y", lambda e, hb_=hb_, tt=tt: e.dma_start(
                    out=A[hb_][:].rearrange("p (h t) -> p h t", t=128),
                    in_=dram["h_sp"][:, tt * 128:(tt + 1) * 128].rearrange(
                        "(h p) t -> p h t", p=128)), dma=True, writes=(hb_,))
                bk = f"P{tt % 2}"
                for et in range(64):
                    op("t", lambda e, bk=bk, et=et, hb_=hb_: e.matmul(
                        A[bk][:], A[hb_][:, et * 128:(et + 1) * 128],
                        A["td"][:, et * 512:(et + 1) * 512], start=(et == 0), stop=(et == 63)),
                       reads=(hb_, "td"), writes=(bk,) if et == 0 else ())
                op("t", lambda e: e.nop(), writes=(bk,))
                if dq < 3:
                    op("v", lambda e, bk=bk: e.tensor_scalar(out=A["fst"][:, 0:512], in0=A[bk][:],
                                                             scalar1=A["srep"][:, 2:3], scalar2=None,
                                                             op0=ALU.mult), reads=(bk, "srep"),
                       writes=("fslot",))
                    op("y", lambda e, tt=tt, dq=dq: e.dma_start(
                        out=dram["f0_sp"][tt * 128:(tt + 1) * 128, dq * 512:(dq + 1) * 512],
                        in_=A["fst"][:, 0:512]), dma=True, reads=("fslot",), writes=("df0sp",))
                else:
                    op("y", lambda e, tt=tt: e.dma_start(
                        out=A["fst"][:, 0:1536], in_=dram["f0_sp"][tt * 128:(tt + 1) * 128, :]),
                       dma=True, reads=("df0sp",), writes=("fst",))
                    op("v", lambda e, bk=bk: e.tensor_scalar(out=A["fst"][:, 1536:2048], in0=A[bk][:],
                                                             scalar1=A["srep"][:, 2:3], scalar2=None,
                                                             op0=ALU.mult), reads=(bk, "srep"),
                       writes=("fst",))
                    op("v", lambda e: e.tensor_reduce(out=A["sumx"][:], in_=A["fst"][:], axis=AX.X,
                                                      op=ALU.add), reads=("fst",), writes=("sumx",))
                    op("v", lambda e: e.scalar_tensor_tensor(out=A["lnst6"][:], in0=A["fst"][:], scalar=1.0,
                                                             in1=A["fst"][:], op0=ALU.mult, op1=ALU.mult,
                                                             accum_out=A["sumq"][:]),
                       reads=("fst",), writes=("lnst6", "sumq"))
                    op("v", lambda e: e.tensor_scalar(out=A["mcol"][:], in0=A["sumx"][:], scalar1=1.0 / D,
                                                      scalar2=None, op0=ALU.mult),
                       reads=("sumx",), writes=("mcol",))
                    op("v", lambda e: e.tensor_tensor(out=A["vcol"][:], in0=A["mcol"][:], in1=A["mcol"][:],
                                                      op=ALU.mult), reads=("mcol",), writes=("vcol",))
                    op("v", lambda e: e.scalar_tensor_tensor(out=A["vcol"][:], in0=A["sumq"][:],
                                                             scalar=1.0 / D, in1=A["vcol"][:],
                                                             op0=ALU.mult, op1=ALU.subtract),
                       reads=("sumq", "vcol"), writes=("vcol",), bump=True)
                    op("s", lambda e: e.activation(out=A["rstd"][:], in_=A["vcol"][:], func=AF.Ln,
                                                   bias=A["epsln"][:]), reads=("vcol", "epsln"),
                       writes=("rstd",))
                    op("s", lambda e: e.activation(out=A["rstd"][:], in_=A["rstd"][:], func=AF.Exp,
                                                   scale=-0.5), reads=("rstd",), writes=("rstd",),
                       bump=True)
                    op("y", lambda e, tt=tt: e.dma_start(out=A["xa6"][:],
                                                         in_=dram["x1_sp"][tt * 128:(tt + 1) * 128, :]),
                       dma=True, writes=("xa6",))
                    op("v", lambda e: e.scalar_tensor_tensor(
                        out=A["x6t"][:], in0=A["fst"][:], scalar=A["mcol"][:],
                        in1=A["rstd"][:].broadcast_to([128, D]), op0=ALU.subtract, op1=ALU.mult),
                       reads=("fst", "mcol", "rstd"), writes=("x6t",))
                    op("v", lambda e: e.tensor_add(out=A["x6t"][:], in0=A["x6t"][:], in1=A["xa6"][:]),
                       reads=("x6t", "xa6"), writes=("x6t",))
                    op("y", lambda e, tt=tt: e.dma_start(out=dram["out"][tt * 128:(tt + 1) * 128, :],
                                                         in_=A["x6t"][:]), dma=True, reads=("x6t",))
        finwait()

    with nc.Block() as blk:
        @blk.tensor
        def _(e):
            program("t", e)

        @blk.vector
        def _(e):
            program("v", e)

        @blk.scalar
        def _(e):
            program("s", e)

        @blk.gpsimd
        def _(e):
            program("g", e)

        @blk.sync
        def _(e):
            program("y", e)

    stack.close()
    return nc


def host_inputs(core_x, W):
    c = np.ascontiguousarray
    cw = W["conv_w"].reshape(64, 128, KC).transpose(1, 0, 2).reshape(128, 64 * KC)
    return {
        "x": c(core_x.astype(np.float32, copy=False)),
        "wqkv": c(W["Wqkv"]), "wz": c(W["Wz"]),
        "wab": c(np.concatenate([W["Wa"], W["Wb"]], axis=1)),
        "cw": c(cw),
        "alog": c(W["A_log"].reshape(32, 1)), "dtb": c(W["dt_bias"].reshape(32, 1)),
        "nw": c(W["norm_w"].reshape(128, 1)),
        "wout": c(W["Wout"]),
        "wupT": c(W["W_up"].T), "wgT": c(W["W_gate"].T), "wdT": c(W["W_down"].T),
    }


def kernel(L0_attn_x, L0_attn_cu_seqlens, L0_attn_Wqkv, L0_attn_Wz, L0_attn_Wa, L0_attn_Wb,
           L0_attn_conv_w, L0_attn_A_log, L0_attn_dt_bias, L0_attn_norm_w, L0_attn_Wout,
           L0_ffn_W_up, L0_ffn_W_gate, L0_ffn_W_down):
    W = {
        "Wqkv": np.asarray(L0_attn_Wqkv, np.float32), "Wz": np.asarray(L0_attn_Wz, np.float32),
        "Wa": np.asarray(L0_attn_Wa, np.float32), "Wb": np.asarray(L0_attn_Wb, np.float32),
        "conv_w": np.asarray(L0_attn_conv_w, np.float32), "A_log": np.asarray(L0_attn_A_log, np.float32),
        "dt_bias": np.asarray(L0_attn_dt_bias, np.float32), "norm_w": np.asarray(L0_attn_norm_w, np.float32),
        "Wout": np.asarray(L0_attn_Wout, np.float32), "W_up": np.asarray(L0_ffn_W_up, np.float32),
        "W_gate": np.asarray(L0_ffn_W_gate, np.float32), "W_down": np.asarray(L0_ffn_W_down, np.float32),
    }
    x = np.asarray(L0_attn_x, np.float32)
    nc = build(L=x.shape[1])
    in_maps = [host_inputs(x[b], W) for b in range(x.shape[0])]
    res = run_bass_kernel_spmd(nc, in_maps, core_ids=list(range(x.shape[0])))
    return np.stack([res.results[b]["out"] for b in range(x.shape[0])], axis=0)


# version 09
# speedup vs baseline: 1.2345x; 1.1768x over previous
import numpy as np
import concourse.bass as bass
import concourse.mybir as mybir
from concourse.bass_utils import run_bass_kernel_spmd

F32 = mybir.dt.float32
BF16 = mybir.dt.bfloat16
AF = mybir.ActivationFunctionType
ALU = mybir.AluOpType
AX = mybir.AxisListType

D = 2048
HK, HV, DK, DV = 16, 32, 128, 128
ECH = 8192
KC = 4
C = 128
EPS_LN = 1e-5
EPS_L2 = 1e-6
EPS_RMS = 1e-6
EPS_TER = 1e-8
LOGC = float(-0.5 * np.log(DK))
NEG = -1e30

DEFSEM = {"t": "pe", "v": "ve", "s": "ac", "g": "po", "y": "so"}


class Prog:

    def __init__(self, E, eng, sems):
        self.E = E
        self.eng = eng
        self.sems = sems
        self.cnt = {k: 0 for k in sems}
        self.lastw = {}
        self.reads = {}
        self.dsems = {}
        self.dpool = iter([f"dm{i:02d}" for i in range(64)])
        self.junk = None
        self.hiw = {}

    def op(self, ec, fn, reads=(), writes=(), sem=None, n=1, xwait=(), dma=False, bump=False):
        if dma:
            key = writes[0] if writes else ("sp:" + reads[0])
            if key not in self.dsems:
                self.dsems[key] = next(self.dpool)
            sem = self.dsems[key]
            n = 16
            for b in reads:
                lw = self.lastw.get(b)
                assert key.startswith("ddbg") or lw is None or lw[2] == "t" or lw[3], \
                    ("dma reads unbumped", b, lw)
        if sem is None:
            sem = DEFSEM[ec]
        waits = {}
        for b in reads:
            lw = self.lastw.get(b)
            if lw is not None and not (lw[2] == "t" and ec == "t"):
                waits[lw[0]] = max(waits.get(lw[0], 0), lw[1])
        for b in writes:
            lw = self.lastw.get(b)
            if lw is not None and not (lw[2] == "t" and ec == "t"):
                waits[lw[0]] = max(waits.get(lw[0], 0), lw[1])
            for rec, (s2, v2) in self.reads.get(b, {}).items():
                if not (rec == "t" and ec == "t"):
                    waits[s2] = max(waits.get(s2, 0), v2)
        for (s2, v2) in xwait:
            waits[s2] = max(waits.get(s2, 0), v2)
        if self.E == ec:
            for s2, v2 in waits.items():
                if v2 > 0:
                    self.eng.wait_ge(self.sems[s2], v2)
            ins = fn(self.eng)
            if bump:
                ins = self.eng.drain()
            if ins is not None:
                ins.then_inc(self.sems[sem], n)
        self.cnt[sem] += n
        ms = (sem, self.cnt[sem])
        for b in reads:
            self.reads.setdefault(b, {})[ec] = ms
        for b in writes:
            self.lastw[b] = (sem, self.cnt[sem], ec, bump or dma)
            self.reads[b] = {}
        return ms

    def barrier(self):
        vals = []
        for ec in ("t", "v", "s", "g", "y"):
            s = DEFSEM[ec]
            self.cnt[s] += 1
            vals.append((ec, s, self.cnt[s]))
            if self.E == ec:
                self.eng.nop().then_inc(self.sems[s], 1)
        dma_waits = [(s, self.cnt[s]) for s in self.dsems.values()]
        for ec2 in ("t", "v", "s", "g", "y"):
            if self.E == ec2:
                for (ec, s, v) in vals:
                    if ec != ec2:
                        self.eng.wait_ge(self.sems[s], v)
                        self.hiw[s] = max(self.hiw.get(s, 0), v)
                for (s, v) in dma_waits:
                    if v > 0:
                        self.eng.wait_ge(self.sems[s], v)
                        self.hiw[s] = max(self.hiw.get(s, 0), v)
        self.lastw = {}
        self.reads = {}


def build(L=2048, dbg=False, maxphase=99, dlim=(99, 99, 99)):
    NT = L // 128
    NCH = L // C
    nc = bass.Bass("TRN2", target_bir_lowering=False, debug=False,
                   detect_race_conditions=False)

    dram = {}
    def dt_(name, shape, dtype, kind):
        dram[name] = nc.dram_tensor(name, shape, dtype, kind=kind).ap()
    dt_("x", [L, D], F32, "ExternalInput")
    dt_("wqkv", [D, ECH], F32, "ExternalInput")
    dt_("wz", [D, HV * DV], F32, "ExternalInput")
    dt_("wab", [D, 64], F32, "ExternalInput")
    dt_("cw", [128, 64 * KC], F32, "ExternalInput")
    dt_("alog", [32, 1], F32, "ExternalInput")
    dt_("dtb", [32, 1], F32, "ExternalInput")
    dt_("nw", [128, 1], F32, "ExternalInput")
    dt_("wout", [HV * DV, D], F32, "ExternalInput")
    dt_("wupT", [D, ECH], F32, "ExternalInput")
    dt_("wgT", [D, ECH], F32, "ExternalInput")
    dt_("wdT", [ECH, D], F32, "ExternalInput")
    dt_("out", [L, D], F32, "ExternalOutput")
    sk = "ExternalOutput" if dbg else "Internal"
    dt_("q_sp", [HK * DK, L], BF16, sk)
    dt_("k_sp", [HK * DK, L], BF16, sk)
    dt_("kt_sp", [L, HK * DK], BF16, sk)
    dt_("v_sp", [L, HV * DV], BF16, sk)
    dt_("sz_sp", [L, HV * DV], BF16, sk)
    dt_("y_sp", [HV * DV, L], BF16, sk)
    dt_("x1_sp", [L, D], F32, sk)
    dt_("h_sp", [ECH, L], BF16, sk)
    dt_("f0_sp", [L, 1536], F32, "Internal")
    dt_("m0_sp", [L, 1536], F32, "Internal")
    dt_("gg_sp", [2, 64, 128], F32, "Internal")
    dt_("gc_sp", [2, 32], F32, "Internal")
    dt_("scl_sp", [1, 8], F32, "Internal")
    if dbg:
        for i in range(10):
            dt_(f"dbg{i}", [128, 512], F32, sk)

    import contextlib
    stack = contextlib.ExitStack()

    PERMW = 10200
    TMPW = 42600
    perm = stack.enter_context(nc.sbuf_tensor("perm", [128, PERMW], F32))
    tmp = stack.enter_context(nc.sbuf_tensor("tmp", [128, TMPW], F32))
    A = {}

    def carve(arena, off, name, parts, words, dtype):
        assert off + words <= (PERMW if arena is perm else TMPW), (name, off, words)
        v = arena[0:parts, off:off + words]
        if dtype == BF16:
            v = v.bitcast(BF16)
        A[name] = v
        return off + words

    o = 0
    o = carve(perm, o, "ones1", 1, 128, F32)
    o = carve(perm, o, "onesc_bf", 128, 1, BF16)
    o = carve(perm, o, "onesc_f", 128, 1, F32)
    o = carve(perm, o, "id_bf", 128, 64, BF16)
    o = carve(perm, o, "id_f", 128, 128, F32)
    o = carve(perm, o, "triu_f", 128, 128, F32)
    o = carve(perm, o, "maskA", 128, 512, F32)
    o = carve(perm, o, "maskM", 128, 512, F32)
    o = carve(perm, o, "epsln", 128, 1, F32)
    o = carve(perm, o, "epsl2", 128, 1, F32)
    o = carve(perm, o, "epsrms", 128, 1, F32)
    o = carve(perm, o, "logcc", 128, 1, F32)
    o = carve(perm, o, "alog", 32, 1, F32)
    o = carve(perm, o, "dtb", 32, 1, F32)
    o = carve(perm, o, "negea", 32, 1, F32)
    o = carve(perm, o, "nw", 128, 1, F32)
    o = carve(perm, o, "cw", 128, 64 * KC, F32)
    o = carve(perm, o, "gT", 32, L, F32)
    o = carve(perm, o, "lbT", 32, L, F32)
    o = carve(perm, o, "beT", 32, L, F32)
    o = carve(perm, o, "ssq", 128, 32 * NT, F32)
    o = carve(perm, o, "lnss", 128, 32 * NT, F32)
    o = carve(perm, o, "scl", 1, 8, F32)
    o = carve(perm, o, "srep", 128, 8, F32)
    o = carve(perm, o, "wacc", 128, 64, F32)
    o = carve(perm, o, "wred", 128, 4, F32)
    for nm in ("sumx", "sumq", "mcol", "vcol", "rstd"):
        o = carve(perm, o, nm, 128, 1, F32)
    for nm in ("g_tm", "lb_tm", "be_tm", "Gtil", "Ghat", "gcrep", "e1", "e2",
               "cS", "egc", "csub", "msq", "rr"):
        o = carve(perm, o, nm, 128, 32, F32)
    o = carve(perm, o, "GG", 128, 64, F32)
    o = carve(perm, o, "GGT", 64, 128, F32)
    o = carve(perm, o, "junk", 1, 4, F32)
    assert o <= PERMW, o

    o = 0
    o = carve(tmp, o, "lnd", 128, 16 * L // 2, BF16)
    o = carve(tmp, o, "wz0", 128, 16 * 512 // 2, BF16)
    o = carve(tmp, o, "wz1", 128, 16 * 512 // 2, BF16)
    o = carve(tmp, o, "wq0", 128, 16 * 128 // 2, BF16)
    o = carve(tmp, o, "wq1", 128, 16 * 128 // 2, BF16)
    o = carve(tmp, o, "wabt", 128, 16 * 64 // 2, BF16)
    oz = o
    WSCW = 4096 if L >= 2048 else 512
    carve(tmp, oz, "wsc", 128, WSCW, F32)
    o = carve(tmp, o, "qkv0", 128, (L + KC - 1 + 1) // 2, BF16)
    o = carve(tmp, o, "qkv1", 128, (L + KC - 1 + 1) // 2, BF16)
    o = carve(tmp, o, "cacc", 128, L // 2, BF16)
    o = carve(tmp, o, "mix", 128, L // 2, BF16)
    o = carve(tmp, o, "sqt", 128, L // 2, BF16)
    o = carve(tmp, o, "tstage", 128, 16 * 128 // 2, BF16)
    o = carve(tmp, o, "absb", 64, L, F32)
    o = carve(tmp, o, "spsc", 32, L, F32)
    o = carve(tmp, o, "xa", 128, D, F32)
    o = carve(tmp, o, "lnst", 128, D, F32)
    o = carve(tmp, o, "lntile", 128, D // 2, BF16)
    assert o <= TMPW, o

    o = 0
    o = carve(tmp, o, "S", 128, HV * DV, F32)
    o = carve(tmp, o, "S0b", 128, HV * DV // 2, BF16)
    o = carve(tmp, o, "grw", 128, 32 * 128, F32)
    o = carve(tmp, o, "qb0", 128, HK * 128 // 2, BF16)
    o = carve(tmp, o, "qb1", 128, HK * 128 // 2, BF16)
    o = carve(tmp, o, "kb0", 128, HK * 128 // 2, BF16)
    o = carve(tmp, o, "kb1", 128, HK * 128 // 2, BF16)
    o = carve(tmp, o, "ktb0", 128, HK * 128 // 2, BF16)
    o = carve(tmp, o, "ktb1", 128, HK * 128 // 2, BF16)
    o = carve(tmp, o, "vb0", 128, HV * 128 // 2, BF16)
    o = carve(tmp, o, "vb1", 128, HV * 128 // 2, BF16)
    o = carve(tmp, o, "zb0", 128, HV * 128 // 2, BF16)
    o = carve(tmp, o, "zb1", 128, HV * 128 // 2, BF16)
    for sfx in ("0", "1"):
        for nm in ("Din", "Din2", "t2f", "t3f", "ot4", "o4", "yg", "ygs"):
            o = carve(tmp, o, nm + sfx, 128, 512, F32)
        for nm in ("EA", "EM", "B4", "M4", "BT4", "MT4", "Zb", "Pb", "Yb", "Ub", "Ut", "y4"):
            o = carve(tmp, o, nm + sfx, 128, 256, BF16)
    o = carve(tmp, o, "ystg", 128, HV * 128 // 2, BF16)
    assert o <= TMPW, o

    o = 16 * L // 2
    o = carve(tmp, o, "wo_sb", 128, 32 * 512 // 2, BF16)
    o = carve(tmp, o, "yt0", 128, 32 * 128 // 2, BF16)
    o = carve(tmp, o, "yt1", 128, 32 * 128 // 2, BF16)
    o = carve(tmp, o, "mstage", 128, D, F32)
    o = carve(tmp, o, "x1t", 128, D, F32)
    o = carve(tmp, o, "xa4", 128, D, F32)
    o = carve(tmp, o, "lnst4", 128, D, F32)
    o = carve(tmp, o, "lnt4", 128, D // 2, BF16)
    assert o <= TMPW, o
    o = 16 * L // 2
    o = carve(tmp, o, "wu0", 128, 16 * 128, F32)
    o = carve(tmp, o, "wu1", 128, 16 * 128, F32)
    o = carve(tmp, o, "wg0", 128, 16 * 128, F32)
    o = carve(tmp, o, "wg1", 128, 16 * 128, F32)
    o = carve(tmp, o, "tqu0", 128, 16 * 128 // 2, BF16)
    o = carve(tmp, o, "tqg0", 128, 16 * 128 // 2, BF16)
    o = carve(tmp, o, "tqu1", 128, 16 * 128 // 2, BF16)
    o = carve(tmp, o, "tqg1", 128, 16 * 128 // 2, BF16)
    o = carve(tmp, o, "sgn10", 128, 16 * 128 // 2, BF16)
    o = carve(tmp, o, "sgn20", 128, 16 * 128 // 2, BF16)
    o = carve(tmp, o, "sgn11", 128, 16 * 128 // 2, BF16)
    o = carve(tmp, o, "sgn21", 128, 16 * 128 // 2, BF16)
    o = carve(tmp, o, "hst", 128, L // 2, BF16)
    o = carve(tmp, o, "gsc", 128, 512 // 2, BF16)
    assert o <= TMPW, o

    o = 0
    o = carve(tmp, o, "td", 128, 64 * 512 // 2, BF16)
    o = carve(tmp, o, "hb0", 128, 64 * 128 // 2, BF16)
    o = carve(tmp, o, "hb1", 128, 64 * 128 // 2, BF16)
    o = carve(tmp, o, "wd0", 128, 512, F32)
    o = carve(tmp, o, "wd1", 128, 512, F32)
    o = carve(tmp, o, "fst", 128, D, F32)
    o = carve(tmp, o, "x6t", 128, D, F32)
    o = carve(tmp, o, "xa6", 128, D, F32)
    o = carve(tmp, o, "lnst6", 128, D, F32)
    o = carve(tmp, o, "sg6a", 128, 512 // 2, BF16)
    o = carve(tmp, o, "sg6b", 128, 512 // 2, BF16)
    assert o <= TMPW, o

    P_ = {}
    for i in range(8):
        P_[i] = stack.enter_context(nc.psum_tensor(f"pb{i}", [128, 512], F32))
        A[f"P{i}"] = P_[i]

    sems = {}
    for s in ("pe", "ve", "ac", "po", "so"):
        sems[s] = stack.enter_context(nc.semaphore(s))
    for i in range(64):
        sems[f"dm{i:02d}"] = stack.enter_context(nc.semaphore(f"dm{i:02d}"))

    lnd3 = A["lnd"][:, :].rearrange("p (dt t) -> p dt t", dt=16)

    def program(E, eng):
        p = Prog(E, eng, sems)
        p.junk = A["junk"]
        op = p.op

        def ln_full(xbuf, lnstbuf, outbf, rkey):
            op("v", lambda e: e.tensor_reduce(out=A["sumx"][:], in_=A[xbuf][:], axis=AX.X, op=ALU.add),
               reads=(xbuf,), writes=("sumx",))
            op("v", lambda e: e.scalar_tensor_tensor(out=A[lnstbuf][:], in0=A[xbuf][:], scalar=1.0,
                                                     in1=A[xbuf][:], op0=ALU.mult, op1=ALU.mult,
                                                     accum_out=A["sumq"][:]),
               reads=(xbuf,), writes=(lnstbuf, "sumq"))
            op("v", lambda e: e.tensor_scalar(out=A["mcol"][:], in0=A["sumx"][:], scalar1=1.0 / D,
                                              scalar2=None, op0=ALU.mult),
               reads=("sumx",), writes=("mcol",))
            op("v", lambda e: e.tensor_tensor(out=A["vcol"][:], in0=A["mcol"][:], in1=A["mcol"][:],
                                              op=ALU.mult), reads=("mcol",), writes=("vcol",))
            op("v", lambda e: e.scalar_tensor_tensor(out=A["vcol"][:], in0=A["sumq"][:], scalar=1.0 / D,
                                                     in1=A["vcol"][:], op0=ALU.mult, op1=ALU.subtract),
               reads=("sumq", "vcol"), writes=("vcol",), bump=True)
            op("s", lambda e: e.activation(out=A["rstd"][:], in_=A["vcol"][:], func=AF.Ln,
                                           bias=A["epsln"][:]), reads=("vcol", "epsln"), writes=("rstd",))
            op("s", lambda e: e.activation(out=A["rstd"][:], in_=A["rstd"][:], func=AF.Exp, scale=-0.5),
               reads=("rstd",), writes=("rstd",), bump=True)
            op("v", lambda e: e.scalar_tensor_tensor(
                out=A[outbf][:], in0=A[xbuf][:], scalar=A["mcol"][:],
                in1=A["rstd"][:].broadcast_to([128, D]), op0=ALU.subtract, op1=ALU.mult),
               reads=(xbuf, "mcol", "rstd"), writes=(outbf,))

        def transpose_to_lnd(srcbf, tt, bank0):
            for g4 in range(4):
                bk = f"P{bank0 + g4 % 2}"
                for j in range(4):
                    dtile = g4 * 4 + j
                    op("t", lambda e, bk=bk, j=j, dtile=dtile: e.matmul(
                        A[bk][:, j * 128:(j + 1) * 128], A[srcbf][:, dtile * 128:(dtile + 1) * 128],
                        A["id_bf"][:], start=True, stop=True),
                       reads=(srcbf, "id_bf"), writes=(bk,))
                op("v", lambda e, bk=bk, g4=g4, tt=tt: e.tensor_copy(
                    out=lnd3[:, g4 * 4:(g4 + 1) * 4, tt * 128:(tt + 1) * 128], in_=A[bk][:]),
                   reads=(bk,), writes=("lnd",))

        def finwait():
            if E == "y":
                for s in list(p.dsems.values()):
                    if p.cnt[s] > 0:
                        eng.wait_ge(sems[s], p.cnt[s])

        op("v", lambda e: e.memset(A["junk"][:], 0.0), writes=("junkb",))
        op("v", lambda e: e.memset(A["ones1"][:], 1.0), writes=("ones1",))
        op("v", lambda e: e.memset(A["onesc_bf"][:], 1.0), writes=("onesc_bf",))
        op("v", lambda e: e.memset(A["onesc_f"][:], 1.0), writes=("onesc_f",))
        op("v", lambda e: e.memset(A["id_bf"][:], 1.0), writes=("id_bf",))
        op("v", lambda e: e.memset(A["id_f"][:], 1.0), writes=("id_f",))
        op("v", lambda e: e.memset(A["triu_f"][:], 1.0), writes=("triu_f",))
        op("v", lambda e: e.memset(A["maskA"][:], 0.0), writes=("maskA",))
        op("v", lambda e: e.memset(A["maskM"][:], 0.0), writes=("maskM",))
        op("v", lambda e: e.memset(A["epsln"][:], EPS_LN), writes=("epsln",))
        op("v", lambda e: e.memset(A["epsl2"][:], EPS_L2), writes=("epsl2",))
        op("v", lambda e: e.memset(A["epsrms"][:], EPS_RMS), writes=("epsrms",))
        op("v", lambda e: e.memset(A["logcc"][:], LOGC), writes=("logcc",))
        op("y", lambda e: e.dma_start(out=A["cw"][:], in_=dram["cw"]), dma=True, writes=("cw",))
        op("y", lambda e: e.dma_start(out=A["alog"][:], in_=dram["alog"]), dma=True, writes=("alog",))
        op("y", lambda e: e.dma_start(out=A["dtb"][:], in_=dram["dtb"]), dma=True, writes=("dtb",))
        op("y", lambda e: e.dma_start(out=A["nw"][:], in_=dram["nw"]), dma=True, writes=("nw",))
        op("g", lambda e: e.affine_select(out=A["id_bf"][:], in_=A["id_bf"][:], pattern=[[1, 128]],
                                          compare_op=ALU.is_equal, fill=0.0, channel_multiplier=-1),
           reads=("id_bf",), writes=("id_bf",))
        op("g", lambda e: e.affine_select(out=A["id_f"][:], in_=A["id_f"][:], pattern=[[1, 128]],
                                          compare_op=ALU.is_equal, fill=0.0, channel_multiplier=-1),
           reads=("id_f",), writes=("id_f",))
        op("g", lambda e: e.affine_select(out=A["triu_f"][:], in_=A["triu_f"][:], pattern=[[1, 128]],
                                          compare_op=ALU.is_ge, fill=0.0, channel_multiplier=-1),
           reads=("triu_f",), writes=("triu_f",))
        op("g", lambda e: e.affine_select(out=A["maskA"][:].rearrange("p (h s) -> p h s", h=4),
                                          in_=A["maskA"][:].rearrange("p (h s) -> p h s", h=4),
                                          pattern=[[0, 4], [-1, 128]], compare_op=ALU.is_ge,
                                          fill=NEG, channel_multiplier=1, base=-1),
           reads=("maskA",), writes=("maskA",))
        op("g", lambda e: e.affine_select(out=A["maskM"][:].rearrange("p (h s) -> p h s", h=4),
                                          in_=A["maskM"][:].rearrange("p (h s) -> p h s", h=4),
                                          pattern=[[0, 4], [-1, 128]], compare_op=ALU.is_ge,
                                          fill=NEG, channel_multiplier=1, base=0),
           reads=("maskM",), writes=("maskM",))
        op("s", lambda e: e.activation(out=A["negea"][:], in_=A["alog"][:], func=AF.Exp),
           reads=("alog",), writes=("negea",), bump=True)

        p.barrier()

        if maxphase < 1:
            finwait()
            return
        for tt in range(NT):
            op("y", lambda e, tt=tt: e.dma_start(out=A["xa"][:], in_=dram["x"][tt * 128:(tt + 1) * 128, :]),
               dma=True, writes=("xa",))
            ln_full("xa", "lnst", "lntile", None)
            if dbg and tt == 0:
                op("y", lambda e: e.dma_start(out=dram["dbg8"][:], in_=A["xa"][:, 0:512]),
                   dma=True, reads=("xa",), writes=("ddbg8",))
                op("v", lambda e: e.tensor_copy(out=A["lnst"][:, 0:512],
                                                in_=A["lntile"][:, 0:512]),
                   reads=("lntile",), writes=("lnst",))
                op("v", lambda e: e.tensor_copy(out=A["lnst"][:, 512:513], in_=A["mcol"][:]),
                   reads=("mcol",), writes=("lnst",))
                op("v", lambda e: e.tensor_copy(out=A["lnst"][:, 513:514], in_=A["rstd"][:]),
                   reads=("rstd",), writes=("lnst",))
                op("v", lambda e: e.tensor_copy(out=A["lnst"][:, 514:515], in_=A["sumq"][:]),
                   reads=("sumq",), writes=("lnst",))
                op("v", lambda e: e.tensor_copy(out=A["lnst"][:, 515:516], in_=A["vcol"][:]),
                   reads=("vcol",), writes=("lnst",))
                op("y", lambda e: e.dma_start(out=dram["dbg9"][:], in_=A["lnst"][:, 0:512]),
                   dma=True, reads=("lnst",), writes=("ddbg9",))
                op("y", lambda e: e.dma_start(out=dram["dbg9"][:, 0:4].rearrange("p f -> p f"),
                                              in_=A["lnst"][:, 512:516]),
                   dma=True, reads=("lnst",), writes=("ddbg9",))
            transpose_to_lnd("lntile", tt, 0)

        if maxphase < 2:
            finwait()
            return
        op("g", lambda e: e.dma_start(out=A["wabt"][:].rearrange("p (t c) -> p t c", t=16),
                                      in_=dram["wab"].rearrange("(t p) c -> p t c", p=128)),
           dma=True, writes=("wabt",))
        for tp in range(L // 512):
            bk = f"P{2 + tp % 2}"
            for dtile in range(16):
                op("t", lambda e, bk=bk, tp=tp, dtile=dtile: e.matmul(
                    A[bk][0:64, :], A["wabt"][:, dtile * 64:(dtile + 1) * 64],
                    lnd3[:, dtile, tp * 512:(tp + 1) * 512],
                    start=(dtile == 0), stop=(dtile == 15)),
                   reads=("wabt", "lnd"), writes=(bk,))
            op("v", lambda e, bk=bk, tp=tp: e.tensor_copy(out=A["absb"][:, tp * 512:(tp + 1) * 512],
                                                          in_=A[bk][0:64, :]),
               reads=(bk,), writes=("absb",))
        op("v", lambda e: e.tensor_scalar(out=A["spsc"][:], in0=A["absb"][0:32, :], scalar1=A["dtb"][:],
                                          scalar2=None, op0=ALU.add), reads=("absb", "dtb"), writes=("spsc",))
        for (dst, src_is_neg) in (("gT", False), ("lbT", True)):
            if src_is_neg:
                op("v", lambda e: e.tensor_scalar(out=A["spsc"][:], in0=A["absb"][32:64, :], scalar1=-1.0,
                                                  scalar2=None, op0=ALU.mult), reads=("absb",), writes=("spsc",))
            op("s", lambda e, dst=dst: e.activation(out=A[dst][:], in_=A["spsc"][:], func=AF.Abs),
               reads=("spsc",), writes=(dst,))
            op("s", lambda e, dst=dst: e.activation(out=A[dst][:], in_=A[dst][:], func=AF.Exp, scale=-1.0),
               reads=(dst,), writes=(dst,))
            op("s", lambda e, dst=dst: e.activation(out=A[dst][:], in_=A[dst][:], func=AF.Ln, bias=1.0),
               reads=(dst,), writes=(dst,))
            op("v", lambda e: e.tensor_scalar(out=A["spsc"][:], in0=A["spsc"][:], scalar1=0.0, scalar2=None,
                                              op0=ALU.max), reads=("spsc",), writes=("spsc",))
            op("v", lambda e, dst=dst: e.tensor_tensor(out=A[dst][:], in0=A["spsc"][:], in1=A[dst][:],
                                                       op=ALU.add), reads=("spsc", dst), writes=(dst,))
        op("v", lambda e: e.tensor_scalar(out=A["gT"][:], in0=A["gT"][:], scalar1=A["negea"][:],
                                          scalar2=-1.0, op0=ALU.mult, op1=ALU.mult),
           reads=("gT", "negea"), writes=("gT",))
        op("v", lambda e: e.tensor_scalar(out=A["lbT"][:], in0=A["lbT"][:], scalar1=-1.0, scalar2=None,
                                          op0=ALU.mult), reads=("lbT",), writes=("lbT",))
        op("s", lambda e: e.activation(out=A["beT"][:], in_=A["lbT"][:], func=AF.Exp),
           reads=("lbT",), writes=("beT",))

        if maxphase < 3:
            finwait()
            return
        op("v", lambda e: e.memset(A["wacc"][:, 0:4], 0.0), writes=("wacc",))

        WCOL = WSCW // 16
        WC2 = WSCW // 64
        NPU = ECH // WCOL
        NPD = D // WC2
        NBLK = 2 * NPU + NPD
        NAB = max(1, WSCW // 2048)

        def scale_block(bi):
            if bi < 2 * NPU:
                wi, cb2 = bi // NPU, bi % NPU
                wn = ("wupT", "wgT")[wi]
                s_ = dram[wn][:, cb2 * WCOL:(cb2 + 1) * WCOL].rearrange("(t p) c -> p t c", p=128)
                tdim = 16
            else:
                wi, cb2 = 2, bi - 2 * NPU
                s_ = dram["wdT"][:, cb2 * WC2:(cb2 + 1) * WC2].rearrange("(t p) c -> p t c", p=128)
                tdim = 64
            op("y", lambda e, s_=s_, tdim=tdim: e.dma_start(
                out=A["wsc"][:].rearrange("p (t c) -> p t c", t=tdim),
                in_=s_), dma=True, writes=("wsc",))
            for kq_ in range(NAB):
                w_ = WSCW // NAB
                op("s", lambda e, kq_=kq_, w_=w_: e.activation(
                    out=A["lnst"][:, 0:w_], in_=A["wsc"][:, kq_ * w_:(kq_ + 1) * w_], func=AF.Abs,
                    accum_out=A["wred"][:, kq_:kq_ + 1]),
                   reads=("wsc",), writes=("lnst", "wred"))
            op("v", lambda e: e.tensor_reduce(
                out=A["wred"][:, 3:4], in_=A["wred"][:, 0:NAB], axis=AX.X, op=ALU.add),
               reads=("wred",), writes=("wred",))
            op("v", lambda e, wi=wi: e.tensor_tensor(
                out=A["wacc"][:, wi:wi + 1], in0=A["wacc"][:, wi:wi + 1], in1=A["wred"][:, 3:4],
                op=ALU.add), reads=("wred", "wacc"), writes=("wacc",), bump=True)

        for cb in range(HV * DV // 512):
            nper_cb = (NBLK + 7) // 8
            for sbi in range(cb * nper_cb, min(NBLK, (cb + 1) * nper_cb)):
                scale_block(sbi)
            wzb = f"wz{cb % 2}"
            op("g", lambda e, wzb=wzb, cb=cb: e.dma_start(
                out=A[wzb][:].rearrange("p (t c) -> p t c", t=16),
                in_=dram["wz"][:, cb * 512:(cb + 1) * 512].rearrange("(t p) c -> p t c", p=128)),
               dma=True, writes=(wzb,))
            for tt in range(NT):
                bk = f"P{4 + tt % 2}"
                for dtile in range(16):
                    op("t", lambda e, bk=bk, wzb=wzb, dtile=dtile, tt=tt: e.matmul(
                        A[bk][:], lnd3[:, dtile, tt * 128:(tt + 1) * 128],
                        A[wzb][:, dtile * 512:(dtile + 1) * 512],
                        start=(dtile == 0), stop=(dtile == 15)),
                       reads=(wzb, "lnd"), writes=(bk,))
                slot = f"lt{tt % 2}"
                op("s", lambda e, bk=bk, tt=tt: e.activation(
                    out=A["lntile"][:, (tt % 2) * 512:(tt % 2 + 1) * 512], in_=A[bk][:], func=AF.Silu),
                   reads=(bk,), writes=(slot,), bump=True)
                op("y", lambda e, tt=tt, cb=cb: e.dma_start(
                    out=dram["sz_sp"][tt * 128:(tt + 1) * 128, cb * 512:(cb + 1) * 512],
                    in_=A["lntile"][:, (tt % 2) * 512:(tt % 2 + 1) * 512]),
                   dma=True, reads=(slot,))

        if maxphase < 4:
            finwait()
            return
        op("v", lambda e: e.tensor_copy(out=A["junk"][0:1, 2:4], in_=A["junk"][0:1, 0:2]),
           reads=("wsc",), writes=("wsc",))
        op("s", lambda e: e.copy(out=A["junk"][0:1, 2:4], in_=A["junk"][0:1, 0:2]),
           reads=("wsc",), writes=("wsc",))
        op("v", lambda e: e.memset(A["qkv0"][:, 0:KC - 1], 0.0), writes=())
        op("v", lambda e: e.memset(A["qkv1"][:, 0:KC - 1], 0.0), writes=())
        for ct in range(64):
            wqb = f"wq{ct % 2}"
            qkb = f"qkv{ct % 2}"
            op("g", lambda e, wqb=wqb, ct=ct: e.dma_start(
                out=A[wqb][:].rearrange("p (t c) -> p t c", t=16),
                in_=dram["wqkv"][:, ct * 128:(ct + 1) * 128].rearrange("(t p) c -> p t c", p=128)),
               dma=True, writes=(wqb,))
            for tp in range(L // 512):
                bk = f"P{tp % 2}"
                for dtile in range(16):
                    op("t", lambda e, bk=bk, wqb=wqb, dtile=dtile, tp=tp: e.matmul(
                        A[bk][:], A[wqb][:, dtile * 128:(dtile + 1) * 128],
                        lnd3[:, dtile, tp * 512:(tp + 1) * 512],
                        start=(dtile == 0), stop=(dtile == 15)),
                       reads=(wqb, "lnd"), writes=(bk,))
                op("v", lambda e, bk=bk, qkb=qkb, tp=tp: e.tensor_copy(
                    out=A[qkb][:, KC - 1 + tp * 512: KC - 1 + (tp + 1) * 512], in_=A[bk][:]),
                   reads=(bk,), writes=(qkb,))
            op("v", lambda e, qkb=qkb, ct=ct: e.tensor_scalar(
                out=A["cacc"][:], in0=A[qkb][:, 0:L], scalar1=A["cw"][:, ct * KC:ct * KC + 1],
                scalar2=None, op0=ALU.mult), reads=(qkb, "cw"), writes=("cacc",))
            for i in range(1, KC):
                op("v", lambda e, qkb=qkb, ct=ct, i=i: e.scalar_tensor_tensor(
                    out=A["cacc"][:], in0=A[qkb][:, i:i + L], scalar=A["cw"][:, ct * KC + i:ct * KC + i + 1],
                    in1=A["cacc"][:], op0=ALU.mult, op1=ALU.add),
                   reads=(qkb, "cw", "cacc"), writes=("cacc",))
            op("s", lambda e: e.activation(out=A["mix"][:], in_=A["cacc"][:], func=AF.Silu),
               reads=("cacc",), writes=("mix",), bump=True)
            if ct < 32:
                dst = dram["q_sp"] if ct < 16 else dram["k_sp"]
                row = (ct % 16) * 128
                op("y", lambda e, dst=dst, row=row: e.dma_start(out=dst[row:row + 128, :], in_=A["mix"][:]),
                   dma=True, reads=("mix",))
                op("s", lambda e: e.activation(out=A["sqt"][:], in_=A["mix"][:], func=AF.Square),
                   reads=("mix",), writes=("sqt",))
                for j in range(NT):
                    op("t", lambda e, j=j: e.matmul(
                        A["P6"][:, j:j + 1], A["sqt"][:, j * 128:(j + 1) * 128], A["onesc_bf"][:, 0:1],
                        start=True, stop=True), reads=("sqt", "onesc_bf"), writes=("P6",) if j == 0 else ())
                op("t", lambda e: e.nop(), writes=("P6",))
                op("v", lambda e, ct=ct: e.tensor_copy(out=A["ssq"][:, ct * NT:(ct + 1) * NT],
                                                       in_=A["P6"][:, 0:NT]),
                   reads=("P6",), writes=("ssq",), bump=True)
            if ct >= 16:
                for g4 in range(NT // 4):
                    for j in range(4):
                        ts_ = g4 * 4 + j
                        op("t", lambda e, j=j, ts_=ts_: e.matmul(
                            A["P7"][:, j * 128:(j + 1) * 128], A["mix"][:, ts_ * 128:(ts_ + 1) * 128],
                            A["id_bf"][:], start=True, stop=True),
                           reads=("mix", "id_bf"), writes=("P7",))
                    op("v", lambda e, g4=g4: e.tensor_copy(
                        out=A["tstage"][:, g4 * 512:(g4 + 1) * 512], in_=A["P7"][:]),
                       reads=("P7",), writes=("tstage",), bump=True)
                if ct < 32:
                    op("y", lambda e, ct=ct: e.dma_start(
                        out=dram["kt_sp"][:, (ct - 16) * 128:(ct - 15) * 128].rearrange(
                            "(j p) w -> p j w", p=128),
                        in_=A["tstage"][:, 0:NT * 128].rearrange("p (j w) -> p j w", w=128)),
                       dma=True, reads=("tstage",))
                else:
                    op("y", lambda e, ct=ct: e.dma_start(
                        out=dram["v_sp"][:, (ct - 32) * 128:(ct - 31) * 128].rearrange(
                            "(j p) w -> p j w", p=128),
                        in_=A["tstage"][:, 0:NT * 128].rearrange("p (j w) -> p j w", w=128)),
                       dma=True, reads=("tstage",))

        if maxphase < 5:
            finwait()
            return
        p.barrier()
        nel = float(D * ECH)
        op("t", lambda e: e.matmul(A["P0"][0:1, 0:4], A["onesc_f"][:], A["wacc"][:, 0:4],
                                   start=True, stop=True), reads=("wacc", "onesc_f"), writes=("P0",))
        op("t", lambda e: e.matmul(A["P0"][0:1, 508:509], A["onesc_bf"][0:1, 0:1],
                           A["onesc_bf"][0:1, 0:1], start=True, stop=True),
           reads=("onesc_bf",), writes=("P0",))
        op("v", lambda e: e.tensor_scalar(out=A["scl"][0:1, 0:3], in0=A["P0"][0:1, 0:3],
                                          scalar1=1.0 / nel, scalar2=EPS_TER, op0=ALU.mult, op1=ALU.add),
           reads=("P0",), writes=("scl",), bump=True)
        op("v", lambda e: e.tensor_scalar(out=A["scl"][0:1, 3:6], in0=A["scl"][0:1, 0:3],
                                          scalar1=0.5, scalar2=None, op0=ALU.mult),
           reads=("scl",), writes=("scl",), bump=True)
        op("y", lambda e: e.dma_start(out=dram["scl_sp"][0:1, 0:6], in_=A["scl"][0:1, 0:6]),
           dma=True, reads=("scl",), writes=("dscl",))
        op("y", lambda e: e.dma_start(
            out=A["srep"][:, 0:3],
            in_=dram["scl_sp"][0:1, 3:6].partition_broadcast(128).rearrange("p o f -> p (o f)")),
           dma=True, reads=("dscl",), writes=("srep",))
        op("y", lambda e: e.dma_start(
            out=A["srep"][:, 3:6],
            in_=dram["scl_sp"][0:1, 3:6].partition_broadcast(128).rearrange("p o f -> p (o f)")),
           dma=True, reads=("dscl",), writes=("srep",))
        p.barrier()

        if maxphase < 6:
            finwait()
            return
        op("s", lambda e: e.activation(out=A["lnss"][:], in_=A["ssq"][:], func=AF.Ln, bias=A["epsl2"][:]),
           reads=("ssq", "epsl2"), writes=("lnss",), bump=True)
        op("v", lambda e: e.memset(A["S"][:], 0.0), writes=("S",))
        lnss3 = A["lnss"][:, :].rearrange("p (h t) -> p h t", h=32)
        GGe = A["GG"][:, 0:32].rearrange("p (h d) -> p h d", d=2)
        Gve = A["GG"][:, 32:64].rearrange("p (h d) -> p h d", d=2)

        def chunk_loads(c):
            b = c % 2
            for (nm, src, hh) in ((f"qb{b}", dram["q_sp"][:, c * 128:(c + 1) * 128].rearrange("(h p) t -> p h t", p=128), 16),
                              (f"kb{b}", dram["k_sp"][:, c * 128:(c + 1) * 128].rearrange("(h p) t -> p h t", p=128), 16),
                              (f"ktb{b}", dram["kt_sp"][c * 128:(c + 1) * 128, :], 0),
                              (f"vb{b}", dram["v_sp"][c * 128:(c + 1) * 128, :], 0),
                              (f"zb{b}", dram["sz_sp"][c * 128:(c + 1) * 128, :], 0)):
                o_ = A[nm][:] if hh == 0 else A[nm][:].rearrange("p (h t) -> p h t", h=hh)
                op("y", lambda e, o_=o_, src=src: e.dma_start(out=o_, in_=src),
                   dma=True, writes=(nm,))

        chunk_loads(0)
        for c in range(min(NCH, dlim[0])):
            b = c % 2
            qb, kb, ktb, vb, zb = (f"qb{b}", f"kb{b}", f"ktb{b}", f"vb{b}", f"zb{b}")
            if dlim[1] == 0 and dlim[2] < 1:
                continue
            for (i, srcn, dstn) in ((0, "gT", "g_tm"), (1, "lbT", "lb_tm"), (2, "beT", "be_tm")):
                op("t", lambda e, i=i, srcn=srcn, c=c: e.matmul(
                    A["P0"][:, i * 32:(i + 1) * 32], A[srcn][:, c * 128:(c + 1) * 128],
                    A["id_f"][0:32, 0:32], start=True, stop=True),
                   reads=(srcn, "id_f"), writes=("P0",) if i == 0 else ())
            op("t", lambda e: e.matmul(A["P0"][0:1, 508:509], A["onesc_bf"][0:1, 0:1],
                                       A["onesc_bf"][0:1, 0:1], start=True, stop=True),
               reads=("onesc_bf",), writes=("P0",))
            for (i, dstn) in ((0, "g_tm"), (1, "lb_tm"), (2, "be_tm")):
                op("v", lambda e, i=i, dstn=dstn: e.tensor_copy(out=A[dstn][:], in_=A["P0"][:, i * 32:(i + 1) * 32]),
                   reads=("P0",), writes=(dstn,), bump=True)
            if dlim[1] == 0 and dlim[2] < 2:
                continue
            op("t", lambda e: e.matmul(A["P1"][:, 0:32], A["triu_f"][:], A["g_tm"][:], start=True, stop=True),
               reads=("triu_f", "g_tm"), writes=("P1",))
            op("t", lambda e: e.matmul(A["P1"][0:1, 508:509], A["onesc_bf"][0:1, 0:1],
                                       A["onesc_bf"][0:1, 0:1], start=True, stop=True),
               reads=("onesc_bf",), writes=("P1",))
            op("v", lambda e: e.tensor_copy(out=A["GG"][:, 32:64], in_=A["P1"][:, 0:32]),
               reads=("P1",), writes=("GG",), bump=True)
            op("v", lambda e, c=c: e.scalar_tensor_tensor(
                out=GGe, in0=lnss3[:, 16:32, c:c + 1].broadcast_to([128, 16, 2]), scalar=0.5,
                in1=Gve, op0=ALU.mult, op1=ALU.add), reads=("lnss", "GG"), writes=("GG",), bump=True)
            op("v", lambda e, c=c: e.scalar_tensor_tensor(
                out=A["Gtil"][:].rearrange("p (h d) -> p h d", d=2),
                in0=lnss3[:, 16:32, c:c + 1].broadcast_to([128, 16, 2]), scalar=-0.5,
                in1=Gve, op0=ALU.mult, op1=ALU.add), reads=("lnss", "GG"), writes=("Gtil",))
            op("v", lambda e: e.tensor_tensor(out=A["Gtil"][:], in0=A["Gtil"][:], in1=A["lb_tm"][:],
                                              op=ALU.add), reads=("Gtil", "lb_tm"), writes=("Gtil",),
               bump=True)
            op("v", lambda e, c=c: e.scalar_tensor_tensor(
                out=A["Ghat"][:].rearrange("p (h d) -> p h d", d=2),
                in0=lnss3[:, 0:16, c:c + 1].broadcast_to([128, 16, 2]), scalar=-0.5,
                in1=Gve, op0=ALU.mult, op1=ALU.add), reads=("lnss", "GG"), writes=("Ghat",), bump=True)
            if dlim[1] == 0 and dlim[2] < 3:
                continue
            op("t", lambda e: e.transpose(A["P2"][0:64, 0:128], A["GG"][:], A["id_f"][:]),
               reads=("GG", "id_f"), writes=("P2",))
            op("t", lambda e: e.matmul(A["P2"][0:1, 508:509], A["onesc_bf"][0:1, 0:1],
                                       A["onesc_bf"][0:1, 0:1], start=True, stop=True),
               reads=("onesc_bf",), writes=("P2",))
            if dlim[1] == 0 and dlim[2] < 30:
                continue
            op("v", lambda e: e.tensor_copy(out=A["GGT"][:], in_=A["P2"][0:64, 0:128]),
               reads=("P2",), writes=("GGT",), bump=True)
            if dlim[1] == 0 and dlim[2] < 31:
                continue
            op("y", lambda e, c=c: e.dma_start(out=dram["gg_sp"][c % 2:c % 2 + 1].rearrange("b p f -> (b p) f"),
                                                  in_=A["GGT"][:]),
               dma=True, reads=("GGT",), writes=(f"gg{c % 2}",))
            if dlim[1] == 0 and dlim[2] < 32:
                continue
            op("y", lambda e, c=c: e.dma_start(
                out=dram["gc_sp"][c % 2:c % 2 + 1].rearrange("b (f o) -> (b f) o", o=1),
                in_=A["GGT"][32:64, 127:128]), dma=True, reads=("GGT",), writes=(f"ggc{c % 2}",))
            if dlim[1] == 0 and dlim[2] < 4:
                continue
            op("y", lambda e, c=c: e.dma_start(
                out=A["grw"][:],
                in_=dram["gg_sp"][c % 2:c % 2 + 1].rearrange("b p f -> b (p f)")[0:1, 0:4096]
                    .partition_broadcast(128).rearrange("p o f -> p (o f)")),
               dma=True, reads=(f"gg{c % 2}",), writes=("grw",))
            op("y", lambda e, c=c: e.dma_start(
                out=A["gcrep"][:],
                in_=dram["gc_sp"][c % 2:c % 2 + 1].partition_broadcast(128).rearrange("p o f -> p (o f)")),
               dma=True, reads=(f"ggc{c % 2}",), writes=("gcrep",))
            if dlim[1] == 0 and dlim[2] < 5:
                continue
            op("v", lambda e: e.tensor_sub(out=A["csub"][:], in0=A["gcrep"][:], in1=A["GG"][:, 0:32]),
               reads=("gcrep", "GG"), writes=("csub",), bump=True)
            op("s", lambda e: e.activation(out=A["e1"][:], in_=A["Gtil"][:], func=AF.Exp),
               reads=("Gtil",), writes=("e1",), bump=True)
            op("s", lambda e: e.activation(out=A["e2"][:], in_=A["Ghat"][:], func=AF.Exp,
                                           bias=A["logcc"][:]),
               reads=("Ghat", "logcc"), writes=("e2",), bump=True)
            op("s", lambda e: e.activation(out=A["cS"][:], in_=A["csub"][:], func=AF.Exp),
               reads=("csub",), writes=("cS",), bump=True)
            op("s", lambda e: e.activation(out=A["egc"][:], in_=A["gcrep"][:], func=AF.Exp),
               reads=("gcrep",), writes=("egc",), bump=True)
            op("v", lambda e: e.tensor_copy(out=A["S0b"][:], in_=A["S"][:]), reads=("S",), writes=("S0b",))
            if dbg and c == 0:
                op("y", lambda e: e.dma_start(out=dram["dbg0"][:, 0:64], in_=A["GG"][:]),
                   dma=True, reads=("GG",), writes=("ddbg0",))
                op("y", lambda e: e.dma_start(out=dram["dbg1"][:], in_=A["grw"][:, 0:512]),
                   dma=True, reads=("grw",), writes=("ddbg1",))
                op("v", lambda e: e.tensor_copy(out=A["GG"][0:32, 0:1], in_=A["negea"][:]),
                   reads=("negea",), writes=("GGdbg",), bump=True)
                op("y", lambda e: e.dma_start(out=dram["dbg0"][0:32, 384:416],
                                              in_=A["gT"][:, 0:32]),
                   dma=True, reads=("gT",), writes=("ddbg0",))
                op("y", lambda e: e.dma_start(out=dram["dbg0"][0:32, 416:417],
                                              in_=A["GG"][0:32, 0:1],
                                              allow_slow_non_contiguous=True),
                   dma=True, reads=("GGdbg",), writes=("ddbg0",))
                for nm_, off_ in (("Gtil", 64), ("Ghat", 96), ("e1", 128), ("e2", 160),
                                  ("cS", 192), ("egc", 224), ("gcrep", 256), ("g_tm", 288),
                                  ("lb_tm", 320), ("be_tm", 352)):
                    op("y", lambda e, nm_=nm_, off_=off_: e.dma_start(
                        out=dram["dbg0"][:, off_:off_ + 32], in_=A[nm_][:]),
                       dma=True, reads=(nm_,), writes=("ddbg0",))
            if c + 1 < NCH:
                chunk_loads(c + 1)
            NGRP = min(HV // 4, dlim[1])
            bb4 = lambda t_: t_.broadcast_to([128, 4, 128])

            def gctx(gi):
                pi = gi % 2
                return (str(pi), gi * 4, (gi * 4) // 2,
                        f"P{4 * pi + 0}", f"P{4 * pi + 1}", f"P{4 * pi + 2}", f"P{4 * pi + 3}")

            def F_stage(gi, part):
                sfx, v0, kq0, b0, b1, b2, b3 = gctx(gi)
                h3 = lambda nm: A[nm + sfx][:].rearrange("p (h s) -> p h s", h=4)
                if part == 0:
                    for kk in range(2):
                        op("t", lambda e, kk=kk: e.matmul(
                            A[b0][:, kk * 128:(kk + 1) * 128],
                            A[kb][:, (kq0 + kk) * 128:(kq0 + kk + 1) * 128],
                            A[kb][:, (kq0 + kk) * 128:(kq0 + kk + 1) * 128], start=True, stop=True),
                           reads=(kb,), writes=(b0,) if kk == 0 else ())
                        op("t", lambda e, kk=kk: e.matmul(
                            A[b0][:, 256 + kk * 128:256 + (kk + 1) * 128],
                            A[qb][:, (kq0 + kk) * 128:(kq0 + kk + 1) * 128],
                            A[kb][:, (kq0 + kk) * 128:(kq0 + kk + 1) * 128], start=True, stop=True),
                           reads=(qb, kb))
                    op("t", lambda e: e.nop(), writes=(b0,))
                elif part == 1:
                    op("v", lambda e: e.scalar_tensor_tensor(
                        out=h3("Din"), in0=A["grw"][:, v0 * 128:(v0 + 4) * 128].rearrange(
                            "p (h s) -> p h s", h=4),
                        scalar=-1.0, in1=bb4(A["Gtil"][:, v0:v0 + 4]), op0=ALU.mult, op1=ALU.add),
                       reads=("grw", "Gtil"), writes=("Din" + sfx,))
                    op("v", lambda e: e.tensor_tensor(out=A["Din" + sfx][:], in0=A["Din" + sfx][:],
                                                      in1=A["maskA"][:], op=ALU.add),
                       reads=("Din" + sfx, "maskA"), writes=("Din" + sfx,))
                    op("v", lambda e: e.scalar_tensor_tensor(
                        out=h3("Din2"), in0=A["grw"][:, v0 * 128:(v0 + 4) * 128].rearrange(
                            "p (h s) -> p h s", h=4),
                        scalar=-1.0, in1=bb4(A["Ghat"][:, v0:v0 + 4]), op0=ALU.mult, op1=ALU.add),
                       reads=("grw", "Ghat"), writes=("Din2" + sfx,))
                    op("v", lambda e: e.tensor_tensor(out=A["Din2" + sfx][:], in0=A["Din2" + sfx][:],
                                                      in1=A["maskM"][:], op=ALU.add),
                       reads=("Din2" + sfx, "maskM"), writes=("Din2" + sfx,))
                    op("s", lambda e: e.activation(out=A["EA" + sfx][:], in_=A["Din" + sfx][:], func=AF.Exp),
                       reads=("Din" + sfx,), writes=("EA" + sfx,))
                    op("s", lambda e: e.activation(out=A["EM" + sfx][:], in_=A["Din2" + sfx][:], func=AF.Exp,
                                                   bias=A["logcc"][:]),
                       reads=("Din2" + sfx, "logcc"), writes=("EM" + sfx,))
                    for kk in range(2):
                        kkx = A[b0][:, kk * 128:(kk + 1) * 128].rearrange(
                            "p (o s) -> p o s", o=1).broadcast_to([128, 2, 128])
                        qkx = A[b0][:, 256 + kk * 128:256 + (kk + 1) * 128].rearrange(
                            "p (o s) -> p o s", o=1).broadcast_to([128, 2, 128])
                        op("v", lambda e, kkx=kkx, kk=kk: e.scalar_tensor_tensor(
                            out=A["B4" + sfx][:, kk * 256:(kk + 1) * 256].rearrange("p (d s) -> p d s", d=2),
                            in0=kkx, scalar=-1.0,
                            in1=A["EA" + sfx][:, kk * 256:(kk + 1) * 256].rearrange("p (d s) -> p d s", d=2),
                            op0=ALU.mult, op1=ALU.mult),
                           reads=(b0, "EA" + sfx), writes=("B4" + sfx,))
                        op("v", lambda e, qkx=qkx, kk=kk: e.scalar_tensor_tensor(
                            out=A["M4" + sfx][:, kk * 256:(kk + 1) * 256].rearrange("p (d s) -> p d s", d=2),
                            in0=qkx, scalar=1.0,
                            in1=A["EM" + sfx][:, kk * 256:(kk + 1) * 256].rearrange("p (d s) -> p d s", d=2),
                            op0=ALU.mult, op1=ALU.mult),
                           reads=(b0, "EM" + sfx), writes=("M4" + sfx,))
                elif part == 2:
                    for j in range(4):
                        op("t", lambda e, j=j: e.matmul(A[b1][:, j * 128:(j + 1) * 128],
                                                        A["B4" + sfx][:, j * 128:(j + 1) * 128],
                                                        A["id_bf"][:], start=True, stop=True),
                           reads=("B4" + sfx, "id_bf"), writes=(b1,) if j == 0 else ())
                    for j in range(4):
                        op("t", lambda e, j=j: e.matmul(A[b2][:, j * 128:(j + 1) * 128],
                                                        A["M4" + sfx][:, j * 128:(j + 1) * 128],
                                                        A["id_bf"][:], start=True, stop=True),
                           reads=("M4" + sfx, "id_bf"), writes=(b2,) if j == 0 else ())
                    for j in range(4):
                        op("t", lambda e, j=j: e.matmul(
                            A[b3][:, j * 128:(j + 1) * 128],
                            A[kb][:, (kq0 + j // 2) * 128:(kq0 + j // 2 + 1) * 128],
                            A["S0b"][:, (v0 + j) * 128:(v0 + j + 1) * 128], start=True, stop=True),
                           reads=(kb, "S0b"), writes=(b3,) if j == 0 else ())
                    for j in range(4):
                        op("t", lambda e, j=j: e.matmul(
                            A[b0][:, j * 128:(j + 1) * 128],
                            A[qb][:, (kq0 + j // 2) * 128:(kq0 + j // 2 + 1) * 128],
                            A["S0b"][:, (v0 + j) * 128:(v0 + j + 1) * 128], start=True, stop=True),
                           reads=(qb, "S0b"), writes=(b0,) if j == 0 else ())
                    op("t", lambda e: e.nop(), writes=(b0, b1, b2, b3))
                elif part == 3:
                    op("v", lambda e: e.tensor_copy(out=A["BT4" + sfx][:], in_=A[b1][:]),
                       reads=(b1,), writes=("BT4" + sfx,))
                    op("v", lambda e: e.tensor_copy(out=A["MT4" + sfx][:], in_=A[b2][:]),
                       reads=(b2,), writes=("MT4" + sfx,))
                    op("v", lambda e: e.scalar_tensor_tensor(
                        out=h3("ot4"), in0=A[b0][:].rearrange("p (h s) -> p h s", h=4), scalar=1.0,
                        in1=bb4(A["e2"][:, v0:v0 + 4]), op0=ALU.mult, op1=ALU.mult),
                       reads=(b0, "e2"), writes=("ot4" + sfx,))
                    op("v", lambda e: e.scalar_tensor_tensor(
                        out=h3("t2f"), in0=A[b3][:].rearrange("p (h s) -> p h s", h=4), scalar=1.0,
                        in1=bb4(A["e1"][:, v0:v0 + 4]), op0=ALU.mult, op1=ALU.mult),
                       reads=(b3, "e1"), writes=("t2f" + sfx,))
                    op("v", lambda e: e.scalar_tensor_tensor(
                        out=h3("t3f"), in0=A[vb][:, v0 * 128:(v0 + 4) * 128].rearrange(
                            "p (h s) -> p h s", h=4),
                        scalar=1.0, in1=bb4(A["be_tm"][:, v0:v0 + 4]), op0=ALU.mult, op1=ALU.mult),
                       reads=(vb, "be_tm"), writes=("t3f" + sfx,))
                    op("v", lambda e: e.tensor_sub(out=A["Yb" + sfx][:], in0=A["t3f" + sfx][:],
                                                   in1=A["t2f" + sfx][:]),
                       reads=("t3f" + sfx, "t2f" + sfx), writes=("Yb" + sfx,))

            def S_level(gi, j):
                sfx, v0, kq0, b0, b1, b2, b3 = gctx(gi)
                zcur = ("BT4" + sfx) if j == 0 else ("Zb" + sfx)
                pcur = ("B4" + sfx) if j == 0 else ("Pb" + sfx)
                for jj in range(4):
                    op("t", lambda e, jj=jj, zcur=zcur: e.matmul(
                        A[b3][:, jj * 128:(jj + 1) * 128], A[zcur][:, jj * 128:(jj + 1) * 128],
                        A["Yb" + sfx][:, jj * 128:(jj + 1) * 128], start=True, stop=True),
                       reads=(zcur, "Yb" + sfx), writes=(b3,) if jj == 0 else ())
                if j < 6:
                    for jj in range(4):
                        op("t", lambda e, jj=jj, zcur=zcur, pcur=pcur: e.matmul(
                            A[b0][:, jj * 128:(jj + 1) * 128], A[pcur][:, jj * 128:(jj + 1) * 128],
                            A[zcur][:, jj * 128:(jj + 1) * 128], start=True, stop=True),
                           reads=(pcur, zcur), writes=(b0,) if jj == 0 else ())
                    for jj in range(4):
                        op("t", lambda e, jj=jj, zcur=zcur, pcur=pcur: e.matmul(
                            A[b1][:, jj * 128:(jj + 1) * 128], A[zcur][:, jj * 128:(jj + 1) * 128],
                            A[pcur][:, jj * 128:(jj + 1) * 128], start=True, stop=True),
                           reads=(zcur, pcur), writes=(b1,) if jj == 0 else ())
                op("t", lambda e: e.nop(), writes=(b3, b0, b1) if j < 6 else (b3,))
                op("v", lambda e: e.tensor_add(out=A["Yb" + sfx][:], in0=A[b3][:], in1=A["Yb" + sfx][:]),
                   reads=(b3, "Yb" + sfx), writes=("Yb" + sfx,))
                if j < 6:
                    op("s", lambda e: e.activation(out=A["Zb" + sfx][:], in_=A[b0][:], func=AF.Copy),
                       reads=(b0,), writes=("Zb" + sfx,))
                    op("s", lambda e: e.activation(out=A["Pb" + sfx][:], in_=A[b1][:], func=AF.Copy),
                       reads=(b1,), writes=("Pb" + sfx,))

            def B_stage(gi, part):
                sfx, v0, kq0, b0, b1, b2, b3 = gctx(gi)
                h3 = lambda nm: A[nm + sfx][:].rearrange("p (h s) -> p h s", h=4)
                if part == 0:
                    op("v", lambda e: e.tensor_copy(out=A["Ub" + sfx][:], in_=A["Yb" + sfx][:]),
                       reads=("Yb" + sfx,), writes=("Ub" + sfx,))
                    for jj in range(4):
                        op("t", lambda e, jj=jj: e.matmul(
                            A[b2][:, jj * 128:(jj + 1) * 128], A["MT4" + sfx][:, jj * 128:(jj + 1) * 128],
                            A["Ub" + sfx][:, jj * 128:(jj + 1) * 128], start=True, stop=True),
                           reads=("MT4" + sfx, "Ub" + sfx), writes=(b2,) if jj == 0 else ())
                    op("t", lambda e: e.nop(), writes=(b2,))
                    op("v", lambda e: e.tensor_add(out=A["o4" + sfx][:], in0=A[b2][:], in1=A["ot4" + sfx][:]),
                       reads=(b2, "ot4" + sfx), writes=("o4" + sfx,))
                    op("v", lambda e: e.tensor_mul(out=A["yg" + sfx][:], in0=A["o4" + sfx][:],
                                                   in1=A[zb][:, v0 * 128:(v0 + 4) * 128]),
                       reads=("o4" + sfx, zb), writes=("yg" + sfx,))
                    op("v", lambda e: e.tensor_mul(out=A["ygs" + sfx][:], in0=A["yg" + sfx][:],
                                                   in1=A["yg" + sfx][:]),
                       reads=("yg" + sfx,), writes=("ygs" + sfx,))
                    op("v", lambda e: e.tensor_reduce(
                        out=A["msq"][:, v0:v0 + 4], in_=A["ygs" + sfx][:].rearrange("p (h s) -> p h s", h=4),
                        axis=AX.X, op=ALU.add), reads=("ygs" + sfx,), writes=("msq",), bump=True)
                    op("s", lambda e: e.activation(out=A["rr"][:, v0:v0 + 4], in_=A["msq"][:, v0:v0 + 4],
                                                   func=AF.Ln, bias=A["epsrms"][:], scale=1.0 / DV),
                       reads=("msq", "epsrms"), writes=("rr",))
                    op("s", lambda e: e.activation(out=A["rr"][:, v0:v0 + 4], in_=A["rr"][:, v0:v0 + 4],
                                                   func=AF.Exp, scale=-0.5), reads=("rr",), writes=("rr",),
                       bump=True)
                elif part == 1:
                    op("v", lambda e: e.scalar_tensor_tensor(
                        out=h3("y4"), in0=A["yg" + sfx][:].rearrange("p (h s) -> p h s", h=4), scalar=1.0,
                        in1=bb4(A["rr"][:, v0:v0 + 4]), op0=ALU.mult, op1=ALU.mult),
                       reads=("yg" + sfx, "rr"), writes=("y4" + sfx,))
                    for jj in range(4):
                        op("t", lambda e, jj=jj: e.matmul(
                            A[b1][:, jj * 128:(jj + 1) * 128], A["y4" + sfx][:, jj * 128:(jj + 1) * 128],
                            A["id_bf"][:], start=True, stop=True),
                           reads=("y4" + sfx, "id_bf"), writes=(b1,) if jj == 0 else ())
                    op("t", lambda e: e.nop(), writes=(b1,))
                    op("v", lambda e: e.tensor_copy(out=A["ystg"][:, v0 * 128:(v0 + 4) * 128], in_=A[b1][:]),
                       reads=(b1,), writes=("ystg",), bump=True)
                    op("v", lambda e: e.scalar_tensor_tensor(
                        out=A["Ut" + sfx][:].rearrange("p (h s) -> p h s", h=4),
                        in0=A["Ub" + sfx][:].rearrange("p (h s) -> p h s", h=4), scalar=1.0,
                        in1=bb4(A["cS"][:, v0:v0 + 4]), op0=ALU.mult, op1=ALU.mult),
                       reads=("Ub" + sfx, "cS"), writes=("Ut" + sfx,))
                elif part == 2:
                    for jj in range(4):
                        op("t", lambda e, jj=jj: e.matmul(
                            A[b0][:, jj * 128:(jj + 1) * 128],
                            A[ktb][:, (kq0 + jj // 2) * 128:(kq0 + jj // 2 + 1) * 128],
                            A["Ut" + sfx][:, jj * 128:(jj + 1) * 128], start=True, stop=True),
                           reads=(ktb, "Ut" + sfx), writes=(b0,) if jj == 0 else ())
                    op("t", lambda e: e.nop(), writes=(b0,))
                    op("v", lambda e: e.scalar_tensor_tensor(
                        out=A["S"][:, v0 * 128:(v0 + 4) * 128].rearrange("p (h s) -> p h s", h=4),
                        in0=A["S"][:, v0 * 128:(v0 + 4) * 128].rearrange("p (h s) -> p h s", h=4),
                        scalar=1.0, in1=bb4(A["egc"][:, v0:v0 + 4]), op0=ALU.mult, op1=ALU.mult),
                       reads=("S", "egc"), writes=())
                    op("v", lambda e: e.tensor_add(out=A["S"][:, v0 * 128:(v0 + 4) * 128],
                                                   in0=A["S"][:, v0 * 128:(v0 + 4) * 128], in1=A[b0][:]),
                       reads=(b0, "S"), writes=("S",))

            for st in range(NGRP + 1):
                if st < NGRP:
                    F_stage(st, 0)
                if st >= 1:
                    S_level(st - 1, 0)
                if st < NGRP:
                    F_stage(st, 1)
                if st >= 1:
                    S_level(st - 1, 1)
                if st < NGRP:
                    F_stage(st, 2)
                if st >= 1:
                    S_level(st - 1, 2)
                if st < NGRP:
                    F_stage(st, 3)
                if st >= 1:
                    for j_ in range(3, 7):
                        S_level(st - 1, j_)
                    B_stage(st - 1, 0)
                    B_stage(st - 1, 1)
                    B_stage(st - 1, 2)
            op("y", lambda e, c=c: e.dma_start(
                out=dram["y_sp"][:, c * 128:(c + 1) * 128].rearrange("(h p) t -> p h t", p=128),
                in_=A["ystg"][:].rearrange("p (h t) -> p h t", t=128)), dma=True, reads=("ystg",))
        p.barrier()

        if maxphase < 7:
            finwait()
            return
        for dq in range(4):
            op("g", lambda e, dq=dq: e.dma_start(
                out=A["wo_sb"][:].rearrange("p (t c) -> p t c", t=32),
                in_=dram["wout"][:, dq * 512:(dq + 1) * 512].rearrange(
                    "(t p) c -> p t c", p=128)), dma=True, writes=("wo_sb",))
            op("v", lambda e: e.tensor_scalar(out=A["wo_sb"][:], in0=A["wo_sb"][:], scalar1=A["nw"][:],
                                              scalar2=None, op0=ALU.mult),
               reads=("wo_sb", "nw"), writes=("wo_sb",))
            for tt in range(NT):
                yb_ = f"yt{tt % 2}"
                op("y", lambda e, yb_=yb_, tt=tt: e.dma_start(
                    out=A[yb_][:].rearrange("p (h t) -> p h t", t=128),
                    in_=dram["y_sp"][:, tt * 128:(tt + 1) * 128].rearrange(
                        "(h p) t -> p h t", p=128)), dma=True, writes=(yb_,))
                bk = f"P{tt % 2}"
                for hv in range(HV):
                    op("t", lambda e, bk=bk, hv=hv, yb_=yb_: e.matmul(
                        A[bk][:], A[yb_][:, hv * 128:(hv + 1) * 128],
                        A["wo_sb"][:, hv * 512:(hv + 1) * 512], start=(hv == 0), stop=(hv == HV - 1)),
                       reads=(yb_, "wo_sb"), writes=(bk,) if hv == 0 else ())
                op("t", lambda e: e.nop(), writes=(bk,))
                if dq < 3:
                    op("v", lambda e, bk=bk, dq=dq: e.tensor_copy(out=A["mstage"][:, 0:512], in_=A[bk][:]),
                       reads=(bk,), writes=("mslot",), bump=True)
                    op("y", lambda e, tt=tt, dq=dq: e.dma_start(
                        out=dram["m0_sp"][tt * 128:(tt + 1) * 128, dq * 512:(dq + 1) * 512],
                        in_=A["mstage"][:, 0:512]), dma=True, reads=("mslot",), writes=("dm0sp",))
                else:
                    op("y", lambda e, tt=tt: e.dma_start(
                        out=A["mstage"][:, 0:1536], in_=dram["m0_sp"][tt * 128:(tt + 1) * 128, :]),
                       dma=True, reads=("dm0sp",), writes=("mstage",))
                    op("v", lambda e, bk=bk: e.tensor_copy(out=A["mstage"][:, 1536:2048], in_=A[bk][:]),
                       reads=(bk,), writes=("mstage",))
                    ln_full("mstage", "lnst4", "lnt4", None)
                    op("y", lambda e, tt=tt: e.dma_start(out=A["xa4"][:],
                                                         in_=dram["x"][tt * 128:(tt + 1) * 128, :]),
                       dma=True, writes=("xa4",))
                    op("v", lambda e: e.scalar_tensor_tensor(
                        out=A["x1t"][:], in0=A["mstage"][:], scalar=A["mcol"][:],
                        in1=A["rstd"][:].broadcast_to([128, D]), op0=ALU.subtract, op1=ALU.mult),
                       reads=("mstage", "mcol", "rstd"), writes=("x1t",))
                    op("v", lambda e: e.tensor_add(out=A["x1t"][:], in0=A["x1t"][:], in1=A["xa4"][:]),
                       reads=("x1t", "xa4"), writes=("x1t",), bump=True)
                    op("y", lambda e, tt=tt: e.dma_start(out=dram["x1_sp"][tt * 128:(tt + 1) * 128, :],
                                                         in_=A["x1t"][:]), dma=True, reads=("x1t",))
                    ln_full("x1t", "lnst4", "lnt4", None)
                    transpose_to_lnd("lnt4", tt, 2)
        p.barrier()

        if maxphase < 8:
            finwait()
            return
        for et in range(64):
            bi = et % 2
            wub, wgb = f"wu{bi}", f"wg{bi}"
            op("y", lambda e, wub=wub, et=et: e.dma_start(
                out=A[wub][:].rearrange("p (t c) -> p t c", t=16),
                in_=dram["wupT"][:, et * 128:(et + 1) * 128].rearrange(
                    "(t p) c -> p t c", p=128)), dma=True, writes=(wub,))
            op("y", lambda e, wgb=wgb, et=et: e.dma_start(
                out=A[wgb][:].rearrange("p (t c) -> p t c", t=16),
                in_=dram["wgT"][:, et * 128:(et + 1) * 128].rearrange(
                    "(t p) c -> p t c", p=128)), dma=True, writes=(wgb,))
            sg1, sg2 = f"sgn1{bi}", f"sgn2{bi}"
            for (wb_, tq, sc) in ((wub, f"tqu{bi}", 3), (wgb, f"tqg{bi}", 4)):
                op("s", lambda e, wb_=wb_, sc=sc, sg1=sg1: e.activation(out=A[sg1][:], in_=A[wb_][:], func=AF.Sign,
                                                               bias=A["srep"][:, sc:sc + 1], scale=-1.0),
                   reads=(wb_, "srep"), writes=(sg1,))
                op("s", lambda e, wb_=wb_, sc=sc, sg2=sg2: e.activation(out=A[sg2][:], in_=A[wb_][:], func=AF.Sign,
                                                               bias=A["srep"][:, sc:sc + 1], scale=1.0),
                   reads=(wb_, "srep"), writes=(sg2,))
                op("v", lambda e, tq=tq, sg1=sg1, sg2=sg2: e.tensor_sub(out=A[tq][:], in0=A[sg2][:], in1=A[sg1][:]),
                   reads=(sg1, sg2), writes=(tq,))
            for tp in range(L // 512):
                bk = f"P{tp % 2}"
                for dtile in range(16):
                    op("t", lambda e, bk=bk, dtile=dtile, tp=tp: e.matmul(
                        A[bk][:], A[f"tqu{bi}"][:, dtile * 128:(dtile + 1) * 128],
                        lnd3[:, dtile, tp * 512:(tp + 1) * 512], start=(dtile == 0), stop=(dtile == 15)),
                       reads=(f"tqu{bi}", "lnd"), writes=(bk,) if dtile == 0 else ())
                op("t", lambda e: e.nop(), writes=(bk,))
                op("s", lambda e, bk=bk, tp=tp: e.activation(
                    out=A["hst"][:, tp * 512:(tp + 1) * 512], in_=A[bk][:], func=AF.Silu,
                    scale=A["srep"][:, 0:1]), reads=(bk, "srep"), writes=("hst",))
            for tp in range(L // 512):
                bk = f"P{2 + tp % 2}"
                for dtile in range(16):
                    op("t", lambda e, bk=bk, dtile=dtile, tp=tp: e.matmul(
                        A[bk][:], A[f"tqg{bi}"][:, dtile * 128:(dtile + 1) * 128],
                        lnd3[:, dtile, tp * 512:(tp + 1) * 512], start=(dtile == 0), stop=(dtile == 15)),
                       reads=(f"tqg{bi}", "lnd"), writes=(bk,) if dtile == 0 else ())
                op("t", lambda e: e.nop(), writes=(bk,))
                op("v", lambda e, bk=bk: e.tensor_scalar(out=A["gsc"][:], in0=A[bk][:],
                                                         scalar1=A["srep"][:, 1:2], scalar2=None,
                                                         op0=ALU.mult), reads=(bk, "srep"), writes=("gsc",))
                op("v", lambda e, tp=tp: e.tensor_mul(out=A["hst"][:, tp * 512:(tp + 1) * 512],
                                                      in0=A["hst"][:, tp * 512:(tp + 1) * 512],
                                                      in1=A["gsc"][:]),
                   reads=("hst", "gsc"), writes=("hst",), bump=True)
            op("y", lambda e, et=et: e.dma_start(out=dram["h_sp"][et * 128:(et + 1) * 128, :],
                                                 in_=A["hst"][:]), dma=True, reads=("hst",))
        p.barrier()

        if maxphase < 9:
            finwait()
            return
        for dq in range(4):
            for et in range(64):
                bi = et % 2
                wdb = f"wd{bi}"
                op("y", lambda e, wdb=wdb, et=et, dq=dq: e.dma_start(
                    out=A[wdb][:], in_=dram["wdT"][et * 128:(et + 1) * 128, dq * 512:(dq + 1) * 512]),
                   dma=True, writes=(wdb,))
                op("s", lambda e, wdb=wdb: e.activation(out=A["sg6a"][:], in_=A[wdb][:], func=AF.Sign,
                                                        bias=A["srep"][:, 5:6], scale=-1.0),
                   reads=(wdb, "srep"), writes=("sg6a",))
                op("s", lambda e, wdb=wdb: e.activation(out=A["sg6b"][:], in_=A[wdb][:], func=AF.Sign,
                                                        bias=A["srep"][:, 5:6], scale=1.0),
                   reads=(wdb, "srep"), writes=("sg6b",))
                op("v", lambda e, et=et: e.tensor_sub(out=A["td"][:, et * 512:(et + 1) * 512],
                                                      in0=A["sg6b"][:], in1=A["sg6a"][:]),
                   reads=("sg6a", "sg6b"), writes=("td",))
            for tt in range(NT):
                bi = tt % 2
                hb_ = f"hb{bi}"
                op("y", lambda e, hb_=hb_, tt=tt: e.dma_start(
                    out=A[hb_][:].rearrange("p (h t) -> p h t", t=128),
                    in_=dram["h_sp"][:, tt * 128:(tt + 1) * 128].rearrange(
                        "(h p) t -> p h t", p=128)), dma=True, writes=(hb_,))
                bk = f"P{tt % 2}"
                for et in range(64):
                    op("t", lambda e, bk=bk, et=et, hb_=hb_: e.matmul(
                        A[bk][:], A[hb_][:, et * 128:(et + 1) * 128],
                        A["td"][:, et * 512:(et + 1) * 512], start=(et == 0), stop=(et == 63)),
                       reads=(hb_, "td"), writes=(bk,) if et == 0 else ())
                op("t", lambda e: e.nop(), writes=(bk,))
                if dq < 3:
                    op("v", lambda e, bk=bk: e.tensor_scalar(out=A["fst"][:, 0:512], in0=A[bk][:],
                                                             scalar1=A["srep"][:, 2:3], scalar2=None,
                                                             op0=ALU.mult), reads=(bk, "srep"),
                       writes=("fslot",), bump=True)
                    op("y", lambda e, tt=tt, dq=dq: e.dma_start(
                        out=dram["f0_sp"][tt * 128:(tt + 1) * 128, dq * 512:(dq + 1) * 512],
                        in_=A["fst"][:, 0:512]), dma=True, reads=("fslot",), writes=("df0sp",))
                else:
                    op("y", lambda e, tt=tt: e.dma_start(
                        out=A["fst"][:, 0:1536], in_=dram["f0_sp"][tt * 128:(tt + 1) * 128, :]),
                       dma=True, reads=("df0sp",), writes=("fst",))
                    op("v", lambda e, bk=bk: e.tensor_scalar(out=A["fst"][:, 1536:2048], in0=A[bk][:],
                                                             scalar1=A["srep"][:, 2:3], scalar2=None,
                                                             op0=ALU.mult), reads=(bk, "srep"),
                       writes=("fst",))
                    op("v", lambda e: e.tensor_reduce(out=A["sumx"][:], in_=A["fst"][:], axis=AX.X,
                                                      op=ALU.add), reads=("fst",), writes=("sumx",))
                    op("v", lambda e: e.scalar_tensor_tensor(out=A["lnst6"][:], in0=A["fst"][:], scalar=1.0,
                                                             in1=A["fst"][:], op0=ALU.mult, op1=ALU.mult,
                                                             accum_out=A["sumq"][:]),
                       reads=("fst",), writes=("lnst6", "sumq"))
                    op("v", lambda e: e.tensor_scalar(out=A["mcol"][:], in0=A["sumx"][:], scalar1=1.0 / D,
                                                      scalar2=None, op0=ALU.mult),
                       reads=("sumx",), writes=("mcol",))
                    op("v", lambda e: e.tensor_tensor(out=A["vcol"][:], in0=A["mcol"][:], in1=A["mcol"][:],
                                                      op=ALU.mult), reads=("mcol",), writes=("vcol",))
                    op("v", lambda e: e.scalar_tensor_tensor(out=A["vcol"][:], in0=A["sumq"][:],
                                                             scalar=1.0 / D, in1=A["vcol"][:],
                                                             op0=ALU.mult, op1=ALU.subtract),
                       reads=("sumq", "vcol"), writes=("vcol",), bump=True)
                    op("s", lambda e: e.activation(out=A["rstd"][:], in_=A["vcol"][:], func=AF.Ln,
                                                   bias=A["epsln"][:]), reads=("vcol", "epsln"),
                       writes=("rstd",))
                    op("s", lambda e: e.activation(out=A["rstd"][:], in_=A["rstd"][:], func=AF.Exp,
                                                   scale=-0.5), reads=("rstd",), writes=("rstd",),
                       bump=True)
                    op("y", lambda e, tt=tt: e.dma_start(out=A["xa6"][:],
                                                         in_=dram["x1_sp"][tt * 128:(tt + 1) * 128, :]),
                       dma=True, writes=("xa6",))
                    op("v", lambda e: e.scalar_tensor_tensor(
                        out=A["x6t"][:], in0=A["fst"][:], scalar=A["mcol"][:],
                        in1=A["rstd"][:].broadcast_to([128, D]), op0=ALU.subtract, op1=ALU.mult),
                       reads=("fst", "mcol", "rstd"), writes=("x6t",))
                    op("v", lambda e: e.tensor_add(out=A["x6t"][:], in0=A["x6t"][:], in1=A["xa6"][:]), bump=True,
                       reads=("x6t", "xa6"), writes=("x6t",))
                    op("y", lambda e, tt=tt: e.dma_start(out=dram["out"][tt * 128:(tt + 1) * 128, :],
                                                         in_=A["x6t"][:]), dma=True, reads=("x6t",))
        finwait()

    with nc.Block() as blk:
        @blk.tensor
        def _(e):
            program("t", e)

        @blk.vector
        def _(e):
            program("v", e)

        @blk.scalar
        def _(e):
            program("s", e)

        @blk.gpsimd
        def _(e):
            program("g", e)

        @blk.sync
        def _(e):
            program("y", e)

    stack.close()
    return nc


def host_inputs(core_x, W):
    c = np.ascontiguousarray
    cw = W["conv_w"].reshape(64, 128, KC).transpose(1, 0, 2).reshape(128, 64 * KC)
    return {
        "x": c(core_x.astype(np.float32, copy=False)),
        "wqkv": c(W["Wqkv"]), "wz": c(W["Wz"]),
        "wab": c(np.concatenate([W["Wa"], W["Wb"]], axis=1)),
        "cw": c(cw),
        "alog": c(W["A_log"].reshape(32, 1)), "dtb": c(W["dt_bias"].reshape(32, 1)),
        "nw": c(W["norm_w"].reshape(128, 1)),
        "wout": c(W["Wout"]),
        "wupT": c(W["W_up"].T), "wgT": c(W["W_gate"].T), "wdT": c(W["W_down"].T),
    }


def kernel(L0_attn_x, L0_attn_cu_seqlens, L0_attn_Wqkv, L0_attn_Wz, L0_attn_Wa, L0_attn_Wb,
           L0_attn_conv_w, L0_attn_A_log, L0_attn_dt_bias, L0_attn_norm_w, L0_attn_Wout,
           L0_ffn_W_up, L0_ffn_W_gate, L0_ffn_W_down):
    W = {
        "Wqkv": np.asarray(L0_attn_Wqkv, np.float32), "Wz": np.asarray(L0_attn_Wz, np.float32),
        "Wa": np.asarray(L0_attn_Wa, np.float32), "Wb": np.asarray(L0_attn_Wb, np.float32),
        "conv_w": np.asarray(L0_attn_conv_w, np.float32), "A_log": np.asarray(L0_attn_A_log, np.float32),
        "dt_bias": np.asarray(L0_attn_dt_bias, np.float32), "norm_w": np.asarray(L0_attn_norm_w, np.float32),
        "Wout": np.asarray(L0_attn_Wout, np.float32), "W_up": np.asarray(L0_ffn_W_up, np.float32),
        "W_gate": np.asarray(L0_ffn_W_gate, np.float32), "W_down": np.asarray(L0_ffn_W_down, np.float32),
    }
    x = np.asarray(L0_attn_x, np.float32)
    nc = build(L=x.shape[1])
    in_maps = [host_inputs(x[b], W) for b in range(x.shape[0])]
    res = run_bass_kernel_spmd(nc, in_maps, core_ids=list(range(x.shape[0])))
    return np.stack([res.results[b]["out"] for b in range(x.shape[0])], axis=0)
